# version 02
# baseline (speedup 1.0000x reference)
import numpy as np
import ml_dtypes
import concourse.bass as bass
import concourse.mybir as mybir
from concourse.bass_utils import run_bass_kernel_spmd

F32 = mybir.dt.float32
BF16 = mybir.dt.bfloat16
AF = mybir.ActivationFunctionType
ALU = mybir.AluOpType
AX = mybir.AxisListType
BF = ml_dtypes.bfloat16

NCORE = 8
B = 4
S = 4096
D = 1024
SL = S // NCORE
TL = B * SL
NDT = D // 128

ENGS = ("sync", "gpsimd", "vector", "scalar", "tensor")


def _nc():
    return bass.Bass("TRN2", target_bir_lowering=False, debug=False)


class Q:

    def __init__(self):
        self.nc = _nc()
        self.cells = {e: [] for e in ENGS}
        self.sems = {}
        self.cnt = {}
        self.pending = {e: {} for e in ENGS}
        self.covered = {e: None for e in ENGS}
        for name in ("lds", "ldg", "sts", "stg", "mm", "ev", "ac",
                     "Dsync", "Dgpsimd", "Dvector", "Dscalar", "Dtensor"):
            self.sems[name] = self.nc.alloc_semaphore(name)
            self.cnt[name] = 0

    def sbuf(self, name, shape, dtype):
        return self.nc.alloc_sbuf_tensor(name, shape, dtype)

    def psum(self, name, shape, dtype=F32):
        return self.nc.alloc_psum_tensor(name, shape, dtype)

    def _flush_waits(self, eng):
        pend = self.pending[eng]
        if pend:
            items = [(self.sems[s], v) for s, v in pend.items()]
            self.cells[eng].append([lambda e, it=tuple(items): [e.wait_ge(h, v) for h, v in it], None, None, "wait"])
            pend.clear()

    def op(self, eng, fn, inc=None):
        self._flush_waits(eng)
        cell = [fn, None, None, "op"]
        if inc is not None:
            sem, k = inc
            self.cnt[sem] += k
            cell[1] = (sem, k)
            cell[2] = self.cnt[sem]
            self.covered[eng] = (sem, self.cnt[sem])
        else:
            self.covered[eng] = None
        self.cells[eng].append(cell)
        return cell[2]

    def wait(self, eng, sem, val):
        if val > 0:
            self.pending[eng][sem] = max(self.pending[eng].get(sem, 0), val)

    def dma(self, eng, out, in_, store=False):
        sem = ("st" if store else "ld") + ("g" if eng == "gpsimd" else "s")
        return self.op(eng, lambda e, o=out, i=in_: e.dma_start(out=o, in_=i), inc=(sem, 16))

    def wait_loads(self, *engs):
        for e in engs:
            self.wait(e, "lds", self.cnt["lds"])
            self.wait(e, "ldg", self.cnt["ldg"])

    def _cover(self, eng):
        cells = self.cells[eng]
        last = None
        for c in reversed(cells):
            if c[3] == "op":
                last = c
                break
        if last is None:
            return None
        if last[1] is None:
            ds = "D" + eng
            self.cnt[ds] += 1
            last[1] = (ds, 1)
            last[2] = self.cnt[ds]
            self.covered[eng] = (ds, self.cnt[ds])
        if self.covered[eng] is None:
            self.covered[eng] = (last[1][0], last[2])
        return self.covered[eng]

    def barrier(self, engs_from=None, engs_to=None):
        engs_from = engs_from or ENGS
        engs_to = engs_to or ENGS
        covers = []
        for e in engs_from:
            c = self._cover(e)
            if c is not None:
                covers.append((e, c))
        for et in engs_to:
            for e, (sem, val) in covers:
                if e == et:
                    continue
                self.wait(et, sem, val)

    def finish(self):
        for e in ENGS:
            self._flush_waits(e)
        nc = self.nc
        with nc.Block() as block:
            for e in ENGS:
                cells = self.cells[e]
                if not cells:
                    continue

                def body(eng, cells=cells):
                    for fn, inc, _, _k in cells:
                        ins = fn(eng)
                        if inc is not None:
                            ins.then_inc(self.sems[inc[0]], inc[1])

                getattr(block, e)(body)
        return nc



def emit_ln_finalize(q, cn):
    q.op("vector", lambda e: e.tensor_scalar(cn["rA"][:], cn["rA"][:], -1.0 / D, None, op0=ALU.mult))
    q.op("vector", lambda e: e.tensor_scalar(cn["rB"][:], cn["rB"][:], 1.0 / D, None, op0=ALU.mult), inc=("ev", 1))
    q.wait("vector", "ev", q.cnt["ev"])
    q.op("vector", lambda e: e.tensor_mul(cn["rC"][:], cn["rA"][:], cn["rA"][:]), inc=("ev", 1))
    q.wait("vector", "ev", q.cnt["ev"])
    q.op("vector", lambda e: e.tensor_sub(cn["rB"][:], cn["rB"][:], cn["rC"][:]), inc=("ev", 1))
    q.wait("vector", "ev", q.cnt["ev"])
    q.op("vector", lambda e: e.tensor_scalar(cn["rB"][:], cn["rB"][:], 1e-5, None, op0=ALU.add))
    q.barrier(engs_from=("vector",), engs_to=("scalar",))
    q.op("scalar", lambda e: e.activation(out=cn["rB"][:], in_=cn["rB"][:], func=AF.Sqrt))
    q.barrier(engs_from=("scalar",), engs_to=("vector",))
    q.op("vector", lambda e: e.reciprocal(out=cn["rB"][:], in_=cn["rB"][:]), inc=("ev", 1))
    q.wait("vector", "ev", q.cnt["ev"])
    q.op("vector", lambda e: e.tensor_mul(cn["rC"][:], cn["rA"][:], cn["rB"][:]))
    q.barrier(engs_from=("vector",), engs_to=("tensor",))


def emit_ln_cm(q, xt, xb, cn, T, mutate_ok=True):
    nch = T // 512
    for t in range(NDT):
        q.op("vector", lambda e, a=xb[t], b=xt[t]: e.tensor_copy(out=a[:], in_=b[:]))
    q.barrier(engs_from=("vector",), engs_to=("scalar", "tensor"))
    for ch in range(nch):
        sl = slice(ch * 512, (ch + 1) * 512)
        for t in range(NDT):
            sq = cn["sqs"][t % 2]
            if ch * NDT + t >= 2:
                q.wait("scalar", "mm", q.cnt["mm"] - 1)
            q.op("scalar", lambda e, a=sq, b=xb[t], s=sl: e.activation(out=a[:], in_=b[:, s], func=AF.Square), inc=("ac", 1))
            q.op("tensor", lambda e, r=xb[t], s=sl, st=(t == 0), sp=(t == NDT - 1), ps=cn["ps_st"][0]:
                 e.matmul(ps[:], cn["onesc_b"][:], r[:, s], start=st, stop=sp))
            q.wait("tensor", "ac", q.cnt["ac"])
            q.op("tensor", lambda e, r=sq, st=(t == 0), sp=(t == NDT - 1), ps=cn["ps_st"][1]:
                 e.matmul(ps[:], cn["onesc_b"][:], r[:], start=st, stop=sp), inc=("mm", 1))
        q.barrier(engs_from=("tensor",), engs_to=("vector",))
        q.op("vector", lambda e, s=sl: e.tensor_copy(out=cn["ssum"][:, s], in_=cn["ps_st"][0][:]))
        q.op("vector", lambda e, s=sl: e.tensor_copy(out=cn["sssq"][:, s], in_=cn["ps_st"][1][:]), inc=("ev", 1))
        q.barrier(engs_from=("vector",), engs_to=("tensor",))
    emit_ln_finalize(q, cn)
    for ch in range(nch):
        sl = slice(ch * 512, (ch + 1) * 512)
        q.op("tensor", lambda e, s=sl: e.matmul(cn["ps_bc"][0][:], cn["onesr_f"][:], cn["rstd"][:, s], start=True, stop=True))
        q.op("tensor", lambda e, s=sl: e.matmul(cn["ps_bc"][1][:], cn["onesr_f"][:], cn["s2"][:, s], start=True, stop=True), inc=("mm", 1))
        q.barrier(engs_from=("tensor",), engs_to=("vector",))
        q.op("vector", lambda e, s=sl: e.tensor_copy(out=cn["bcs"][0][:, s], in_=cn["ps_bc"][0][:]))
        q.op("vector", lambda e, s=sl: e.tensor_copy(out=cn["bcs"][1][:, s], in_=cn["ps_bc"][1][:]), inc=("ev", 1))
        q.barrier(engs_from=("vector",), engs_to=("tensor",))
    for t in range(NDT):
        q.op("vector", lambda e, a=xt[t]: e.tensor_mul(a[:], a[:], cn["bcs"][0][:, 0:T]))
        q.op("vector", lambda e, a=xt[t]: e.tensor_add(a[:], a[:], cn["bcs"][1][:, 0:T]))
        q.op("vector", lambda e, a=xt[t], o=xb[t]: e.tensor_copy(out=o[:], in_=a[:]))


def alloc_ln_common(q, T, ps_pool=None):
    cn = {}
    cn["sqs"] = [q.sbuf(f"lnsq{i}", [128, 512], BF16) for i in range(2)]
    for n in ("rA", "rB", "rC"):
        cn[n] = q.sbuf("ln_" + n, [1, T], F32)
    cn["ssum"] = cn["mun"] = cn["rA"]
    cn["sssq"] = cn["m2"] = cn["var"] = cn["rstd"] = cn["rB"]
    cn["s2"] = cn["rC"]
    cn["bcs"] = (q.sbuf("ln_bc0", [128, T], F32), q.sbuf("ln_bc1", [128, T], F32))
    cn["onesc_b"] = q.sbuf("onescb", [128, 1], BF16)
    cn["onesr_f"] = q.sbuf("onesrf", [1, 128], F32)
    if ps_pool is None:
        cn["ps_st"] = [q.psum(f"ps_st{i}", [1, 512]) for i in range(2)]
        cn["ps_bc"] = [q.psum(f"ps_bc{i}", [128, 512]) for i in range(2)]
    else:
        cn["ps_st"] = [ps_pool[0][0:1, :], ps_pool[1][0:1, :]]
        cn["ps_bc"] = [ps_pool[2][:], ps_pool[3][:]]
    return cn


def load_ln_consts(q, cn, onesc_ap, onesr_ap):
    q.dma("gpsimd", cn["onesc_b"][:], onesc_ap[:])
    q.dma("gpsimd", cn["onesr_f"][:], onesr_ap[:])



def emit_proj(q, wtiles, rhs_tiles, pp, evac, n_out_tiles=NDT, T=TL, kin_tiles=None):
    kt = kin_tiles if kin_tiles is not None else len(rhs_tiles)
    nch = T // 512
    hist = []
    q.barrier(engs_from=("vector", "scalar"), engs_to=("tensor",))
    for ot in range(n_out_tiles):
        for ch in range(nch):
            gi = ot * nch + ch
            sl = slice(ch * 512, (ch + 1) * 512)
            ps = pp[gi % 2]
            if gi >= 2:
                for sem, val in hist[gi - 2]:
                    q.wait("tensor", sem, val)
            for kin in range(kt):
                inc = ("mm", 1) if kin == kt - 1 else None
                q.op("tensor", lambda e, w=wtiles[kin], o=ot, s=sl, r=rhs_tiles[kin], ps=ps,
                     st=(kin == 0), sp=(kin == kt - 1):
                     e.matmul(ps[:], w[:, o * 128:(o + 1) * 128], r[:, s], start=st, stop=sp), inc=inc)
            covers = evac(ot, sl, ps, q.cnt["mm"])
            hist.append(covers)



def build_A1():
    q = Q()
    nc = q.nc
    x = nc.dram_tensor("x", [D, TL], F32, kind="ExternalInput").ap()
    wk = nc.dram_tensor("wk", [D, D], F32, kind="ExternalInput").ap()
    wv = nc.dram_tensor("wv", [D, D], F32, kind="ExternalInput").ap()
    wr = nc.dram_tensor("wr", [D, D], F32, kind="ExternalInput").ap()
    onesc = nc.dram_tensor("onesc", [128, 1], F32, kind="ExternalInput").ap()
    onesr = nc.dram_tensor("onesr", [1, 128], F32, kind="ExternalInput").ap()
    kv_o = nc.dram_tensor("kv", [D, TL], F32, kind="ExternalOutput").ap()
    r_o = nc.dram_tensor("r", [D, TL], BF16, kind="ExternalOutput").ap()

    xt = [q.sbuf(f"xt{t}", [128, TL], F32) for t in range(NDT)]
    xb = [q.sbuf(f"xb{t}", [128, TL], BF16) for t in range(NDT)]
    rt = [q.sbuf(f"rt{t}", [128, TL], BF16) for t in range(NDT)]
    wA = [q.sbuf(f"wA{t}", [128, D], BF16) for t in range(NDT)]
    wB = [q.sbuf(f"wB{t}", [128, D], BF16) for t in range(NDT)]
    kvt = xt
    cn = alloc_ln_common(q, TL)
    pp = (q.psum("ps_p0", [128, 512]), q.psum("ps_p1", [128, 512]))

    for t in range(NDT):
        q.dma("sync", xt[t][:], x[t * 128:(t + 1) * 128, :])
    for t in range(NDT):
        q.dma("gpsimd", wA[t][:], wk[t * 128:(t + 1) * 128, :])
    for t in range(NDT):
        q.dma("gpsimd", wB[t][:], wv[t * 128:(t + 1) * 128, :])
    load_ln_consts(q, cn, onesc, onesr)
    q.wait_loads("vector", "scalar", "tensor")

    emit_ln_cm(q, xt, xb, cn, TL)
    q.barrier(engs_from=("vector",), engs_to=("tensor", "scalar"))

    def evac_k(ot, sl, ps, mmv):
        q.wait("vector", "mm", mmv)
        q.op("vector", lambda e, o=ot, s=sl, ps=ps: e.tensor_copy(out=kvt[o][:, s], in_=ps[:]), inc=("ev", 1))
        return [("ev", q.cnt["ev"])]

    emit_proj(q, wA, xb, pp, evac_k)
    mm_after_k = q.cnt["mm"]
    q.wait("gpsimd", "mm", mm_after_k)
    for t in range(NDT):
        q.dma("gpsimd", wA[t][:], wr[t * 128:(t + 1) * 128, :])
    ld_r = q.cnt["ldg"]

    def evac_v(ot, sl, ps, mmv):
        q.wait("vector", "mm", mmv)
        q.op("vector", lambda e, o=ot, s=sl, ps=ps: e.tensor_mul(kvt[o][:, s], kvt[o][:, s], ps[:]), inc=("ev", 1))
        return [("ev", q.cnt["ev"])]

    emit_proj(q, wB, xb, pp, evac_v)
    q.wait("tensor", "ldg", ld_r)

    def evac_r(ot, sl, ps, mmv):
        q.wait("scalar", "mm", mmv)
        q.op("scalar", lambda e, o=ot, s=sl, ps=ps: e.activation(out=rt[o][:, s], in_=ps[:], func=AF.Sigmoid), inc=("ac", 1))
        return [("ac", q.cnt["ac"])]

    emit_proj(q, wA, xb, pp, evac_r)
    q.barrier(engs_from=("vector", "scalar"), engs_to=("gpsimd", "sync"))
    for t in range(NDT):
        q.dma("gpsimd", kv_o[t * 128:(t + 1) * 128, :], kvt[t][:], store=True)
        q.dma("sync", r_o[t * 128:(t + 1) * 128, :], rt[t][:], store=True)
    q.wait("gpsimd", "stg", q.cnt["stg"])
    q.wait("gpsimd", "sts", q.cnt["sts"])
    q._flush_waits("gpsimd")
    return q.finish()



def build_A2():
    q = Q()
    nc = q.nc
    kv = nc.dram_tensor("kv", [128, B * S], F32, kind="ExternalInput").ap()
    r = nc.dram_tensor("r", [128, B * S], BF16, kind="ExternalInput").ap()
    td = nc.dram_tensor("td", [128, 1], F32, kind="ExternalInput").ap()
    tf = nc.dram_tensor("tf", [128, 1], F32, kind="ExternalInput").ap()
    out = nc.dram_tensor("out", [128, B * S], BF16, kind="ExternalOutput").ap()

    kvb = [q.sbuf(f"kv{i}", [128, S], F32) for i in range(2)]
    rb = [q.sbuf(f"r{i}", [128, S], BF16) for i in range(2)]
    sb = q.sbuf("s", [128, S], F32)
    eb = q.sbuf("e", [128, S], F32)
    ob = [q.sbuf(f"o{i}", [128, S], BF16) for i in range(2)]
    wcol = q.sbuf("wcol", [128, 1], F32)
    dcol = q.sbuf("dcol", [128, 1], F32)
    ucol = q.sbuf("ucol", [128, 1], F32)
    tdc = q.sbuf("tdc", [128, 1], F32)
    tfc = q.sbuf("tfc", [128, 1], F32)

    q.dma("sync", tdc[:], td[:])
    q.dma("sync", tfc[:], tf[:])
    q.wait("scalar", "lds", 32)
    q.op("scalar", lambda e: e.activation(out=wcol[:], in_=tdc[:], func=AF.Exp))
    q.barrier(engs_from=("scalar",), engs_to=("vector",))
    q.op("vector", lambda e: e.tensor_scalar(wcol[:], wcol[:], -1.0, None, op0=ALU.mult))
    q.barrier(engs_from=("vector",), engs_to=("scalar",))
    q.op("scalar", lambda e: e.activation(out=dcol[:], in_=wcol[:], func=AF.Exp))
    q.op("scalar", lambda e: e.activation(out=ucol[:], in_=tfc[:], func=AF.Exp))
    q.barrier(engs_from=("scalar",), engs_to=("vector",))

    for b in range(B):
        i = b % 2
        if b >= 2:
            q.wait("sync", "ev", b - 1)
        q.dma("sync", kvb[i][:], kv[:, b * S:(b + 1) * S])
        q.dma("sync", rb[i][:], r[:, b * S:(b + 1) * S])
        q.wait("vector", "lds", 32 + 32 * (b + 1))
        if b >= 2:
            q.wait("vector", "stg", 16 * (b - 1))
        q.op("vector", lambda e, k=kvb[i]: e.tensor_tensor_scan(
            out=sb[:], data0=dcol[:].to_broadcast([128, S]), data1=k[:],
            initial=0.0, op0=ALU.mult, op1=ALU.add))
        q.op("vector", lambda e, k=kvb[i]: e.tensor_scalar(k[:], k[:], ucol[:], None, op0=ALU.mult))
        q.op("vector", lambda e, k=kvb[i]: e.tensor_copy(out=eb[:, 0:1], in_=k[:, 0:1]))
        q.op("vector", lambda e, k=kvb[i]: e.tensor_add(eb[:, 1:S], sb[:, 0:S - 1], k[:, 1:S]))
        q.op("vector", lambda e, rr=rb[i], o=ob[i]: e.tensor_mul(o[:], rr[:], eb[:]), inc=("ev", 1))
        q.wait("gpsimd", "ev", q.cnt["ev"])
        q.dma("gpsimd", out[:, b * S:(b + 1) * S], ob[i][:], store=True)
    q.wait("gpsimd", "stg", q.cnt["stg"])
    q._flush_waits("gpsimd")
    return q.finish()



def build_A3(dbg=0):
    q = Q()
    nc = q.nc
    rw = nc.dram_tensor("rw", [D, TL], BF16, kind="ExternalInput").ap()
    xr = nc.dram_tensor("xr", [D, TL], F32, kind="ExternalInput").ap()
    wo = nc.dram_tensor("wo", [D, D], F32, kind="ExternalInput").ap()
    onesc = nc.dram_tensor("onesc", [128, 1], F32, kind="ExternalInput").ap()
    onesr = nc.dram_tensor("onesr", [1, 128], F32, kind="ExternalInput").ap()
    x1_o = nc.dram_tensor("x1", [D, TL], F32, kind="ExternalOutput").ap()
    ln1_o = nc.dram_tensor("ln1", [D, TL], BF16, kind="ExternalOutput").ap()

    rwt = [q.sbuf(f"rw{t}", [128, TL], BF16) for t in range(NDT)]
    xt = [q.sbuf(f"xs{t}", [128, TL], F32) for t in range(NDT)]
    ab = [q.sbuf(f"ab{t}", [128, TL], BF16) for t in range(NDT)]
    wsb = [q.sbuf(f"wo{t}", [128, D], BF16) for t in range(NDT)]
    tmp = q.sbuf("tmp", [128, TL], F32)
    cn = alloc_ln_common(q, TL)
    pp = (q.psum("ps_p0", [128, 512]), q.psum("ps_p1", [128, 512]))

    for t in range(NDT):
        q.dma("sync", rwt[t][:], rw[t * 128:(t + 1) * 128, :])
        q.dma("sync", xt[t][:], xr[t * 128:(t + 1) * 128, :])
    for t in range(NDT):
        q.dma("gpsimd", wsb[t][:], wo[t * 128:(t + 1) * 128, :])
    load_ln_consts(q, cn, onesc, onesr)
    q.wait_loads("vector", "scalar", "tensor")

    if dbg == 4:
        q.wait_loads("gpsimd")
        q.barrier()
        for t in range(NDT):
            q.dma("gpsimd", ln1_o[t * 128:(t + 1) * 128, :], rwt[t][:], store=True)
            q.dma("sync", x1_o[t * 128:(t + 1) * 128, :], xt[t][:], store=True)
        q.wait("gpsimd", "stg", q.cnt["stg"])
        q.wait("gpsimd", "sts", q.cnt["sts"])
        q._flush_waits("gpsimd")
        return q.finish()
    if dbg == 5:
        def evac5(ot, sl, ps, mmv):
            q.wait("vector", "mm", mmv)
            q.op("vector", lambda e, o=ot, s=sl, ps=ps: e.tensor_copy(out=ab[o][:, s], in_=ps[:]), inc=("ev", 1))
            return [("ev", q.cnt["ev"])]
        emit_proj(q, wsb, rwt, pp, evac5)
        q.barrier()
        for t in range(NDT):
            q.dma("gpsimd", ln1_o[t * 128:(t + 1) * 128, :], ab[t][:], store=True)
            q.dma("sync", x1_o[t * 128:(t + 1) * 128, :], xt[t][:], store=True)
        q.wait("gpsimd", "stg", q.cnt["stg"])
        q.wait("gpsimd", "sts", q.cnt["sts"])
        q._flush_waits("gpsimd")
        return q.finish()
    evdone = []
    mmstat = []
    for ch in range(4):
        sl = slice(ch * 512, (ch + 1) * 512)
        for ot in range(NDT):
            gi = ch * NDT + ot
            ps = pp[gi % 2]
            if gi >= 2:
                for sem, val in evdone[gi - 2]:
                    q.wait("tensor", sem, val)
            for kin in range(NDT):
                inc = ("mm", 1) if kin == NDT - 1 else None
                q.op("tensor", lambda e, w=wsb[kin], o=ot, s=sl, rr=rwt[kin], ps=ps,
                     st=(kin == 0), sp=(kin == NDT - 1):
                     e.matmul(ps[:], w[:, o * 128:(o + 1) * 128], rr[:, s], start=st, stop=sp), inc=inc)
            mmv = q.cnt["mm"]
            q.wait("vector", "mm", mmv)
            q.wait("scalar", "mm", mmv)
            q.op("vector", lambda e, o=ot, s=sl, ps=ps: e.tensor_copy(out=ab[o][:, s], in_=ps[:]), inc=("ev", 1))
            sq = cn["sqs"][gi % 2]
            if gi >= 2:
                q.wait("scalar", "mm", mmstat[gi - 2])
            q.wait("scalar", "ev", q.cnt["ev"])
            q.op("scalar", lambda e, o=ot, s=sl, a=sq: e.activation(out=a[:], in_=ab[o][:, s], func=AF.Square), inc=("ac", 1))
            evdone.append([("ev", q.cnt["ev"]), ("ac", q.cnt["ac"])])
            if dbg != 3:
                q.wait("tensor", "ev", q.cnt["ev"])
                q.wait("tensor", "ac", q.cnt["ac"])
                q.op("tensor", lambda e, o=ot, s=sl, st=(ot == 0), sp=(ot == NDT - 1), ps=cn["ps_st"][0], a=ab[ot]:
                     e.matmul(ps[:], cn["onesc_b"][:], a[:, s], start=st, stop=sp))
                q.op("tensor", lambda e, st=(ot == 0), sp=(ot == NDT - 1), ps=cn["ps_st"][1], a=sq:
                     e.matmul(ps[:], cn["onesc_b"][:], a[:], start=st, stop=sp), inc=("mm", 1))
            mmstat.append(q.cnt["mm"])
        q.barrier(engs_from=("tensor",), engs_to=("vector",))
        q.op("vector", lambda e, s=sl: e.tensor_copy(out=cn["ssum"][:, s], in_=cn["ps_st"][0][:]))
        q.op("vector", lambda e, s=sl: e.tensor_copy(out=cn["sssq"][:, s], in_=cn["ps_st"][1][:]), inc=("ev", 1))
        q.barrier(engs_from=("vector",), engs_to=("tensor",))
    if dbg in (1, 3):
        q.barrier()
        for t in range(NDT):
            q.dma("gpsimd", ln1_o[t * 128:(t + 1) * 128, :], ab[t][:], store=True)
            q.dma("sync", x1_o[t * 128:(t + 1) * 128, :], xt[t][:], store=True)
        q.wait("gpsimd", "stg", q.cnt["stg"])
        q.wait("gpsimd", "sts", q.cnt["sts"])
        q._flush_waits("gpsimd")
        return q.finish()
    emit_ln_finalize(q, cn)
    for ch in range(4):
        sl = slice(ch * 512, (ch + 1) * 512)
        q.op("tensor", lambda e, s=sl: e.matmul(cn["ps_bc"][0][:], cn["onesr_f"][:], cn["rstd"][:, s], start=True, stop=True))
        q.op("tensor", lambda e, s=sl: e.matmul(cn["ps_bc"][1][:], cn["onesr_f"][:], cn["s2"][:, s], start=True, stop=True), inc=("mm", 1))
        q.barrier(engs_from=("tensor",), engs_to=("vector",))
        q.op("vector", lambda e, s=sl: e.tensor_copy(out=cn["bcs"][0][:, s], in_=cn["ps_bc"][0][:]))
        q.op("vector", lambda e, s=sl: e.tensor_copy(out=cn["bcs"][1][:, s], in_=cn["ps_bc"][1][:]), inc=("ev", 1))
        q.barrier(engs_from=("vector",), engs_to=("tensor",))
    for t in range(NDT):
        q.op("vector", lambda e, x=xt[t]: e.tensor_add(x[:], x[:], cn["bcs"][1][:]))
        q.op("vector", lambda e, a=ab[t]: e.tensor_mul(tmp[:], a[:], cn["bcs"][0][:]))
        q.op("vector", lambda e, x=xt[t]: e.tensor_add(x[:], x[:], tmp[:]), inc=("ev", 1))
        q.wait("sync", "ev", q.cnt["ev"])
        q.dma("sync", x1_o[t * 128:(t + 1) * 128, :], xt[t][:], store=True)
    if dbg == 2:
        q.barrier()
        for t in range(NDT):
            q.dma("gpsimd", ln1_o[t * 128:(t + 1) * 128, :], ab[t][:], store=True)
        q.wait("gpsimd", "stg", q.cnt["stg"])
        q.wait("gpsimd", "sts", q.cnt["sts"])
        q._flush_waits("gpsimd")
        return q.finish()
    q.wait("vector", "sts", q.cnt["sts"])
    emit_ln_cm(q, xt, ab, cn, TL)
    q.barrier(engs_from=("vector",), engs_to=("gpsimd",))
    for t in range(NDT):
        q.dma("gpsimd", ln1_o[t * 128:(t + 1) * 128, :], ab[t][:], store=True)
    q.wait("gpsimd", "stg", q.cnt["stg"])
    q.wait("gpsimd", "sts", q.cnt["sts"])
    q._flush_waits("gpsimd")
    return q.finish()



def _cm(x_bsd, c):
    sl = x_bsd[:, c * SL:(c + 1) * SL, :]
    return np.ascontiguousarray(sl.transpose(2, 0, 1).reshape(D, TL))


_ONESC = np.ones((128, 1), np.float32)
_ONESR = np.ones((1, 128), np.float32)


def _run(nc, in_maps):
    return run_bass_kernel_spmd(nc, in_maps, core_ids=list(range(NCORE)))


def stage_l0attn(x_cm, Wk, Wv, Wr, Wo, td, tf):
    wkT = np.ascontiguousarray(Wk.T)
    wvT = np.ascontiguousarray(Wv.T)
    wrT = np.ascontiguousarray(Wr.T)
    woT = np.ascontiguousarray(Wo.T)

    nc1 = _prog("A1", build_A1)
    maps = [{"x": x_cm[c], "wk": wkT, "wv": wvT, "wr": wrT, "onesc": _ONESC, "onesr": _ONESR}
            for c in range(NCORE)]
    r1 = _run(nc1, maps).results

    kv_full = np.empty((D, B, S), np.float32)
    r_full = np.empty((D, B, S), BF)
    for c in range(NCORE):
        kv_full[:, :, c * SL:(c + 1) * SL] = r1[c]["kv"].reshape(D, B, SL)
        r_full[:, :, c * SL:(c + 1) * SL] = r1[c]["r"].reshape(D, B, SL)
    kv_full = kv_full.reshape(D, B * S)
    r_full = r_full.reshape(D, B * S)
    td8 = td.reshape(NCORE, 128, 1).astype(np.float32)
    tf8 = tf.reshape(NCORE, 128, 1).astype(np.float32)

    nc2 = _prog("A2", build_A2)
    maps = [{"kv": np.ascontiguousarray(kv_full[c * 128:(c + 1) * 128]),
             "r": np.ascontiguousarray(r_full[c * 128:(c + 1) * 128]),
             "td": td8[c], "tf": tf8[c]} for c in range(NCORE)]
    r2 = _run(nc2, maps).results

    rw_full = np.concatenate([r2[c]["out"].reshape(128, B, S) for c in range(NCORE)], axis=0)
    nc3 = _prog("A3", build_A3)
    maps = []
    for c in range(NCORE):
        rw_c = np.ascontiguousarray(rw_full[:, :, c * SL:(c + 1) * SL].reshape(D, TL))
        maps.append({"rw": rw_c, "xr": x_cm[c], "wo": woT, "onesc": _ONESC, "onesr": _ONESR})
    r3 = _run(nc3, maps).results
    x1 = [r3[c]["x1"] for c in range(NCORE)]
    ln1 = [r3[c]["ln1"] for c in range(NCORE)]
    return x1, ln1


def kernel(**inputs):
    inp = {k: np.asarray(v) for k, v in inputs.items()}
    x = inp["L0_attn_x"].astype(np.float32, copy=False)
    x_cm = [_cm(x, c) for c in range(NCORE)]
    x1, ln1 = stage_l0attn(x_cm, inp["L0_attn_Wk"], inp["L0_attn_Wv"], inp["L0_attn_Wr"],
                           inp["L0_attn_Wo"], inp["L0_attn_time_decay"], inp["L0_attn_time_first"])
    ffn = {k[len("L0_ffn_"):]: v for k, v in inp.items() if k.startswith("L0_ffn_")}
    x2, ln2b, ln2f = stage_l0ffn(ln1, x1, ffn)
    att = {k[len("L1_attn_"):]: v for k, v in inp.items() if k.startswith("L1_attn_")}
    l1o = stage_l1attn(ln2b, att)
    x3, ln3b, ln3f = stage_sandwich(l1o, x2)
    moe = {k[len("L1_ffn_"):]: v for k, v in inp.items() if k.startswith("L1_ffn_")}
    x4, _, _ = stage_l1ffn(ln3b, ln3f, x3, moe)
    out = np.empty((B, S, D), np.float32)
    for c in range(NCORE):
        out[:, c * SL:(c + 1) * SL, :] = np.asarray(x4[c]).reshape(D, B, SL).transpose(1, 2, 0)
    return out


INNER = 512
HEADS = 8
DH = 64
HID = 2730
HIDP = 2816
NJ = 44
NHT = HIDP // 128


def _shift_tiles(q, xf, xprev, hs):
    for i in range(4):
        t = 4 + i
        for b in range(B):
            q.op("vector", lambda e, i=i, t=t, b=b: e.tensor_copy(
                out=hs[i][:, b * SL + 1:(b + 1) * SL], in_=xf[t][:, b * SL:(b + 1) * SL - 1]))
            q.op("vector", lambda e, i=i, t=t, b=b: e.tensor_copy(
                out=hs[i][:, b * SL:b * SL + 1], in_=xprev[t][:, b:b + 1]))


def build_B0():
    q = Q()
    nc = q.nc
    xf = nc.dram_tensor("xf", [D, TL], BF16, kind="ExternalInput").ap()
    xprev = nc.dram_tensor("xprev", [D, B], BF16, kind="ExternalInput").ap()
    wqkv = nc.dram_tensor("wqkv", [D, 3 * INNER], F32, kind="ExternalInput").ap()
    csq = nc.dram_tensor("csq", [128, TL], BF16, kind="ExternalInput").ap()
    ssq = nc.dram_tensor("ssq", [128, TL], BF16, kind="ExternalInput").ap()
    csk = nc.dram_tensor("csk", [128, TL], BF16, kind="ExternalInput").ap()
    ssk = nc.dram_tensor("ssk", [128, TL], BF16, kind="ExternalInput").ap()
    qscol = nc.dram_tensor("qscol", [128, 1], F32, kind="ExternalInput").ap()
    kscol = nc.dram_tensor("kscol", [128, 1], F32, kind="ExternalInput").ap()
    selqk = nc.dram_tensor("selqk", [128, 8 * 16], F32, kind="ExternalInput").ap()
    expd = nc.dram_tensor("expd", [16, 8 * 128], F32, kind="ExternalInput").ap()
    p32 = nc.dram_tensor("p32", [128, 128], F32, kind="ExternalInput").ap()
    idf = nc.dram_tensor("idf", [128, 128], F32, kind="ExternalInput").ap()
    qf_o = nc.dram_tensor("qf", [INNER, TL], BF16, kind="ExternalOutput").ap()
    kf_o = nc.dram_tensor("kf", [INNER, TL], BF16, kind="ExternalOutput").ap()
    vt_o = nc.dram_tensor("vt", [TL, INNER], BF16, kind="ExternalOutput").ap()

    xft = [q.sbuf(f"xf{t}", [128, TL], BF16) for t in range(NDT)]
    xpv = [q.sbuf(f"xpv{t}", [128, B], BF16) for t in range(NDT)]
    hs = [q.sbuf(f"hs{i}", [128, TL], BF16) for i in range(4)]
    wt = [q.sbuf(f"w{t}", [128, 3 * INNER], BF16) for t in range(NDT)]
    qr = [q.sbuf(f"qr{t}", [128, TL], BF16) for t in range(4)]
    kr = [q.sbuf(f"kr{t}", [128, TL], BF16) for t in range(4)]
    vr = [q.sbuf(f"vr{t}", [128, TL], BF16) for t in range(4)]
    qf = [q.sbuf(f"qfo{t}", [128, TL], BF16) for t in range(4)]
    kf = [q.sbuf(f"kfo{t}", [128, TL], BF16) for t in range(4)]
    vtok = [q.sbuf(f"vt{t}", [128, INNER], BF16) for t in range(16)]
    tq1 = q.sbuf("tq1", [128, TL], BF16)
    tq2 = q.sbuf("tq2", [128, TL], BF16)
    tk1 = q.sbuf("tk1", [128, TL], BF16)
    tk2 = q.sbuf("tk2", [128, TL], BF16)
    qs8 = q.sbuf("qs8", [128, 1], F32)
    qs8w = q.sbuf("qs8w", [128, 1], F32)
    ksc = q.sbuf("ksc", [128, 1], F32)
    kscw = q.sbuf("kscw", [128, 1], F32)
    selb = q.sbuf("selb", [128, 8 * 16], BF16)
    expb = q.sbuf("expb", [16, 8 * 128], F32)
    p32b = q.sbuf("p32b", [128, 128], BF16)
    idb = q.sbuf("idb", [128, 128], BF16)
    sqs = [q.sbuf(f"sq{i}", [128, 512], BF16) for i in range(2)]
    nrm = q.sbuf("nrm", [16, TL], F32)
    tA = q.sbuf("tA", [128, 512], F32)
    tB = q.sbuf("tB", [128, 512], F32)
    tC = q.sbuf("tC", [128, 512], F32)
    ps_p = [q.psum(f"psp{i}", [128, 512]) for i in range(2)]
    ps_n = q.psum("psn", [16, 512])
    ps_sw = [q.psum(f"pssw{i}", [128, 512]) for i in range(2)]
    ps_bc = [q.psum(f"psbc{i}", [128, 512]) for i in range(2)]
    ps_tr = q.psum("pstr", [128, 128], BF16)

    for t in range(NDT):
        q.dma("sync", xft[t][:], xf[t * 128:(t + 1) * 128, :])
        q.dma("sync", xpv[t][:], xprev[t * 128:(t + 1) * 128, :])
        q.dma("gpsimd", wt[t][:], wqkv[t * 128:(t + 1) * 128, :])
    for dst, src in ((tq1, csq), (tq2, ssq), (tk1, csk), (tk2, ssk)):
        q.dma("sync", dst[:], src[:])
    for dst, src in ((qs8, qscol), (ksc, kscol)):
        q.dma("sync", dst[:], src[:])
    q.dma("gpsimd", selb[:], selqk[:])
    q.dma("sync", expb[:], expd[:])
    q.dma("gpsimd", p32b[:], p32[:])
    q.dma("gpsimd", idb[:], idf[:])
    q.wait_loads("vector", "scalar", "tensor")

    _shift_tiles(q, xft, xpv, hs)
    q.op("vector", lambda e: e.tensor_scalar(qs8[:], qs8[:], 8.0, None, op0=ALU.mult), inc=("ev", 1))
    q.wait("vector", "ev", q.cnt["ev"])
    for base in (0, 64):
        q.op("vector", lambda e, b=base: e.tensor_copy(out=qs8w[b:b + 32, :], in_=qs8[b + 32:b + 64, :]))
        q.op("vector", lambda e, b=base: e.tensor_copy(out=qs8w[b + 32:b + 64, :], in_=qs8[b:b + 32, :]))
        q.op("vector", lambda e, b=base: e.tensor_copy(out=kscw[b:b + 32, :], in_=ksc[b + 32:b + 64, :]))
        q.op("vector", lambda e, b=base: e.tensor_copy(out=kscw[b + 32:b + 64, :], in_=ksc[b:b + 32, :]))
    q.op("vector", lambda e: e.engine_nop(), inc=("ev", 1))
    q.wait("vector", "ev", q.cnt["ev"])
    q.op("vector", lambda e: e.tensor_scalar(tq1[:], tq1[:], qs8[:], None, op0=ALU.mult))
    q.op("vector", lambda e: e.tensor_scalar(tq2[:], tq2[:], qs8w[:], None, op0=ALU.mult))
    q.op("vector", lambda e: e.tensor_scalar(tk1[:], tk1[:], ksc[:], None, op0=ALU.mult))
    q.op("vector", lambda e: e.tensor_scalar(tk2[:], tk2[:], kscw[:], None, op0=ALU.mult))
    q.barrier(engs_from=("vector",), engs_to=("tensor", "scalar"))

    rhs = xft[:4] + hs

    outs = qr + kr + vr

    def evac_qkv(ot, sl, ps, mmv):
        q.wait("vector", "mm", mmv)
        q.op("vector", lambda e, o=outs[ot], s=sl, ps=ps: e.tensor_copy(out=o[:, s], in_=ps[:]), inc=("ev", 1))
        return [("ev", q.cnt["ev"])]

    emit_proj(q, wt, rhs, ps_p, evac_qkv, n_out_tiles=12)
    q.barrier()

    tr = 0
    for vti in range(16):
        for cj in range(4):
            q.op("tensor", lambda e, v=vr[cj], c=vti: e.transpose(ps_tr[:], v[:, c * 128:(c + 1) * 128], idb[:]), inc=("mm", 1))
            mmv = q.cnt["mm"]
            q.wait("vector", "mm", mmv)
            q.op("vector", lambda e, t=vti, c=cj: e.tensor_copy(out=vtok[t][:, c * 128:(c + 1) * 128], in_=ps_tr[:]), inc=("ev", 1))
            q.wait("tensor", "ev", q.cnt["ev"])
            tr += 1
    q.barrier()

    srcs = qr + kr
    for ch in range(4):
        sl = slice(ch * 512, (ch + 1) * 512)
        for t in range(8):
            sq = sqs[t % 2]
            gi = ch * 8 + t
            if gi >= 2:
                q.wait("scalar", "mm", q.cnt["mm"] - 1)
            q.op("scalar", lambda e, a=sq, b=srcs[t], s=sl: e.activation(out=a[:], in_=b[:, s], func=AF.Square), inc=("ac", 1))
            q.wait("tensor", "ac", q.cnt["ac"])
            q.op("tensor", lambda e, a=sq, t=t, st=(t == 0), sp=(t == 7):
                 e.matmul(ps_n[:], selb[:, t * 16:(t + 1) * 16], a[:], start=st, stop=sp), inc=("mm", 1))
        q.barrier(engs_from=("tensor",), engs_to=("vector",))
        q.op("vector", lambda e, s=sl: e.tensor_copy(out=nrm[:, s], in_=ps_n[:]), inc=("ev", 1))
        q.barrier(engs_from=("vector",), engs_to=("tensor",))
    q.barrier(engs_from=("vector",), engs_to=("scalar",))
    q.op("scalar", lambda e: e.activation(out=nrm[:], in_=nrm[:], func=AF.Sqrt))
    q.barrier(engs_from=("scalar",), engs_to=("vector",))
    q.op("vector", lambda e: e.tensor_scalar(nrm[:], nrm[:], 1e-12, None, op0=ALU.max))
    q.op("vector", lambda e: e.reciprocal(out=nrm[:], in_=nrm[:]))
    q.barrier(engs_from=("vector",), engs_to=("tensor",))

    tabs1 = [tq1] * 4 + [tk1] * 4
    tabs2 = [tq2] * 4 + [tk2] * 4
    fins = qf + kf
    rothist = []
    for t in range(8):
        for ch in range(4):
            sl = slice(ch * 512, (ch + 1) * 512)
            gi = t * 4 + ch
            pb = ps_bc[gi % 2]
            pw = ps_sw[gi % 2]
            if gi >= 2:
                q.wait("tensor", "ev", rothist[gi - 2])
            q.op("tensor", lambda e, t=t, s=sl, pb=pb: e.matmul(pb[:], expb[:, t * 128:(t + 1) * 128], nrm[:, s], start=True, stop=True))
            q.op("tensor", lambda e, t=t, s=sl, pw=pw: e.matmul(pw[:], p32b[:], srcs[t][:, s], start=True, stop=True), inc=("mm", 1))
            mmv = q.cnt["mm"]
            q.wait("vector", "mm", mmv)
            q.op("vector", lambda e, pb=pb: e.tensor_copy(out=tC[:], in_=pb[:]))
            q.op("vector", lambda e, pw=pw: e.tensor_copy(out=tB[:], in_=pw[:]))
            q.op("vector", lambda e, t=t, s=sl: e.tensor_mul(tA[:], srcs[t][:, s], tC[:]))
            q.op("vector", lambda e: e.tensor_mul(tB[:], tB[:], tC[:]))
            q.op("vector", lambda e, t=t, s=sl: e.tensor_mul(tA[:], tA[:], tabs1[t][:, s]))
            q.op("vector", lambda e, t=t, s=sl: e.tensor_mul(tB[:], tB[:], tabs2[t][:, s]))
            q.op("vector", lambda e, t=t, s=sl: e.tensor_add(fins[t][:, s], tA[:], tB[:]), inc=("ev", 1))
            rothist.append(q.cnt["ev"])
    q.barrier()
    for t in range(4):
        q.dma("gpsimd", qf_o[t * 128:(t + 1) * 128, :], qf[t][:], store=True)
        q.dma("gpsimd", kf_o[t * 128:(t + 1) * 128, :], kf[t][:], store=True)
    for t in range(16):
        q.dma("sync", vt_o[t * 128:(t + 1) * 128, :], vtok[t][:], store=True)
    q.wait("gpsimd", "stg", q.cnt["stg"])
    q.wait("gpsimd", "sts", q.cnt["sts"])
    q._flush_waits("gpsimd")
    return q.finish()


def build_B1():
    q = Q()
    nc = q.nc
    qf = nc.dram_tensor("qf", [INNER, TL], BF16, kind="ExternalInput").ap()
    kfx = nc.dram_tensor("kfx", [INNER, 2 * TL], BF16, kind="ExternalInput").ap()
    vtx = nc.dram_tensor("vtx", [2 * TL, INNER], BF16, kind="ExternalInput").ap()
    xfb = nc.dram_tensor("xfb", [D, TL], BF16, kind="ExternalInput").ap()
    wo = nc.dram_tensor("wo", [INNER, D], F32, kind="ExternalInput").ap()
    mtok = nc.dram_tensor("mtok", [128, 4 * 1024], BF16, kind="ExternalInput").ap()
    mkm = nc.dram_tensor("mkm", [128, 8 * 512], BF16, kind="ExternalInput").ap()
    idf = nc.dram_tensor("idf", [128, 128], F32, kind="ExternalInput").ap()
    onesc = nc.dram_tensor("onesc", [128, 1], F32, kind="ExternalInput").ap()
    onesr = nc.dram_tensor("onesr", [1, 128], F32, kind="ExternalInput").ap()
    xm_o = nc.dram_tensor("xm", [D, TL], F32, kind="ExternalOutput").ap()
    xmb_o = nc.dram_tensor("xmb", [D, TL], BF16, kind="ExternalOutput").ap()

    qft = [q.sbuf(f"qf{t}", [128, TL], BF16) for t in range(4)]
    kft = [q.sbuf(f"kf{t}", [128, 2 * TL], BF16) for t in range(4)]
    vtt = [q.sbuf(f"vt{t}", [128, INNER], BF16) for t in range(32)]
    xft = [q.sbuf(f"xf{t}", [128, TL], BF16) for t in range(NDT)]
    wt = [q.sbuf(f"wo{t}", [128, D], BF16) for t in range(4)]
    mtokb = q.sbuf("mtokb", [128, 4 * 1024], BF16)
    mkmb = q.sbuf("mkmb", [128, 8 * 512], BF16)
    idb = q.sbuf("idb", [128, 128], BF16)
    onescb = q.sbuf("onescb", [128, 1], BF16)
    onesrf = q.sbuf("onesrf", [1, 128], F32)
    obuf = [q.sbuf(f"ob{t}", [128, TL], BF16) for t in range(4)]
    pT = q.sbuf("pT", [128, 8 * 512], BF16)
    qstage = q.sbuf("qstage", [65, 512], BF16)
    kstage = q.sbuf("kstage", [65, 1024], BF16)
    mcols = q.sbuf("mcols", [128, 4], F32)
    mcolsb = q.sbuf("mcolsb", [128, 4], BF16)
    zrow = q.sbuf("zrow", [1, 512], F32)
    zb = q.sbuf("zb", [64, 512], F32)
    xstg = [q.sbuf(f"xstg{i}", [128, 512], F32) for i in range(2)]
    xstgb = [q.sbuf(f"xstgb{i}", [128, 512], BF16) for i in range(2)]
    ps_tok = q.psum("ps_tok", [128, 1024])
    ps_mr = q.psum("ps_mr", [1, 512])
    ps_T = [q.psum(f"ps_T{i}", [128, 512]) for i in range(2)]
    ps_z = q.psum("ps_z", [1, 512])
    ps_o = q.psum("ps_o", [64, 512])
    ps_zb = q.psum("ps_zb", [64, 512])

    for t in range(4):
        q.dma("sync", qft[t][:], qf[t * 128:(t + 1) * 128, :])
        q.dma("sync", kft[t][:], kfx[t * 128:(t + 1) * 128, :])
    for t in range(32):
        q.dma("sync", vtt[t][:], vtx[t * 128:(t + 1) * 128, :])
    for t in range(NDT):
        q.dma("sync", xft[t][:], xfb[t * 128:(t + 1) * 128, :])
    for t in range(4):
        q.dma("gpsimd", wt[t][:], wo[t * 128:(t + 1) * 128, :])
    q.dma("sync", mtokb[:], mtok[:])
    q.dma("sync", mkmb[:], mkm[:])
    q.dma("gpsimd", idb[:], idf[:])
    q.dma("gpsimd", onescb[:], onesc[:])
    q.dma("gpsimd", onesrf[:], onesr[:])
    q.wait_loads("vector", "scalar", "tensor")

    q.op("vector", lambda e: e.memset(kstage[64:65, :], 1.0))

    for b in range(B):
        for h in range(HEADS):
            ht, hr = h // 2, (h % 2) * 64
            for qt in range(4):
                q.wait("tensor", "ev", q.cnt["ev"])
                for kh in range(2):
                    q.op("tensor", lambda e, qt=qt, kh=kh: e.matmul(
                        ps_tok[:, kh * 512:(kh + 1) * 512], idb[:],
                        mtokb[:, qt * 1024 + kh * 512: qt * 1024 + (kh + 1) * 512], start=True, stop=False))
                    q.op("tensor", lambda e, b=b, qt=qt, kh=kh, ht=ht, hr=hr: e.matmul(
                        ps_tok[:, kh * 512:(kh + 1) * 512],
                        qft[ht][hr:hr + 64, b * 512 + qt * 128: b * 512 + (qt + 1) * 128],
                        kft[ht][hr:hr + 64, b * 1024 + kh * 512: b * 1024 + (kh + 1) * 512],
                        start=False, stop=True), inc=("mm", 1))
                q.wait("vector", "mm", q.cnt["mm"])
                q.op("vector", lambda e, qt=qt: e.reduce_max(mcols[:, qt:qt + 1], ps_tok[:], axis=AX.X), inc=("ev", 1))
            q.wait("vector", "ev", q.cnt["ev"])
            q.op("vector", lambda e: e.tensor_copy(out=mcolsb[:], in_=mcols[:]), inc=("ev", 1))
            q.op("vector", lambda e, b=b, ht=ht, hr=hr: e.tensor_copy(
                out=qstage[0:64, :], in_=qft[ht][hr:hr + 64, b * 512:(b + 1) * 512]), inc=("ev", 1))
            q.barrier(engs_from=("vector",), engs_to=("tensor",))
            for qt in range(4):
                q.op("tensor", lambda e, qt=qt: e.matmul(
                    ps_mr[:, qt * 128:(qt + 1) * 128], mcolsb[:, qt:qt + 1], idb[:], start=True, stop=True),
                    inc=("mm", 1))
            q.wait("vector", "mm", q.cnt["mm"])
            q.op("vector", lambda e: e.tensor_scalar(qstage[64:65, :], ps_mr[:], -1.0, None, op0=ALU.mult), inc=("ev", 1))
            q.op("vector", lambda e, b=b, ht=ht, hr=hr: e.tensor_copy(
                out=kstage[0:64, :], in_=kft[ht][hr:hr + 64, b * 1024:(b + 1) * 1024]), inc=("ev", 1))
            q.barrier(engs_from=("vector",), engs_to=("tensor", "scalar"))
            for kt in range(8):
                ps = ps_T[kt % 2]
                if kt >= 2:
                    q.wait("tensor", "ac", q.cnt["ac"] - 1)
                q.op("tensor", lambda e, kt=kt, ps=ps: e.matmul(
                    ps[:], idb[:], mkmb[:, kt * 512:(kt + 1) * 512], start=True, stop=False))
                q.op("tensor", lambda e, kt=kt, ps=ps: e.matmul(
                    ps[:], kstage[:, kt * 128:(kt + 1) * 128], qstage[:], start=False, stop=True), inc=("mm", 1))
                q.wait("scalar", "mm", q.cnt["mm"])
                q.op("scalar", lambda e, kt=kt, ps=ps: e.activation(
                    out=pT[:, kt * 512:(kt + 1) * 512], in_=ps[:], func=AF.Exp), inc=("ac", 1))
            q.barrier(engs_from=("scalar",), engs_to=("tensor",))
            for kt in range(8):
                q.op("tensor", lambda e, kt=kt, st=(kt == 0), sp=(kt == 7): e.matmul(
                    ps_z[:], onescb[:], pT[:, kt * 512:(kt + 1) * 512], start=st, stop=sp))
            for kt in range(8):
                inc = ("mm", 1) if kt == 7 else None
                q.op("tensor", lambda e, b=b, kt=kt, h=h, st=(kt == 0), sp=(kt == 7): e.matmul(
                    ps_o[:], vtt[b * 8 + kt][:, h * 64:(h + 1) * 64], pT[:, kt * 512:(kt + 1) * 512],
                    start=st, stop=sp), inc=inc)
            q.wait("vector", "mm", q.cnt["mm"])
            q.op("vector", lambda e: e.tensor_copy(out=zrow[:], in_=ps_z[:]))
            q.op("vector", lambda e: e.reciprocal(out=zrow[:], in_=zrow[:]), inc=("ev", 1))
            q.barrier(engs_from=("vector",), engs_to=("tensor",))
            q.op("tensor", lambda e: e.matmul(ps_zb[:], onesrf[:, 0:64], zrow[:], start=True, stop=True), inc=("mm", 1))
            q.wait("vector", "mm", q.cnt["mm"])
            q.op("vector", lambda e: e.tensor_copy(out=zb[:], in_=ps_zb[:]))
            q.op("vector", lambda e, b=b, ht=ht, hr=hr: e.tensor_mul(
                obuf[ht][hr:hr + 64, b * 512:(b + 1) * 512], ps_o[:], zb[:]), inc=("ev", 1))
            q.barrier(engs_from=("vector",), engs_to=("tensor",))
            q.barrier(engs_from=("tensor",), engs_to=("vector",))
    q.barrier()

    for ot in range(NDT):
        for ch in range(4):
            sl = slice(ch * 512, (ch + 1) * 512)
            gi = ot * 4 + ch
            ps = ps_T[gi % 2]
            ii = gi % 2
            if gi >= 2:
                q.wait("tensor", "stg", 16 * (2 * (gi - 1)))
            for kin in range(4):
                inc = ("mm", 1) if kin == 3 else None
                q.op("tensor", lambda e, kin=kin, ot=ot, s=sl, ps=ps, st=(kin == 0), sp=(kin == 3): e.matmul(
                    ps[:], wt[kin][:, ot * 128:(ot + 1) * 128], obuf[kin][:, s], start=st, stop=sp), inc=inc)
            q.wait("vector", "mm", q.cnt["mm"])
            q.op("vector", lambda e, ot=ot, s=sl, ps=ps, ii=ii: e.tensor_add(xstg[ii][:], xft[ot][:, s], ps[:]))
            q.op("vector", lambda e, ii=ii: e.tensor_copy(out=xstgb[ii][:], in_=xstg[ii][:]), inc=("ev", 1))
            q.wait("gpsimd", "ev", q.cnt["ev"])
            q.dma("gpsimd", xm_o[ot * 128:(ot + 1) * 128, sl], xstg[ii][:], store=True)
            q.dma("gpsimd", xmb_o[ot * 128:(ot + 1) * 128, sl], xstgb[ii][:], store=True)
    q.wait("gpsimd", "stg", q.cnt["stg"])
    q._flush_waits("gpsimd")
    return q.finish()



def _rot_tables(c, for_q):
    i = np.arange(32)
    freq = 10000.0 ** (-(2 * i) / 64.0)
    base = (2 * i + 0.4 * 64) / (1.4 * 64)
    t_glob = (512 * c + np.arange(SL)).astype(np.float64)
    theta = t_glob[None, :] * freq[:, None]
    power = (t_glob - 2048.0) / 256.0
    scale = base[:, None] ** power[None, :]
    if not for_q:
        scale = 1.0 / scale
    cos = np.cos(theta) * scale
    sin = np.sin(theta) * scale
    cs64 = np.concatenate([cos, cos], 0)
    ss64 = np.concatenate([-sin, sin], 0)
    CS = np.tile(np.tile(cs64, (2, 1)), (1, B)).astype(BF)
    SS = np.tile(np.tile(ss64, (2, 1)), (1, B)).astype(BF)
    return CS, SS


def _sel_exp_consts():
    selqk = np.zeros((128, 8 * 16), np.float32)
    expd = np.zeros((16, 8 * 128), np.float32)
    for t in range(8):
        for half in range(2):
            row = (2 * t + half) if t < 4 else (8 + 2 * (t - 4) + half)
            selqk[half * 64:(half + 1) * 64, t * 16 + row] = 1.0
            expd[row, t * 128 + half * 64: t * 128 + (half + 1) * 64] = 1.0
    p32 = np.zeros((128, 128), np.float32)
    for r in range(128):
        k = r + 32 if (r % 64) < 32 else r - 32
        p32[k, r] = 1.0
    idf = np.eye(128, dtype=np.float32)
    return selqk, expd, p32, idf


def _ffn_masks(c):
    mtok = np.zeros((128, 4 * 1024), np.float32)
    mkm = np.zeros((128, 8 * 512), np.float32)
    qpos = 512 * c + np.arange(512)
    kpos = 512 * (c - 1) + np.arange(1024)
    kvalid = (kpos >= 0)
    allowed = (kpos[None, :] <= qpos[:, None]) & (kpos[None, :] >= qpos[:, None] - 512) & kvalid[None, :]
    neg = -1e30
    for qt in range(4):
        blk = np.where(allowed[qt * 128:(qt + 1) * 128, :], 0.0, neg)
        mtok[:, qt * 1024:(qt + 1) * 1024] = blk
    for kt in range(8):
        blk = np.where(allowed[:, kt * 128:(kt + 1) * 128], 0.0, neg).T
        mkm[:, kt * 512:(kt + 1) * 512] = blk
    return mtok.astype(BF), mkm.astype(BF)


def _qs_col(qs):
    return np.tile(np.asarray(qs, np.float32), 2).reshape(128, 1)


def stage_l0ffn_B0(xf_cm, wqkv_l, q_scale_l, k_scale_l):
    selqk, expd, p32, idf = _sel_exp_consts()
    nc = _prog("B0", build_B0)
    maps = []
    for c in range(NCORE):
        xprev = np.zeros((D, B), BF) if c == 0 else np.ascontiguousarray(
            xf_cm[c - 1][:, SL - 1::SL])
        csq, ssq = _rot_tables(c, True)
        csk, ssk = _rot_tables(c, False)
        maps.append({
            "xf": xf_cm[c], "xprev": xprev, "wqkv": np.ascontiguousarray(wqkv_l, np.float32),
            "csq": csq, "ssq": ssq, "csk": csk, "ssk": ssk,
            "qscol": _qs_col(q_scale_l), "kscol": _qs_col(k_scale_l),
            "selqk": selqk, "expd": expd, "p32": p32, "idf": idf,
        })
    r = _run(nc, maps).results
    return ([r[c]["qf"] for c in range(NCORE)], [r[c]["kf"] for c in range(NCORE)],
            [r[c]["vt"] for c in range(NCORE)])


def stage_l0ffn_B1(qf, kf, vt, xf_cm, wo_l):
    selqk, expd, p32, idf = _sel_exp_consts()
    nc = _prog("B1", build_B1)
    maps = []
    for c in range(NCORE):
        kfx = np.zeros((INNER, 2 * TL), BF)
        vtx = np.zeros((2 * TL, INNER), BF)
        for b in range(B):
            if c > 0:
                kfx[:, b * 1024: b * 1024 + 512] = kf[c - 1][:, b * 512:(b + 1) * 512]
                vtx[b * 1024: b * 1024 + 512, :] = vt[c - 1][b * 512:(b + 1) * 512, :]
            kfx[:, b * 1024 + 512:(b + 1) * 1024] = kf[c][:, b * 512:(b + 1) * 512]
            vtx[b * 1024 + 512:(b + 1) * 1024, :] = vt[c][b * 512:(b + 1) * 512, :]
        mtok, mkm = _ffn_masks(c)
        maps.append({
            "qf": qf[c], "kfx": kfx, "vtx": vtx, "xfb": xf_cm[c],
            "wo": np.ascontiguousarray(wo_l, np.float32),
            "mtok": mtok, "mkm": mkm, "idf": idf, "onesc": _ONESC, "onesr": _ONESR,
        })
    r = _run(nc, maps).results
    return [r[c]["xm"] for c in range(NCORE)], [r[c]["xmb"] for c in range(NCORE)]


def build_B2(dbg=0):
    q = Q()
    nc = q.nc
    xmb = nc.dram_tensor("xmb", [D, TL], BF16, kind="ExternalInput").ap()
    xmprev = nc.dram_tensor("xmprev", [D, B], BF16, kind="ExternalInput").ap()
    xm = nc.dram_tensor("xm", [D, TL], F32, kind="ExternalInput").ap()
    lngb = nc.dram_tensor("lngb", [128, 16], F32, kind="ExternalInput").ap()
    w1r = nc.dram_tensor("w1r", [NJ * 128, D], F32, kind="ExternalInput").ap()
    b1r = nc.dram_tensor("b1r", [128, NJ], F32, kind="ExternalInput").ap()
    w2r = nc.dram_tensor("w2r", [HIDP, D], F32, kind="ExternalInput").ap()
    b2c = nc.dram_tensor("b2c", [128, NDT], F32, kind="ExternalInput").ap()
    onesc = nc.dram_tensor("onesc", [128, 1], F32, kind="ExternalInput").ap()
    onesr = nc.dram_tensor("onesr", [1, 128], F32, kind="ExternalInput").ap()
    xf2_o = nc.dram_tensor("xf2", [D, TL], F32, kind="ExternalOutput").ap()
    xf2b_o = nc.dram_tensor("xf2b", [D, TL], BF16, kind="ExternalOutput").ap()
    hdbg_o = nc.dram_tensor("hdbg", [HIDP, TL], BF16, kind="ExternalOutput").ap() if dbg == 2 else None

    xmt = [q.sbuf(f"xmb{t}", [128, TL], BF16) for t in range(NDT)]
    xpv = [q.sbuf(f"xpv{t}", [128, B], BF16) for t in range(NDT)]
    sh = [q.sbuf(f"sh{i}", [128, TL], BF16) for i in range(4)]
    hbuf = [q.sbuf(f"hb{j}", [128, TL], BF16) for j in range(NHT)]
    w1t = [q.sbuf(f"w1t{i}", [128, D], BF16) for i in range(2)]
    lngbs = q.sbuf("lngbs", [128, 16], F32)
    b1s = q.sbuf("b1s", [128, NJ], F32)
    b2s = q.sbuf("b2s", [128, NDT], F32)
    gl = [q.sbuf(f"gl{i}", [64, 512], F32) for i in range(2)]
    asb = q.sbuf("asb", [64, 512], F32)
    tmpf = q.sbuf("tmpf", [128, 512], F32)
    xmf = [q.sbuf(f"xmf{i}", [128, 512], F32) for i in range(2)]
    stg = [q.sbuf(f"stg{i}", [128, 512], F32) for i in range(2)]
    stgb = [q.sbuf(f"stgb{i}", [128, 512], BF16) for i in range(2)]
    cn = alloc_ln_common(q, TL)
    pp = (q.psum("ps_p0", [128, 512]), q.psum("ps_p1", [128, 512]))

    for t in range(NDT):
        q.dma("sync", xmt[t][:], xmb[t * 128:(t + 1) * 128, :])
        q.dma("sync", xpv[t][:], xmprev[t * 128:(t + 1) * 128, :])
    q.dma("sync", lngbs[:], lngb[:])
    q.dma("sync", b1s[:], b1r[:])
    q.dma("sync", b2s[:], b2c[:])
    load_ln_consts(q, cn, onesc, onesr)
    q.wait_loads("vector", "scalar", "tensor")

    _shift_tiles(q, xmt, xpv, sh)
    q.barrier(engs_from=("vector",), engs_to=("scalar", "tensor", "vector"))

    srcs = xmt[:4] + sh
    lnx = srcs
    nch = 4
    for ch in range(nch):
        sl = slice(ch * 512, (ch + 1) * 512)
        for t in range(NDT):
            sq = cn["sqs"][t % 2]
            if ch * NDT + t >= 2:
                q.wait("scalar", "mm", q.cnt["mm"] - 1)
            q.op("scalar", lambda e, a=sq, b=srcs[t], s=sl: e.activation(out=a[:], in_=b[:, s], func=AF.Square), inc=("ac", 1))
            q.op("tensor", lambda e, r=srcs[t], s=sl, st=(t == 0), sp=(t == NDT - 1), ps=cn["ps_st"][0]:
                 e.matmul(ps[:], cn["onesc_b"][:], r[:, s], start=st, stop=sp))
            q.wait("tensor", "ac", q.cnt["ac"])
            q.op("tensor", lambda e, r=sq, st=(t == 0), sp=(t == NDT - 1), ps=cn["ps_st"][1]:
                 e.matmul(ps[:], cn["onesc_b"][:], r[:], start=st, stop=sp), inc=("mm", 1))
        q.barrier(engs_from=("tensor",), engs_to=("vector",))
        q.op("vector", lambda e, s=sl: e.tensor_copy(out=cn["ssum"][:, s], in_=cn["ps_st"][0][:]))
        q.op("vector", lambda e, s=sl: e.tensor_copy(out=cn["sssq"][:, s], in_=cn["ps_st"][1][:]), inc=("ev", 1))
        q.barrier(engs_from=("vector",), engs_to=("tensor",))
    emit_ln_finalize(q, cn)
    for ch in range(nch):
        sl = slice(ch * 512, (ch + 1) * 512)
        q.op("tensor", lambda e, s=sl: e.matmul(cn["ps_bc"][0][:], cn["onesr_f"][:], cn["rB"][:, s], start=True, stop=True))
        q.op("tensor", lambda e, s=sl: e.matmul(cn["ps_bc"][1][:], cn["onesr_f"][:], cn["rC"][:, s], start=True, stop=True), inc=("mm", 1))
        q.barrier(engs_from=("tensor",), engs_to=("vector",))
        q.op("vector", lambda e, s=sl: e.tensor_copy(out=cn["bcs"][0][:, s], in_=cn["ps_bc"][0][:]))
        q.op("vector", lambda e, s=sl: e.tensor_copy(out=cn["bcs"][1][:, s], in_=cn["ps_bc"][1][:]), inc=("ev", 1))
        q.barrier(engs_from=("vector",), engs_to=("tensor",))
    for t in range(NDT):
        for ch in range(4):
            sl = slice(ch * 512, (ch + 1) * 512)
            q.op("vector", lambda e, a=srcs[t], s=sl: e.tensor_mul(tmpf[:], a[:, s], cn["bcs"][0][:, s]))
            q.op("vector", lambda e, s=sl: e.tensor_add(tmpf[:], tmpf[:], cn["bcs"][1][:, s]))
            q.op("vector", lambda e, t=t, o=lnx[t], s=sl: e.tensor_scalar(
                o[:, s], tmpf[:], lngbs[:, t:t + 1], lngbs[:, NDT + t:NDT + t + 1], op0=ALU.mult, op1=ALU.add), inc=("ev", 1))
    q.barrier()
    if dbg == 1:
        for t in range(NDT):
            q.dma("gpsimd", xf2b_o[t * 128:(t + 1) * 128, :], lnx[t][:], store=True)
        q.dma("gpsimd", xf2_o[:, :], xm[:, :], store=True)
        q.wait("gpsimd", "stg", q.cnt["stg"])
        q._flush_waits("gpsimd")
        return q.finish()

    for j in range(NJ):
        wb = w1t[j % 2]
        if j >= 2:
            q.wait("gpsimd", "mm", q.cnt["mm"] - 4)
        q.dma("gpsimd", wb[:], w1r[j * 128:(j + 1) * 128, :])
        ldg_j = q.cnt["ldg"]
        jj, half = j // 2, (j % 2) * 64
        for ch in range(4):
            sl = slice(ch * 512, (ch + 1) * 512)
            gi = j * 4 + ch
            ps = pp[gi % 2]
            if gi >= 2:
                q.wait("tensor", "ev", q.cnt["ev"] - 1)
            q.wait("tensor", "ldg", ldg_j)
            for kin in range(NDT):
                inc = ("mm", 1) if kin == NDT - 1 else None
                q.op("tensor", lambda e, kin=kin, s=sl, ps=ps, wb=wb, st=(kin == 0), sp=(kin == NDT - 1):
                     e.matmul(ps[:], wb[:, kin * 128:(kin + 1) * 128], lnx[kin][:, s], start=st, stop=sp), inc=inc)
            mmv = q.cnt["mm"]
            q.wait("scalar", "mm", mmv)
            if gi >= 2:
                q.wait("scalar", "ev", q.cnt["ev"] - 1)
            gli = gl[gi % 2]
            q.op("scalar", lambda e, ps=ps, j=j, g=gli: e.activation(
                out=g[:], in_=ps[64:128, :], func=AF.Gelu, bias=b1s[64:128, j:j + 1]), inc=("ac", 1))
            q.wait("vector", "ac", q.cnt["ac"])
            q.op("vector", lambda e, ps=ps, j=j: e.tensor_scalar(asb[:], ps[0:64, :], b1s[0:64, j:j + 1], None, op0=ALU.add))
            q.op("vector", lambda e, jj=jj, half=half, s=sl, g=gli: e.tensor_mul(hbuf[jj][half:half + 64, s], asb[:], g[:]), inc=("ev", 1))
    q.barrier()

    if dbg == 2:
        q.barrier()
        for jj in range(NHT):
            q.dma("gpsimd", hdbg_o[jj * 128:(jj + 1) * 128, :], hbuf[jj][:], store=True)
        q.dma("gpsimd", xf2_o[:, :], xm[:, :], store=True)
        for t in range(NDT):
            q.dma("sync", xf2b_o[t * 128:(t + 1) * 128, :], lnx[t][:], store=True)
        q.wait("gpsimd", "stg", q.cnt["stg"])
        q.wait("gpsimd", "sts", q.cnt["sts"])
        q._flush_waits("gpsimd")
        return q.finish()

    w2t = []
    for jj in range(NHT):
        if jj < 16:
            view = xmt[jj // 2][:, (jj % 2) * D:(jj % 2) * D + D]
        else:
            k = jj - 16
            view = sh[k // 2][:, (k % 2) * D:(k % 2) * D + D]
        w2t.append(view)
        q.dma("gpsimd", view, w2r[jj * 128:(jj + 1) * 128, :])
    q.wait_loads("tensor", "vector")
    hist3 = []
    lds0 = q.cnt["lds"]
    for ot in range(NDT):
        for ch in range(4):
            sl = slice(ch * 512, (ch + 1) * 512)
            gi = ot * 4 + ch
            ps = pp[gi % 2]
            if gi >= 2:
                q.wait("sync", "ev", hist3[gi - 2])
            q.dma("sync", xmf[gi % 2][:], xm[ot * 128:(ot + 1) * 128, sl])
            if gi >= 2:
                q.wait("tensor", "ev", hist3[gi - 2])
            for jj in range(NHT):
                inc = ("mm", 1) if jj == NHT - 1 else None
                q.op("tensor", lambda e, jj=jj, ot=ot, s=sl, ps=ps, st=(jj == 0), sp=(jj == NHT - 1):
                     e.matmul(ps[:], w2t[jj][:, ot * 128:(ot + 1) * 128], hbuf[jj][:, s], start=st, stop=sp), inc=inc)
            q.wait("vector", "mm", q.cnt["mm"])
            q.wait("vector", "lds", q.cnt["lds"])
            if gi >= 2:
                q.wait("vector", "stg", 16 * (2 * (gi - 1)))
            ii = gi % 2
            q.op("vector", lambda e, ps=ps, ot=ot, ii=ii: e.scalar_tensor_tensor(
                out=stg[ii][:], in0=ps[:], scalar=b2s[:, ot:ot + 1], in1=xmf[ii][:],
                op0=ALU.add, op1=ALU.add))
            q.op("vector", lambda e, ii=ii: e.tensor_copy(out=stgb[ii][:], in_=stg[ii][:]), inc=("ev", 1))
            hist3.append(q.cnt["ev"])
            q.wait("gpsimd", "ev", q.cnt["ev"])
            q.dma("gpsimd", xf2_o[ot * 128:(ot + 1) * 128, sl], stg[ii][:], store=True)
            q.dma("gpsimd", xf2b_o[ot * 128:(ot + 1) * 128, sl], stgb[ii][:], store=True)
    q.wait("gpsimd", "stg", q.cnt["stg"])
    q._flush_waits("gpsimd")
    return q.finish()


def _w1_prep(w1_l, b1_l):
    w1 = np.asarray(w1_l, np.float32)
    b1 = np.asarray(b1_l, np.float32)
    a, g = w1[:, :HID], w1[:, HID:]
    ba, bg = b1[:HID], b1[HID:]
    ap = np.zeros((D, HIDP), np.float32); ap[:, :HID] = a
    gp = np.zeros((D, HIDP), np.float32); gp[:, :HID] = g
    bap = np.zeros(HIDP, np.float32); bap[:HID] = ba
    bgp = np.zeros(HIDP, np.float32); bgp[:HID] = bg
    w1r = np.zeros((NJ * 128, D), np.float32)
    b1r = np.zeros((128, NJ), np.float32)
    for j in range(NJ):
        cols_a = ap[:, j * 64:(j + 1) * 64]
        cols_g = gp[:, j * 64:(j + 1) * 64]
        w128 = np.concatenate([cols_a, cols_g], axis=1)
        w1r[j * 128:(j + 1) * 128] = w128.reshape(NDT, 128, 128).transpose(1, 0, 2).reshape(128, D)
        b1r[0:64, j] = bap[j * 64:(j + 1) * 64]
        b1r[64:128, j] = bgp[j * 64:(j + 1) * 64]
    return w1r, b1r


def _w2_prep(w2_l, b2_l):
    w2 = np.zeros((HIDP, D), np.float32)
    w2[:HID] = np.asarray(w2_l, np.float32)
    b2c = np.asarray(b2_l, np.float32).reshape(NDT, 128).T.copy()
    return w2, b2c


def stage_l0ffn_B2(xm, xmb, w1_l, b1_l, w2_l, b2_l, lng_l, lnb_l):
    w1r, b1r = _w1_prep(w1_l, b1_l)
    w2r, b2c = _w2_prep(w2_l, b2_l)
    lngb = np.stack([np.asarray(lng_l, np.float32).reshape(NDT, 128).T,
                     np.asarray(lnb_l, np.float32).reshape(NDT, 128).T], axis=1).reshape(128, 16)
    lngb = np.concatenate([np.asarray(lng_l, np.float32).reshape(NDT, 128).T,
                           np.asarray(lnb_l, np.float32).reshape(NDT, 128).T], axis=1)
    nc = _prog("B2", build_B2)
    maps = []
    for c in range(NCORE):
        xmprev = np.zeros((D, B), BF) if c == 0 else np.ascontiguousarray(xmb[c - 1][:, SL - 1::SL])
        maps.append({
            "xmb": xmb[c], "xmprev": xmprev, "xm": xm[c], "lngb": lngb,
            "w1r": w1r, "b1r": b1r, "w2r": w2r, "b2c": b2c,
            "onesc": _ONESC, "onesr": _ONESR,
        })
    r = _run(nc, maps).results
    return [r[c]["xf2"] for c in range(NCORE)], [r[c]["xf2b"] for c in range(NCORE)]


def build_SW():
    q = Q()
    nc = q.nc
    y = nc.dram_tensor("y", [D, TL], F32, kind="ExternalInput").ap()
    res = nc.dram_tensor("res", [D, TL], F32, kind="ExternalInput").ap()
    onesc = nc.dram_tensor("onesc", [128, 1], F32, kind="ExternalInput").ap()
    onesr = nc.dram_tensor("onesr", [1, 128], F32, kind="ExternalInput").ap()
    x_o = nc.dram_tensor("x", [D, TL], F32, kind="ExternalOutput").ap()
    ln_o = nc.dram_tensor("ln", [D, TL], BF16, kind="ExternalOutput").ap()
    lnf_o = nc.dram_tensor("lnf", [D, TL], F32, kind="ExternalOutput").ap()

    yt = [q.sbuf(f"y{t}", [128, TL], F32) for t in range(NDT)]
    rt = [q.sbuf(f"r{t}", [128, TL], F32) for t in range(NDT)]
    xb = [q.sbuf(f"b{t}", [128, TL], BF16) for t in range(NDT)]
    cn = alloc_ln_common(q, TL)
    for t in range(NDT):
        q.dma("sync", yt[t][:], y[t * 128:(t + 1) * 128, :])
        q.dma("sync", rt[t][:], res[t * 128:(t + 1) * 128, :])
    load_ln_consts(q, cn, onesc, onesr)
    q.wait_loads("vector", "scalar", "tensor")
    emit_ln_cm(q, yt, xb, cn, TL)
    for t in range(NDT):
        q.op("vector", lambda e, t=t: e.tensor_add(rt[t][:], rt[t][:], yt[t][:]), inc=("ev", 1))
        q.wait("sync", "ev", q.cnt["ev"])
        q.dma("sync", x_o[t * 128:(t + 1) * 128, :], rt[t][:], store=True)
    q.wait("vector", "sts", q.cnt["sts"])
    emit_ln_cm(q, rt, xb, cn, TL)
    q.barrier(engs_from=("vector",), engs_to=("gpsimd", "sync"))
    for t in range(NDT):
        q.dma("gpsimd", ln_o[t * 128:(t + 1) * 128, :], xb[t][:], store=True)
        q.dma("sync", lnf_o[t * 128:(t + 1) * 128, :], rt[t][:], store=True)
    q.wait("gpsimd", "stg", q.cnt["stg"])
    q.wait("gpsimd", "sts", q.cnt["sts"])
    q._flush_waits("gpsimd")
    return q.finish()


_PROG = {}


def _prog(name, builder):
    if name not in _PROG:
        _PROG[name] = builder()
    return _PROG[name]


def stage_sandwich(y_cm, res_cm):
    nc = _prog("SW", build_SW)
    maps = [{"y": y_cm[c], "res": res_cm[c], "onesc": _ONESC, "onesr": _ONESR}
            for c in range(NCORE)]
    r = _run(nc, maps).results
    return ([r[c]["x"] for c in range(NCORE)], [r[c]["ln"] for c in range(NCORE)],
            [r[c]["lnf"] for c in range(NCORE)])


def stage_l0ffn(ln1_cm, x1_cm, ffn):
    xf = ln1_cm
    xf_f32 = None
    for l in range(2):
        qf, kf, vt = stage_l0ffn_B0(xf, ffn["wqkv"][l], ffn["q_scale"][l], ffn["k_scale"][l])
        xm, xmb = stage_l0ffn_B1(qf, kf, vt, xf, ffn["wo"][l])
        xf2, xf2b = stage_l0ffn_B2(xm, xmb, ffn["w1"][l], ffn["b1"][l], ffn["w2"][l], ffn["b2"][l],
                                   ffn["ln_g"][l], ffn["ln_b"][l])
        xf = xf2b
        xf_f32 = xf2
    return stage_sandwich(xf_f32, x1_cm)


H16 = 16
DK = 32
DV = 64
PAIRS = [(t, s) for t in range(B) for s in range(t + 1)]


def build_D(dbg=0):
    q = Q()
    nc = q.nc
    xn_i = nc.dram_tensor("xn", [D, TL], BF16, kind="ExternalInput").ap()
    convw = nc.dram_tensor("convw", [128, 32], F32, kind="ExternalInput").ap()
    wq = nc.dram_tensor("wq", [D, 512], F32, kind="ExternalInput").ap()
    wkg = nc.dram_tensor("wkg", [D, 512], F32, kind="ExternalInput").ap()
    wv = nc.dram_tensor("wv", [D, D], F32, kind="ExternalInput").ap()
    wg = nc.dram_tensor("wg", [D, D], F32, kind="ExternalInput").ap()
    wout = nc.dram_tensor("wout", [D, D], F32, kind="ExternalInput").ap()
    bgc = nc.dram_tensor("bgc", [128, NDT], F32, kind="ExternalInput").ap()
    augc = nc.dram_tensor("augc", [128, 4], F32, kind="ExternalInput").ap()
    selH = nc.dram_tensor("selH", [128, 64], F32, kind="ExternalInput").ap()
    selV = nc.dram_tensor("selV", [128, 128], F32, kind="ExternalInput").ap()
    expH = nc.dram_tensor("expH", [16, 8 * 128], F32, kind="ExternalInput").ap()
    l1o_o = nc.dram_tensor("l1o", [D, TL], F32, kind="ExternalOutput").ap()
    gsd = nc.dram_tensor("gsd", [D, TL], BF16, kind="Internal").ap()
    dbg_o = nc.dram_tensor("dbg", [D, TL], BF16, kind="ExternalOutput").ap() if dbg else None

    xnb = [q.sbuf(f"xn{t}", [128, TL], BF16) for t in range(NDT)]
    wts = {}
    for nm, wd in (("wq", 4), ("wkg", 4), ("wv", 8), ("wg", 8)):
        wts[nm] = [q.sbuf(f"{nm}{t}", [128, wd * 128], BF16) for t in range(NDT)]
    qb = [q.sbuf(f"qb{t}", [128, TL], BF16) for t in range(4)]
    gkf = [q.sbuf(f"gk{t}", [128, TL], F32) for t in range(4)]
    k1b = [q.sbuf(f"k1{t}", [128, TL], BF16) for t in range(4)]
    vb = [q.sbuf(f"vb{t}", [128, TL], BF16) for t in range(NDT)]
    gss = [q.sbuf(f"gss{i}", [128, 512], BF16) for i in range(2)]
    gsl = [q.sbuf(f"gsl{i}", [128, 512], BF16) for i in range(2)]
    wob = [qb[t][:, 0:D] for t in range(4)] + [k1b[t][:, 0:D] for t in range(4)]
    ob = [gkf[dt // 4][:, (dt % 4) * 512:(dt % 4 + 1) * 512] for dt in range(NDT)]
    obb = [gkf[2][:].bitcast(BF16)[:, dt * 512:(dt + 1) * 512] for dt in range(NDT)]
    gob = [gkf[3][:].bitcast(BF16)[:, dt * 512:(dt + 1) * 512] for dt in range(NDT)]
    abuf = q.sbuf("abuf", [16, 10 * 512], BF16)
    augw = q.sbuf("augw", [16, 4 * 512], BF16)
    cw = q.sbuf("cw", [128, 32], F32)
    bgs = q.sbuf("bgs", [128, NDT], F32)
    augs = q.sbuf("augs", [128, 4], F32)
    selHb = q.sbuf("selHb", [128, 64], BF16)
    selAb = q.sbuf("selAb", [128, 64], BF16)
    selVb = q.sbuf("selVb", [128, 128], BF16)
    selVf = q.sbuf("selVf", [128, 128], F32)
    expHf = q.sbuf("expHf", [16, 8 * 128], F32)
    expHb = q.sbuf("expHb", [16, 8 * 128], BF16)
    scr = [q.sbuf(f"scr{i}", [128, 512], F32) for i in range(3)]
    scrb = [gss[0], gss[1], gsl[0], gsl[1]]
    escr = scr[2]
    nrA = wts["wg"][2][0:16, :].bitcast(F32)
    nrB = wts["wg"][3][0:16, :].bitcast(F32)
    nrC = wts["wg"][4][0:16, :].bitcast(F32)
    stg = [wts["wg"][0][:].bitcast(F32), wts["wg"][1][:].bitcast(F32)]
    ps_p = [q.psum(f"psp{i}", [128, 512]) for i in range(2)]
    ps_A = [q.psum(f"psA{i}", [16, 512]) for i in range(2)]
    ps_B = [q.psum(f"psB{i}", [128, 512]) for i in range(2)]
    ps_s = [q.psum(f"pss{i}", [16, 512]) for i in range(2)]

    for t in range(NDT):
        q.dma("sync", xnb[t][:], xn_i[t * 128:(t + 1) * 128, :])
        q.dma("gpsimd", wts["wq"][t][:], wq[t * 128:(t + 1) * 128, :])
        q.dma("gpsimd", wts["wkg"][t][:], wkg[t * 128:(t + 1) * 128, :])
        q.dma("gpsimd", wts["wv"][t][:], wv[t * 128:(t + 1) * 128, :])
        q.dma("gpsimd", wts["wg"][t][:], wg[t * 128:(t + 1) * 128, :])
    q.dma("sync", cw[:], convw[:])
    q.dma("sync", bgs[:], bgc[:])
    q.dma("sync", augs[:], augc[:])
    q.dma("gpsimd", selHb[:], selH[:])
    q.dma("gpsimd", selVb[:], selV[:])
    q.dma("sync", selVf[:], selV[:])
    q.dma("sync", expHf[:], expH[:])
    q.dma("gpsimd", expHb[:], expH[:])
    q.wait_loads("vector", "scalar", "tensor")

    for dt in range(NDT):
        for b in reversed(range(B)):
            terms = [(bp, 3 - (b - bp)) for bp in range(max(0, b - 3), b + 1)]
            first_bp, first_k = terms[0]
            inc0 = ("ev", 1) if len(terms) == 1 else None
            q.op("vector", lambda e, dt=dt, bp=first_bp, k=first_k: e.tensor_scalar(
                scr[0][:], xnb[dt][:, bp * SL:(bp + 1) * SL], cw[:, dt * 4 + k:dt * 4 + k + 1], None, op0=ALU.mult), inc=inc0)
            si = 0
            for ti, (bp, k) in enumerate(terms[1:]):
                inc1 = ("ev", 1) if ti == len(terms) - 2 else None
                q.op("vector", lambda e, dt=dt, bp=bp, k=k, si=si: e.scalar_tensor_tensor(
                    out=scr[1 - si][:], in0=xnb[dt][:, bp * SL:(bp + 1) * SL],
                    scalar=cw[:, dt * 4 + k:dt * 4 + k + 1], in1=scr[si][:], op0=ALU.mult, op1=ALU.add), inc=inc1)
                si = 1 - si
            q.wait("scalar", "ev", q.cnt["ev"])
            q.op("scalar", lambda e, dt=dt, b=b, si=si: e.activation(
                out=xnb[dt][:, b * SL:(b + 1) * SL], in_=scr[si][:], func=AF.Silu), inc=("ac", 1))
            q.wait("vector", "ac", q.cnt["ac"])
    q.barrier()
    if dbg == 1:
        for t in range(NDT):
            q.dma("gpsimd", dbg_o[t * 128:(t + 1) * 128, :], xnb[t][:], store=True)
        q.op("vector", lambda e: e.memset(stg[0], 0.0), inc=("ev", 1))
        q.wait("gpsimd", "ev", q.cnt["ev"])
        for t in range(NDT):
            for ch in range(4):
                q.dma("gpsimd", l1o_o[t * 128:(t + 1) * 128, ch * 512:(ch + 1) * 512], stg[0], store=True)
        q.wait("gpsimd", "stg", q.cnt["stg"])
        q._flush_waits("gpsimd")
        return q.finish()

    def evac_copy(dst):
        def f(ot, sl, ps, mmv):
            q.wait("vector", "mm", mmv)
            q.op("vector", lambda e, o=dst[ot], s=sl, ps=ps: e.tensor_copy(out=o[:, s], in_=ps[:]), inc=("ev", 1))
            return [("ev", q.cnt["ev"])]
        return f

    emit_proj(q, wts["wq"], xnb, ps_p, evac_copy(qb), n_out_tiles=4)

    def evac_kg(ot, sl, ps, mmv):
        q.wait("scalar", "mm", mmv)
        q.wait("scalar", "ev", q.cnt["ev"])
        q.op("scalar", lambda e, ps=ps: e.activation(out=scr[0][:], in_=ps[:], func=AF.Abs), inc=("ac", 1))
        q.wait("vector", "ac", q.cnt["ac"])
        q.op("vector", lambda e, ps=ps: e.tensor_scalar_min(scr[1][:], ps[:], 0.0), inc=("ev", 1))
        q.op("scalar", lambda e: e.activation(out=scr[2][:], in_=scr[0][:], func=AF.Exp, scale=-1.0))
        q.op("scalar", lambda e: e.activation(out=scr[2][:], in_=scr[2][:], func=AF.Ln, bias=1.0), inc=("ac", 1))
        q.wait("vector", "ac", q.cnt["ac"])
        q.op("vector", lambda e: e.tensor_sub(scr[1][:], scr[1][:], scr[2][:]))
        q.op("vector", lambda e, o=gkf[ot], s=sl: e.tensor_scalar(o[:, s], scr[1][:], 1.0 / 16.0, None, op0=ALU.mult), inc=("ev", 1))
        return [("ev", q.cnt["ev"]), ("ac", q.cnt["ac"])]

    emit_proj(q, wts["wkg"], xnb, ps_p, evac_kg, n_out_tiles=4)
    emit_proj(q, wts["wv"], xnb, ps_p, evac_copy(vb), n_out_tiles=8)

    g_ctr = [0]

    def evac_g(ot, sl, ps, mmv):
        gi = g_ctr[0]; g_ctr[0] += 1
        ii = gi % 2
        q.wait("scalar", "mm", mmv)
        if gi >= 2:
            q.wait("scalar", "stg", 16 * (gi - 1))
        q.op("scalar", lambda e, s=sl, ps=ps, ot=ot, ii=ii: e.activation(
            out=gss[ii][:], in_=ps[:], func=AF.Silu, bias=bgs[:, ot:ot + 1]), inc=("ac", 1))
        q.wait("gpsimd", "ac", q.cnt["ac"])
        q.dma("gpsimd", gsd[ot * 128:(ot + 1) * 128, sl], gss[ii][:], store=True)
        return [("ac", q.cnt["ac"])]

    emit_proj(q, wts["wg"], xnb, ps_p, evac_g, n_out_tiles=8)
    q.barrier()
    if dbg == 2:
        for t in range(4):
            q.dma("gpsimd", dbg_o[t * 128:(t + 1) * 128, :], qb[t][:], store=True)
        for t in range(4):
            q.dma("sync", l1o_o[(4 + t) * 128:(5 + t) * 128, :], gkf[t][:], store=True)
        q.op("vector", lambda e: e.memset(stg[0], 0.0), inc=("ev", 1))
        q.wait("gpsimd", "ev", q.cnt["ev"])
        for t in range(4):
            for ch in range(4):
                q.dma("gpsimd", l1o_o[t * 128:(t + 1) * 128, ch * 512:(ch + 1) * 512], stg[0], store=True)
        q.wait("gpsimd", "stg", q.cnt["stg"])
        q.wait("gpsimd", "sts", q.cnt["sts"])
        q._flush_waits("gpsimd")
        return q.finish()

    for dt in range(4):
        for b in range(B):
            bs = slice(b * SL, (b + 1) * SL)
            q.op("scalar", lambda e, dt=dt, bs=bs: e.activation(out=escr[:], in_=gkf[dt][:, bs], func=AF.Exp), inc=("ac", 1))
            q.wait("vector", "ac", q.cnt["ac"])
            q.op("vector", lambda e, dt=dt, bs=bs: e.tensor_scalar(k1b[dt][:, bs], escr[:], -1.0, 1.0, op0=ALU.mult, op1=ALU.add), inc=("ev", 1))
            q.wait("scalar", "ev", q.cnt["ev"])
    for dt in range(4):
        q.op("vector", lambda e, dt=dt: e.tensor_scalar(
            selAb[:, dt * 16:(dt + 1) * 16], selHb[:, dt * 16:(dt + 1) * 16],
            augs[:, dt:dt + 1], None, op0=ALU.mult), inc=("ev", 1))
    q.barrier(engs_from=("vector",), engs_to=("tensor",))
    for t in range(B):
        pa = ps_A[t % 2]
        if t >= 2:
            q.wait("tensor", "ev", q.cnt["ev"] - 1)
        for dt in range(4):
            q.op("vector", lambda e, dt=dt, t=t: e.tensor_mul(
                scrb[dt][:], qb[dt][:, t * SL:(t + 1) * SL], k1b[dt][:, t * SL:(t + 1) * SL]))
        q.barrier(engs_from=("vector",), engs_to=("tensor",))
        for dt in range(4):
            q.op("tensor", lambda e, dt=dt, pa=pa, st=(dt == 0), sp=(dt == 3): e.matmul(
                pa[:], selAb[:, dt * 16:(dt + 1) * 16], scrb[dt][:], start=st, stop=sp),
                inc=("mm", 1) if dt == 3 else None)
        q.wait("vector", "mm", q.cnt["mm"])
        q.op("vector", lambda e, t=t, pa=pa: e.tensor_copy(out=augw[:, t * 512:(t + 1) * 512], in_=pa[:]), inc=("ev", 1))
    q.barrier()
    for dt in range(4):
        for b in range(1, B):
            q.op("vector", lambda e, dt=dt, b=b: e.tensor_add(
                gkf[dt][:, b * SL:(b + 1) * SL], gkf[dt][:, b * SL:(b + 1) * SL],
                gkf[dt][:, (b - 1) * SL:b * SL]))
    q.barrier(engs_from=("vector",), engs_to=("scalar",))
    for dt in range(4):
        for b in range(B):
            bs = slice(b * SL, (b + 1) * SL)
            q.op("scalar", lambda e, dt=dt, bs=bs: e.activation(out=escr[:], in_=gkf[dt][:, bs], func=AF.Exp), inc=("ac", 1))
            q.wait("vector", "ac", q.cnt["ac"])
            q.op("vector", lambda e, dt=dt, bs=bs: e.tensor_mul(qb[dt][:, bs], qb[dt][:, bs], escr[:]), inc=("ev", 1))
            q.wait("scalar", "ev", q.cnt["ev"])
            q.op("scalar", lambda e, dt=dt, bs=bs: e.activation(out=escr[:], in_=gkf[dt][:, bs], func=AF.Exp, scale=-1.0), inc=("ac", 1))
            q.wait("vector", "ac", q.cnt["ac"])
            q.op("vector", lambda e, dt=dt, bs=bs: e.tensor_mul(k1b[dt][:, bs], k1b[dt][:, bs], escr[:]), inc=("ev", 1))
            q.wait("scalar", "ev", q.cnt["ev"])
    q.barrier()
    for pi, (t, s) in enumerate(PAIRS):
        pa = ps_A[pi % 2]
        if pi >= 2:
            q.wait("tensor", "ev", q.cnt["ev"] - 1)
        for dt in range(4):
            q.op("vector", lambda e, dt=dt, t=t, s=s: e.tensor_mul(
                scrb[dt][:], qb[dt][:, t * SL:(t + 1) * SL], k1b[dt][:, s * SL:(s + 1) * SL]))
        q.barrier(engs_from=("vector",), engs_to=("tensor",))
        for dt in range(4):
            q.op("tensor", lambda e, dt=dt, pa=pa, st=(dt == 0), sp=(dt == 3): e.matmul(
                pa[:], selHb[:, dt * 16:(dt + 1) * 16], scrb[dt][:], start=st, stop=sp),
                inc=("mm", 1) if dt == 3 else None)
        q.wait("vector", "mm", q.cnt["mm"])
        q.op("vector", lambda e, pi=pi, pa=pa: e.tensor_scalar(
            abuf[:, pi * 512:(pi + 1) * 512], pa[:], 0.17677669529663687, None, op0=ALU.mult), inc=("ev", 1))
    q.barrier()
    if dbg == 4:
        q.dma("gpsimd", dbg_o[0:16, 0:2048], abuf[:, 0:2048], store=True)
        q.dma("gpsimd", dbg_o[16:32, 0:2048], abuf[:, 2048:4096], store=True)
        q.dma("gpsimd", dbg_o[32:48, 0:1024], abuf[:, 4096:5120], store=True)
        q.dma("gpsimd", dbg_o[48:64, 0:2048], augw[:], store=True)
        q.op("vector", lambda e: e.memset(stg[0], 0.0), inc=("ev", 1))
        q.wait("gpsimd", "ev", q.cnt["ev"])
        for t in range(NDT):
            for ch in range(4):
                q.dma("gpsimd", l1o_o[t * 128:(t + 1) * 128, ch * 512:(ch + 1) * 512], stg[0], store=True)
        q.wait("gpsimd", "stg", q.cnt["stg"])
        q._flush_waits("gpsimd")
        return q.finish()
    if dbg == 3:
        for t in range(4):
            q.dma("gpsimd", dbg_o[t * 128:(t + 1) * 128, :], qb[t][:], store=True)
            q.dma("gpsimd", dbg_o[(4 + t) * 128:(5 + t) * 128, :], k1b[t][:], store=True)
        q.op("vector", lambda e: e.memset(stg[0], 0.0), inc=("ev", 1))
        q.wait("gpsimd", "ev", q.cnt["ev"])
        for t in range(NDT):
            for ch in range(4):
                q.dma("gpsimd", l1o_o[t * 128:(t + 1) * 128, ch * 512:(ch + 1) * 512], stg[0], store=True)
        q.wait("gpsimd", "stg", q.cnt["stg"])
        q._flush_waits("gpsimd")
        return q.finish()
    for t in range(NDT):
        q.dma("gpsimd", wob[t], wout[t * 128:(t + 1) * 128, :])
    q.wait("tensor", "ldg", q.cnt["ldg"])

    pair_idx = {(t, s): i for i, (t, s) in enumerate(PAIRS)}
    for t in range(B):
        sl = slice(t * SL, (t + 1) * SL)
        for dt in range(NDT):
            for s in range(t + 1):
                pi = pair_idx[(t, s)]
                pb = ps_B[0]
                q.op("tensor", lambda e, dt=dt, pi=pi, pb=pb: e.matmul(
                    pb[:], expHb[:, dt * 128:(dt + 1) * 128], abuf[:, pi * 512:(pi + 1) * 512],
                    start=True, stop=True), inc=("mm", 1))
                q.wait("vector", "mm", q.cnt["mm"])
                if s == 0:
                    q.op("vector", lambda e, dt=dt, s=s, pb=pb: e.tensor_mul(
                        ob[dt][:], vb[dt][:, s * SL:(s + 1) * SL], pb[:]), inc=("ev", 1))
                else:
                    q.op("vector", lambda e, dt=dt, s=s, pb=pb: e.tensor_mul(scr[0][:], vb[dt][:, s * SL:(s + 1) * SL], pb[:]))
                    q.op("vector", lambda e, dt=dt: e.tensor_add(ob[dt][:], ob[dt][:], scr[0][:]), inc=("ev", 1))
                q.wait("tensor", "ev", q.cnt["ev"])
            pb = ps_B[1]
            q.op("tensor", lambda e, dt=dt, t=t, pb=pb: e.matmul(
                pb[:], expHb[:, dt * 128:(dt + 1) * 128], augw[:, t * 512:(t + 1) * 512],
                start=True, stop=True), inc=("mm", 1))
            q.wait("vector", "mm", q.cnt["mm"])
            q.op("vector", lambda e, dt=dt, s=sl, pb=pb: e.tensor_mul(scr[1][:], vb[dt][:, s], pb[:]), inc=("ev", 1))
            q.wait("scalar", "ev", q.cnt["ev"])
            q.op("scalar", lambda e: e.activation(out=scr[2][:], in_=scr[1][:], func=AF.Sigmoid), inc=("ac", 1))
            q.wait("vector", "ac", q.cnt["ac"])
            q.op("vector", lambda e, dt=dt: e.tensor_add(ob[dt][:], ob[dt][:], scr[2][:]))
            q.op("vector", lambda e, dt=dt: e.tensor_copy(out=obb[dt][:], in_=ob[dt][:]), inc=("ev", 1))
            q.wait("tensor", "ev", q.cnt["ev"])
        q.barrier(engs_from=("vector",), engs_to=("tensor", "scalar"))
        if dbg == 5 and t == 1:
            q.barrier()
            for dt in range(NDT):
                q.dma("gpsimd", l1o_o[dt * 128:(dt + 1) * 128, 0:512], ob[dt], store=True)
            q.wait("gpsimd", "stg", q.cnt["stg"])
            q._flush_waits("gpsimd")
            return q.finish()
        for dt in range(NDT):
            q.op("tensor", lambda e, dt=dt, st=(dt == 0), sp=(dt == NDT - 1): e.matmul(
                ps_s[0][:], selVf[:, dt * 16:(dt + 1) * 16], ob[dt], start=st, stop=sp), inc=("mm", 1))
        q.barrier(engs_from=("tensor",), engs_to=("vector",))
        q.op("vector", lambda e: e.tensor_scalar(nrA, ps_s[0][:], -1.0 / DV, None, op0=ALU.mult), inc=("ev", 1))
        q.wait("vector", "ev", q.cnt["ev"])
        q.barrier(engs_from=("vector",), engs_to=("tensor",))
        for dt in range(NDT):
            pb = ps_B[dt % 2]
            if dt >= 2:
                q.wait("tensor", "ev", q.cnt["ev"] - 1)
                q.wait("tensor", "ac", q.cnt["ac"] - 1)
            q.op("tensor", lambda e, dt=dt, pb=pb: e.matmul(
                pb[:], expHf[:, dt * 128:(dt + 1) * 128], nrA, start=True, stop=True), inc=("mm", 1))
            q.wait("vector", "mm", q.cnt["mm"])
            q.op("vector", lambda e, dt=dt, pb=pb: e.tensor_add(ob[dt], ob[dt], pb[:]))
            q.op("vector", lambda e, dt=dt: e.tensor_copy(out=obb[dt], in_=ob[dt]), inc=("ev", 1))
            sq = scrb[dt % 2]
            q.wait("scalar", "ev", q.cnt["ev"])
            if dt >= 2:
                q.wait("scalar", "mm", q.cnt["mm"])
            q.op("scalar", lambda e, dt=dt, sq=sq: e.activation(out=sq[:], in_=obb[dt], func=AF.Square), inc=("ac", 1))
            q.wait("tensor", "ac", q.cnt["ac"])
            q.op("tensor", lambda e, sq=sq, st=(dt == 0), sp=(dt == NDT - 1), dt=dt: e.matmul(
                ps_s[1][:], selVb[:, dt * 16:(dt + 1) * 16], sq[:], start=st, stop=sp), inc=("mm", 1))
        q.barrier(engs_from=("tensor",), engs_to=("vector",))
        q.op("vector", lambda e: e.tensor_scalar(nrB, ps_s[1][:], 1.0 / DV, None, op0=ALU.mult), inc=("ev", 1))
        q.wait("vector", "ev", q.cnt["ev"])
        q.op("vector", lambda e: e.tensor_scalar(nrB, nrB, 1e-5, None, op0=ALU.add), inc=("ev", 1))
        q.barrier(engs_from=("vector",), engs_to=("scalar", "tensor"))
        q.op("scalar", lambda e: e.activation(out=nrB, in_=nrB, func=AF.Sqrt), inc=("ac", 1))
        q.wait("vector", "ac", q.cnt["ac"])
        q.op("vector", lambda e: e.reciprocal(out=nrB, in_=nrB), inc=("ev", 1))
        q.wait("vector", "ev", q.cnt["ev"])
        q.barrier(engs_from=("vector",), engs_to=("tensor",))
        if dbg == 7 and t == 1:
            q.barrier()
            q.dma("gpsimd", l1o_o[0:16, 0:512], nrA, store=True)
            q.dma("gpsimd", l1o_o[16:32, 0:512], nrB, store=True)
            q.wait("gpsimd", "stg", q.cnt["stg"])
            q._flush_waits("gpsimd")
            return q.finish()
        for dt in range(NDT):
            pb = ps_B[dt % 2]
            pa = ps_p[dt % 2]
            if dt >= 2:
                q.wait("tensor", "ev", q.cnt["ev"] - 1)
                q.wait("sync", "ev", q.cnt["ev"] - 1)
            q.dma("sync", gsl[dt % 2][:], gsd[dt * 128:(dt + 1) * 128, sl])
            q.op("tensor", lambda e, dt=dt, pb=pb: e.matmul(
                pb[:], expHf[:, dt * 128:(dt + 1) * 128], nrB, start=True, stop=True), inc=("mm", 1))
            q.wait("vector", "mm", q.cnt["mm"])
            q.wait("vector", "lds", q.cnt["lds"])
            q.op("vector", lambda e, dt=dt, pb=pb: e.tensor_mul(scr[0][:], ob[dt], pb[:]))
            q.op("vector", lambda e, dt=dt: e.tensor_mul(gob[dt], gsl[dt % 2][:], scr[0][:]), inc=("ev", 1))
        q.barrier()
        if dbg == 6 and t == 1:
            for dt in range(NDT):
                q.dma("gpsimd", dbg_o[dt * 128:(dt + 1) * 128, 0:512], gob[dt], store=True)
            q.op("vector", lambda e: e.memset(stg[0], 0.0), inc=("ev", 1))
            q.wait("gpsimd", "ev", q.cnt["ev"])
            for tt in range(NDT):
                for ch in range(4):
                    q.dma("gpsimd", l1o_o[tt * 128:(tt + 1) * 128, ch * 512:(ch + 1) * 512], stg[0], store=True)
            q.wait("gpsimd", "stg", q.cnt["stg"])
            q._flush_waits("gpsimd")
            return q.finish()
        for ot in range(NDT):
            psw = ps_p[ot % 2]
            if ot >= 2:
                q.wait("tensor", "stg", q.cnt["stg"] - 16)
            for kin in range(NDT):
                inc = ("mm", 1) if kin == NDT - 1 else None
                q.op("tensor", lambda e, kin=kin, ot=ot, psw=psw, st=(kin == 0), sp=(kin == NDT - 1):
                     e.matmul(psw[:], wob[kin][:, ot * 128:(ot + 1) * 128], gob[kin], start=st, stop=sp), inc=inc)
            q.wait("vector", "mm", q.cnt["mm"])
            q.op("vector", lambda e, psw=psw, ii=ot % 2: e.tensor_copy(out=stg[ii], in_=psw[:]), inc=("ev", 1))
            q.wait("gpsimd", "ev", q.cnt["ev"])
            q.dma("gpsimd", l1o_o[ot * 128:(ot + 1) * 128, sl], stg[ot % 2], store=True)
        q.barrier()

    q.wait("gpsimd", "stg", q.cnt["stg"])
    q._flush_waits("gpsimd")
    return q.finish()


def _d_consts():
    selH = np.zeros((128, 64), np.float32)
    selV = np.zeros((128, 128), np.float32)
    expH = np.zeros((16, 8 * 128), np.float32)
    for dt in range(4):
        for r4 in range(4):
            h = 4 * dt + r4
            selH[r4 * 32:(r4 + 1) * 32, dt * 16 + h] = 1.0
    for dt in range(8):
        for r2 in range(2):
            h = 2 * dt + r2
            selV[r2 * 64:(r2 + 1) * 64, dt * 16 + h] = 1.0
            expH[h, dt * 128 + r2 * 64: dt * 128 + (r2 + 1) * 64] = 1.0
    return selH, selV, expH


def stage_l1attn(ln2_cm, att):
    selH, selV, expH = _d_consts()
    nc = _prog("D", build_D)
    bgc = np.asarray(att["bg"], np.float32).reshape(NDT, 128).T.copy()
    augc = np.asarray(att["aug_balance"], np.float32).reshape(4, 128).T.copy()
    cwc = np.asarray(att["conv_w"], np.float32).reshape(NDT, 128, 4).transpose(1, 0, 2).reshape(128, 32)
    maps = []
    for c in range(NCORE):
        maps.append({
            "xn": ln2_cm[c], "convw": cwc,
            "wq": np.asarray(att["Wq"], np.float32), "wkg": np.asarray(att["Wkg"], np.float32),
            "wv": np.asarray(att["Wv"], np.float32), "wg": np.asarray(att["Wg"], np.float32),
            "wout": np.asarray(att["Wout"], np.float32),
            "bgc": bgc, "augc": augc, "selH": selH, "selV": selV, "expH": expH,
        })
    r = _run(nc, maps).results
    return [r[c]["l1o"] for c in range(NCORE)]


EXP = 8
HME = 4096
NJ_E = HME // 128
BLK = 512
NBLK = TL // BLK
NC2 = BLK // 512


def build_E(dbg=0):
    q = Q()
    nc = q.nc
    ln3 = nc.dram_tensor("ln3", [D, TL], BF16, kind="ExternalInput").ap()
    ln3f = nc.dram_tensor("ln3f", [D, TL], F32, kind="ExternalInput").ap()
    gam = nc.dram_tensor("gam", [128, NDT], F32, kind="ExternalInput").ap()
    bet = nc.dram_tensor("bet", [128, NDT], F32, kind="ExternalInput").ap()
    rw = nc.dram_tensor("rw", [128, NDT * EXP], F32, kind="ExternalInput").ap()
    w1r = nc.dram_tensor("w1r", [EXP * NJ_E * 128, D], F32, kind="ExternalInput").ap()
    w2r = nc.dram_tensor("w2r", [EXP * HME, D], F32, kind="ExternalInput").ap()
    idf = nc.dram_tensor("idf", [128, 128], F32, kind="ExternalInput").ap()
    onesc = nc.dram_tensor("onesc", [128, 1], F32, kind="ExternalInput").ap()
    onesr = nc.dram_tensor("onesr", [1, 128], F32, kind="ExternalInput").ap()
    ys_o = nc.dram_tensor("ys", [D, TL], F32, kind="ExternalOutput").ap()
    wd_d = nc.dram_tensor("wd_d", [EXP, TL], F32, kind="Internal").ap()
    dbg_o = nc.dram_tensor("dbgE", [D, TL], BF16, kind="ExternalOutput").ap() if dbg else None

    l3b = [q.sbuf(f"l3{t}", [128, TL], BF16) for t in range(NDT)]
    xnb = [q.sbuf(f"xnb{t}", [128, TL], BF16) for t in range(NDT)]
    hb = [q.sbuf(f"hb{j}", [128, BLK], BF16) for j in range(NJ_E)]
    acc = [q.sbuf(f"acc{t}", [128, BLK], F32) for t in range(NDT)]
    w1t = [q.sbuf(f"w1t{i}", [128, D], BF16) for i in range(2)]
    w2t = [q.sbuf(f"w2t{i}", [128, D], BF16) for i in range(2)]
    rwt = q.sbuf("rwt", [128, EXP * NDT], BF16)
    gamс = q.sbuf("gamc", [128, NDT], F32)
    betc = q.sbuf("betc", [128, NDT], F32)
    idb = q.sbuf("idb", [128, 128], BF16)
    logr = q.sbuf("logr", [8, TL], BF16)
    ltok = [q.sbuf(f"lt{i}", [128, EXP], BF16) for i in range(16)]
    wdr = q.sbuf("wdr", [8, TL], F32)
    wdBs = [q.sbuf(f"wdB{i}", [128, 512], F32) for i in range(2)]
    wrow = [q.sbuf(f"wrow{i}", [1, 512], F32) for i in range(2)]
    mcol = q.sbuf("mcol", [128, 1], F32)
    zcol = q.sbuf("zcol", [128, 1], F32)
    srt = q.sbuf("srt", [128, 8], F32)
    prb = q.sbuf("prb", [128, 8], F32)
    msk = q.sbuf("msk", [128, 8], F32)
    wdt = q.sbuf("wdt", [128, 8], BF16)
    tmpf = q.sbuf("tmpE", [128, 512], F32)
    l3s = [q.sbuf(f"l3s{i}", [128, 512], F32) for i in range(2)]
    stg = [q.sbuf(f"stgE{i}", [128, 512], F32) for i in range(2)]
    pp = (q.psum("pse0", [128, 512]), q.psum("pse1", [128, 512]))
    ps_y = [q.psum(f"psy{i}", [128, 512]) for i in range(6)]
    cn = alloc_ln_common(q, TL, ps_pool=[ps_y[0], ps_y[1], ps_y[2], ps_y[3]])
    ps_tr = ps_y[4][:].bitcast(BF16)[:, 0:128]
    dbgt = q.sbuf("dbgt", [128, 1024], BF16) if dbg == 3 else None

    for t in range(NDT):
        q.dma("sync", l3b[t][:], ln3[t * 128:(t + 1) * 128, :])
    q.dma("sync", gamс[:], gam[:])
    q.dma("sync", betc[:], bet[:])
    q.dma("gpsimd", rwt[:], rw[:])
    q.dma("gpsimd", idb[:], idf[:])
    load_ln_consts(q, cn, onesc, onesr)
    q.wait_loads("vector", "scalar", "tensor")

    srcs = l3b
    for ch in range(4):
        sl = slice(ch * 512, (ch + 1) * 512)
        for t in range(NDT):
            sq = cn["sqs"][t % 2]
            if ch * NDT + t >= 2:
                q.wait("scalar", "mm", q.cnt["mm"] - 1)
            q.op("scalar", lambda e, a=sq, b=srcs[t], s=sl: e.activation(out=a[:], in_=b[:, s], func=AF.Square), inc=("ac", 1))
            q.op("tensor", lambda e, r=srcs[t], s=sl, st=(t == 0), sp=(t == NDT - 1), ps=cn["ps_st"][0]:
                 e.matmul(ps[:], cn["onesc_b"][:], r[:, s], start=st, stop=sp))
            q.wait("tensor", "ac", q.cnt["ac"])
            q.op("tensor", lambda e, r=sq, st=(t == 0), sp=(t == NDT - 1), ps=cn["ps_st"][1]:
                 e.matmul(ps[:], cn["onesc_b"][:], r[:], start=st, stop=sp), inc=("mm", 1))
        q.barrier(engs_from=("tensor",), engs_to=("vector",))
        q.op("vector", lambda e, s=sl: e.tensor_copy(out=cn["ssum"][:, s], in_=cn["ps_st"][0][:]))
        q.op("vector", lambda e, s=sl: e.tensor_copy(out=cn["sssq"][:, s], in_=cn["ps_st"][1][:]), inc=("ev", 1))
        q.barrier(engs_from=("vector",), engs_to=("tensor",))
    emit_ln_finalize(q, cn)
    for ch in range(4):
        sl = slice(ch * 512, (ch + 1) * 512)
        q.op("tensor", lambda e, s=sl: e.matmul(cn["ps_bc"][0][:], cn["onesr_f"][:], cn["rB"][:, s], start=True, stop=True))
        q.op("tensor", lambda e, s=sl: e.matmul(cn["ps_bc"][1][:], cn["onesr_f"][:], cn["rC"][:, s], start=True, stop=True), inc=("mm", 1))
        q.barrier(engs_from=("tensor",), engs_to=("vector",))
        q.op("vector", lambda e, s=sl: e.tensor_copy(out=cn["bcs"][0][:, s], in_=cn["ps_bc"][0][:]))
        q.op("vector", lambda e, s=sl: e.tensor_copy(out=cn["bcs"][1][:, s], in_=cn["ps_bc"][1][:]), inc=("ev", 1))
        q.barrier(engs_from=("vector",), engs_to=("tensor",))
    for t in range(NDT):
        for ch in range(4):
            sl = slice(ch * 512, (ch + 1) * 512)
            q.op("vector", lambda e, a=srcs[t], s=sl: e.tensor_mul(tmpf[:], a[:, s], cn["bcs"][0][:, s]))
            q.op("vector", lambda e, s=sl: e.tensor_add(tmpf[:], tmpf[:], cn["bcs"][1][:, s]))
            q.op("vector", lambda e, t=t, s=sl: e.tensor_scalar(
                xnb[t][:, s], tmpf[:], gamс[:, t:t + 1], betc[:, t:t + 1], op0=ALU.mult, op1=ALU.add), inc=("ev", 1))
    q.barrier()

    for ch in range(4):
        sl = slice(ch * 512, (ch + 1) * 512)
        ps = pp[ch % 2]
        if ch >= 2:
            q.wait("tensor", "ev", q.cnt["ev"] - 1)
        for kin in range(NDT):
            inc = ("mm", 1) if kin == NDT - 1 else None
            q.op("tensor", lambda e, kin=kin, s=sl, ps=ps, st=(kin == 0), sp=(kin == NDT - 1):
                 e.matmul(ps[0:8, :], rwt[:, kin * 8:(kin + 1) * 8], xnb[kin][:, s], start=st, stop=sp), inc=inc)
        q.wait("vector", "mm", q.cnt["mm"])
        q.op("vector", lambda e, s=sl, ps=ps: e.tensor_copy(out=logr[:, s], in_=ps[0:8, :]), inc=("ev", 1))
    q.barrier()
    if dbg == 2:
        q.dma("gpsimd", ys_o[8:16, :], logr[:, :], store=True)
        q.op("vector", lambda e: e.memset(stg[0][:], 0.0), inc=("ev", 1))
        q.wait("gpsimd", "ev", q.cnt["ev"])
        for ot in range(NDT):
            for ch in range(4):
                q.dma("gpsimd", ys_o[ot * 128:(ot + 1) * 128, ch * 512:(ch + 1) * 512], stg[0][:], store=True)
        q.wait("gpsimd", "stg", q.cnt["stg"])
        q._flush_waits("gpsimd")
        return q.finish()
    for i in range(16):
        q.op("tensor", lambda e, i=i: e.transpose(ps_tr[:, 0:8], logr[:, i * 128:(i + 1) * 128], idb[0:8, 0:8]), inc=("mm", 1))
        q.wait("vector", "mm", q.cnt["mm"])
        q.op("vector", lambda e, i=i: e.tensor_copy(out=ltok[i][:], in_=ps_tr[0:128, 0:EXP]), inc=("ev", 1))
        q.wait("tensor", "ev", q.cnt["ev"])
    q.barrier()
    def vsync():
        q.op("vector", lambda e: e.engine_nop(), inc=("ev", 1)) if False else None
    for i in range(16):
        q.op("vector", lambda e, i=i: e.reduce_max(mcol[:], ltok[i][:], axis=AX.X), inc=("ev", 1))
        q.wait("vector", "ev", q.cnt["ev"])
        q.op("vector", lambda e: e.tensor_scalar(mcol[:], mcol[:], -1.0 / 1.5, None, op0=ALU.mult), inc=("ev", 1))
        q.wait("scalar", "ev", q.cnt["ev"])
        q.op("scalar", lambda e, i=i: e.activation(out=prb[:], in_=ltok[i][:], func=AF.Exp,
                                                   scale=1.0 / 1.5, bias=mcol[:], accum_out=zcol[:]), inc=("ac", 1))
        q.wait("vector", "ac", q.cnt["ac"])
        q.op("vector", lambda e: e.reciprocal(out=zcol[:], in_=zcol[:]), inc=("ev", 1))
        q.wait("vector", "ev", q.cnt["ev"])
        q.op("vector", lambda e: e.tensor_scalar(prb[:], prb[:], zcol[:], None, op0=ALU.mult), inc=("ev", 1))
        q.wait("vector", "ev", q.cnt["ev"])
        q.op("vector", lambda e: e.max(out=srt[:], in_=prb[:]), inc=("ev", 1))
        q.wait("vector", "ev", q.cnt["ev"])
        q.op("vector", lambda e: e.tensor_scalar(msk[:], prb[:], srt[:, 1:2], None, op0=ALU.is_ge), inc=("ev", 1))
        q.wait("vector", "ev", q.cnt["ev"])
        q.op("vector", lambda e: e.tensor_mul(wdt[:], prb[:], msk[:]), inc=("ev", 1))
        if dbg == 3:
            q.op("vector", lambda e, i=i: e.tensor_copy(out=dbgt[:, i * 64:i * 64 + 8], in_=prb[:]))
            q.op("vector", lambda e, i=i: e.tensor_copy(out=dbgt[:, i * 64 + 8:i * 64 + 16], in_=srt[:]))
            q.op("vector", lambda e, i=i: e.tensor_copy(out=dbgt[:, i * 64 + 16:i * 64 + 24], in_=msk[:]))
            q.op("vector", lambda e, i=i: e.tensor_copy(out=dbgt[:, i * 64 + 24:i * 64 + 32], in_=ltok[i][:]))
            q.op("vector", lambda e, i=i: e.tensor_copy(out=dbgt[:, i * 64 + 32:i * 64 + 33], in_=mcol[:]))
            q.op("vector", lambda e, i=i: e.tensor_copy(out=dbgt[:, i * 64 + 33:i * 64 + 34], in_=zcol[:]), inc=("ev", 1))
        q.barrier(engs_from=("vector",), engs_to=("tensor",))
        q.op("tensor", lambda e, i=i: e.transpose(ps_tr[0:8, :], wdt[:], idb[:]), inc=("mm", 1))
        q.wait("vector", "mm", q.cnt["mm"])
        q.op("vector", lambda e, i=i: e.tensor_copy(out=wdr[:, i * 128:(i + 1) * 128], in_=ps_tr[0:8, 0:128]), inc=("ev", 1))
        q.wait("tensor", "ev", q.cnt["ev"])
    q.barrier()
    q.dma("gpsimd", wd_d[:, :], wdr[:, :], store=True)
    q.wait("sync", "stg", q.cnt["stg"])
    if dbg == 3:
        q.barrier()
        q.dma("gpsimd", dbg_o[0:128, 0:1024], dbgt[:], store=True)
        q.dma("gpsimd", ys_o[0:8, :], wdr[:, :], store=True)
        q.wait("gpsimd", "stg", q.cnt["stg"])
        q._flush_waits("gpsimd")
        return q.finish()
    if dbg == 1:
        for t in range(NDT):
            q.dma("gpsimd", dbg_o[t * 128:(t + 1) * 128, :], xnb[t][:], store=True)
        q.dma("gpsimd", ys_o[0:8, :], wdr[:, :], store=True)
        q.dma("gpsimd", ys_o[8:16, :], logr[:, :], store=True)
        q.wait("gpsimd", "stg", q.cnt["stg"])
        q._flush_waits("gpsimd")
        return q.finish()

    for blk in range(NBLK):
        bsl = slice(blk * BLK, (blk + 1) * BLK)
        for e_i in range(EXP):
            for c2 in range(NC2):
                ps = pp[c2]
                wr = wrow[(blk * EXP + e_i) % 2]
                q.dma("sync", wr[:], wd_d[e_i:e_i + 1, blk * BLK + c2 * 512: blk * BLK + (c2 + 1) * 512])
                q.wait("tensor", "lds", q.cnt["lds"])
                q.op("tensor", lambda e, ps=ps, wr=wr: e.matmul(
                    ps[:], cn["onesr_f"][:], wr[:], start=True, stop=True), inc=("mm", 1))
                q.wait("vector", "mm", q.cnt["mm"])
                q.op("vector", lambda e, c2=c2, ps=ps: e.tensor_copy(out=wdBs[c2][:], in_=ps[:]), inc=("ev", 1))
            q.barrier(engs_from=("vector",), engs_to=("tensor", "sync"))
            for j in range(NJ_E):
                wb = w1t[j % 2]
                if j >= 2:
                    q.wait("gpsimd", "mm", q.cnt["mm"] - NC2)
                q.dma("gpsimd", wb[:], w1r[(e_i * NJ_E + j) * 128:(e_i * NJ_E + j + 1) * 128, :])
                ldg_j = q.cnt["ldg"]
                for c2 in range(NC2):
                    ps = pp[(j * NC2 + c2) % 2]
                    q.wait("tensor", "ldg", ldg_j)
                    if j * NC2 + c2 >= 2:
                        q.wait("tensor", "ac", q.cnt["ac"] - 1)
                    for kin in range(NDT):
                        inc = ("mm", 1) if kin == NDT - 1 else None
                        q.op("tensor", lambda e, kin=kin, blk=blk, c2=c2, ps=ps, wb=wb, st=(kin == 0), sp=(kin == NDT - 1):
                             e.matmul(ps[:], wb[:, kin * 128:(kin + 1) * 128],
                                      xnb[kin][:, blk * BLK + c2 * 512: blk * BLK + (c2 + 1) * 512],
                                      start=st, stop=sp), inc=inc)
                    q.wait("scalar", "mm", q.cnt["mm"])
                    q.op("scalar", lambda e, j=j, c2=c2, ps=ps: e.activation(
                        out=hb[j][:, c2 * 512:(c2 + 1) * 512], in_=ps[:], func=AF.Silu), inc=("ac", 1))
            q.barrier(engs_from=("scalar",), engs_to=("tensor",))
            for c2 in range(NC2):
                for jj in range(NJ_E):
                    wb = w2t[jj % 2]
                    if not (c2 == 1):
                        pass
                    if jj >= 2 or c2 == 1 or e_i > 0 or blk > 0:
                        q.wait("gpsimd", "mm", q.cnt["mm"] - 1)
                    q.dma("gpsimd", wb[:], w2r[(e_i * NJ_E + jj) * 128:(e_i * NJ_E + jj + 1) * 128, :])
                    ldg_j = q.cnt["ldg"]
                    q.wait("tensor", "ldg", ldg_j)
                    for ot in range(NDT):
                        inc = ("mm", 1) if ot == NDT - 1 else None
                        q.op("tensor", lambda e, jj=jj, ot=ot, c2=c2, wb=wb, st=(jj == 0), sp=(jj == NJ_E - 1):
                             e.matmul(ps_y[ot % 6][:] if False else ps_y[ot - 2][:] if ot >= 2 else pp[ot][:],
                                      wb[:, ot * 128:(ot + 1) * 128], hb[jj][:, c2 * 512:(c2 + 1) * 512],
                                      start=st, stop=sp), inc=inc)
                q.wait("vector", "mm", q.cnt["mm"])
                for ot in range(NDT):
                    psx = ps_y[ot - 2] if ot >= 2 else pp[ot]
                    if e_i == 0:
                        q.op("vector", lambda e, ot=ot, c2=c2, psx=psx: e.tensor_mul(
                            acc[ot][:, c2 * 512:(c2 + 1) * 512], psx[:], wdBs[c2][:]), inc=("ev", 1))
                    else:
                        q.op("vector", lambda e, ot=ot, c2=c2, psx=psx: e.tensor_mul(tmpf[:], psx[:], wdBs[c2][:]))
                        q.op("vector", lambda e, ot=ot, c2=c2: e.tensor_add(
                            acc[ot][:, c2 * 512:(c2 + 1) * 512], acc[ot][:, c2 * 512:(c2 + 1) * 512], tmpf[:]), inc=("ev", 1))
                q.barrier(engs_from=("vector",), engs_to=("tensor", "gpsimd"))
        stg0 = q.cnt["stg"]
        for ot in range(NDT):
            for c2 in range(NC2):
                gi = ot * NC2 + c2
                ii = gi % 2
                if gi >= 2:
                    q.wait("sync", "ev", q.cnt["ev"] - 1)
                q.dma("sync", l3s[ii][:], ln3f[ot * 128:(ot + 1) * 128, blk * BLK + c2 * 512: blk * BLK + (c2 + 1) * 512])
                q.wait("vector", "lds", q.cnt["lds"])
                if gi >= 2:
                    q.wait("vector", "stg", stg0 + 16 * (gi - 1))
                q.op("vector", lambda e, ot=ot, c2=c2, ii=ii, blk=blk: e.tensor_add(
                    stg[ii][:], acc[ot][:, c2 * 512:(c2 + 1) * 512], l3s[ii][:]), inc=("ev", 1))
                q.wait("gpsimd", "ev", q.cnt["ev"])
                q.dma("gpsimd", ys_o[ot * 128:(ot + 1) * 128, blk * BLK + c2 * 512: blk * BLK + (c2 + 1) * 512],
                      stg[ii][:], store=True)
        q.barrier()
    q.wait("gpsimd", "stg", q.cnt["stg"])
    q._flush_waits("gpsimd")
    return q.finish()


def stage_l1ffn(ln3b_cm, ln3f_cm, x3_cm, moe):
    W1 = np.asarray(moe["W1"], np.float32)
    w1r = np.empty((EXP * NJ_E * 128, D), np.float32)
    for e_i in range(EXP):
        for j in range(NJ_E):
            w1r[(e_i * NJ_E + j) * 128:(e_i * NJ_E + j + 1) * 128] = (
                W1[e_i][:, j * 128:(j + 1) * 128].reshape(NDT, 128, 128).transpose(1, 0, 2).reshape(128, D))
    w2r = np.asarray(moe["W2"], np.float32).reshape(EXP * HME, D)
    gam = np.asarray(moe["gamma"], np.float32).reshape(NDT, 128).T.copy()
    bet = np.asarray(moe["beta"], np.float32).reshape(NDT, 128).T.copy()
    rwT = np.ascontiguousarray(np.asarray(moe["router_w"], np.float32).T)
    rwc = rwT.reshape(NDT, 128, EXP).transpose(1, 0, 2).reshape(128, NDT * EXP)
    nc = _prog("E", build_E)
    maps = []
    for c in range(NCORE):
        maps.append({
            "ln3": ln3b_cm[c], "ln3f": ln3f_cm[c], "gam": gam, "bet": bet,
            "rw": rwc, "w1r": w1r, "w2r": w2r, "idf": np.eye(128, dtype=np.float32),
            "onesc": _ONESC, "onesr": _ONESR,
        })
    r = _run(nc, maps).results
    ys = [r[c]["ys"] for c in range(NCORE)]
    return stage_sandwich(ys, x3_cm)


# version 07
# speedup vs baseline: 1.1428x; 1.1428x over previous
import numpy as np
import ml_dtypes
import concourse.bass as bass
import concourse.mybir as mybir
from concourse.bass_utils import run_bass_kernel_spmd

F32 = mybir.dt.float32
BF16 = mybir.dt.bfloat16
AF = mybir.ActivationFunctionType
ALU = mybir.AluOpType
AX = mybir.AxisListType
BF = ml_dtypes.bfloat16

NCORE = 8
B = 4
S = 4096
D = 1024
SL = S // NCORE
TL = B * SL
NDT = D // 128

ENGS = ("sync", "gpsimd", "vector", "scalar", "tensor")


def _nc():
    return bass.Bass("TRN2", target_bir_lowering=False, debug=False)


class Q:

    def __init__(self):
        self.nc = _nc()
        self.cells = {e: [] for e in ENGS}
        self.sems = {}
        self.cnt = {}
        self.pending = {e: {} for e in ENGS}
        self.covered = {e: None for e in ENGS}
        for name in ("lds", "ldg", "sts", "stg", "mm", "ev", "ac",
                     "Dsync", "Dgpsimd", "Dvector", "Dscalar", "Dtensor"):
            self.sems[name] = self.nc.alloc_semaphore(name)
            self.cnt[name] = 0

    def sbuf(self, name, shape, dtype):
        return self.nc.alloc_sbuf_tensor(name, shape, dtype)

    def psum(self, name, shape, dtype=F32):
        return self.nc.alloc_psum_tensor(name, shape, dtype)

    def _flush_waits(self, eng):
        pend = self.pending[eng]
        if pend:
            items = [(self.sems[s], v) for s, v in pend.items()]
            self.cells[eng].append([lambda e, it=tuple(items): [e.wait_ge(h, v) for h, v in it], None, None, "wait"])
            pend.clear()

    def op(self, eng, fn, inc=None):
        self._flush_waits(eng)
        cell = [fn, None, None, "op"]
        if inc is not None:
            sem, k = inc
            self.cnt[sem] += k
            cell[1] = (sem, k)
            cell[2] = self.cnt[sem]
            self.covered[eng] = (sem, self.cnt[sem])
        else:
            self.covered[eng] = None
        self.cells[eng].append(cell)
        return cell[2]

    def wait(self, eng, sem, val):
        if val > 0:
            self.pending[eng][sem] = max(self.pending[eng].get(sem, 0), val)

    def dma(self, eng, out, in_, store=False):
        sem = ("st" if store else "ld") + ("g" if eng == "gpsimd" else "s")
        return self.op(eng, lambda e, o=out, i=in_: e.dma_start(out=o, in_=i), inc=(sem, 16))

    def wait_loads(self, *engs):
        for e in engs:
            self.wait(e, "lds", self.cnt["lds"])
            self.wait(e, "ldg", self.cnt["ldg"])

    def _cover(self, eng):
        cells = self.cells[eng]
        last = None
        for c in reversed(cells):
            if c[3] == "op":
                last = c
                break
        if last is None:
            return None
        if last[1] is None:
            ds = "D" + eng
            self.cnt[ds] += 1
            last[1] = (ds, 1)
            last[2] = self.cnt[ds]
            self.covered[eng] = (ds, self.cnt[ds])
        if self.covered[eng] is None:
            self.covered[eng] = (last[1][0], last[2])
        return self.covered[eng]

    def barrier(self, engs_from=None, engs_to=None):
        engs_from = engs_from or ENGS
        engs_to = engs_to or ENGS
        covers = []
        for e in engs_from:
            c = self._cover(e)
            if c is not None:
                covers.append((e, c))
        for et in engs_to:
            for e, (sem, val) in covers:
                if e == et:
                    continue
                self.wait(et, sem, val)

    def finish(self):
        for e in ENGS:
            self._flush_waits(e)
        nc = self.nc
        with nc.Block() as block:
            for e in ENGS:
                cells = self.cells[e]
                if not cells:
                    continue

                def body(eng, cells=cells):
                    for fn, inc, _, _k in cells:
                        ins = fn(eng)
                        if inc is not None:
                            ins.then_inc(self.sems[inc[0]], inc[1])

                getattr(block, e)(body)
        return nc



def emit_ln_finalize(q, cn):
    q.op("vector", lambda e: e.tensor_scalar(cn["rA"][:], cn["rA"][:], -1.0 / D, None, op0=ALU.mult))
    q.op("vector", lambda e: e.tensor_scalar(cn["rB"][:], cn["rB"][:], 1.0 / D, None, op0=ALU.mult), inc=("ev", 1))
    q.wait("vector", "ev", q.cnt["ev"])
    q.op("vector", lambda e: e.tensor_mul(cn["rC"][:], cn["rA"][:], cn["rA"][:]), inc=("ev", 1))
    q.wait("vector", "ev", q.cnt["ev"])
    q.op("vector", lambda e: e.tensor_sub(cn["rB"][:], cn["rB"][:], cn["rC"][:]), inc=("ev", 1))
    q.wait("vector", "ev", q.cnt["ev"])
    q.op("vector", lambda e: e.tensor_scalar(cn["rB"][:], cn["rB"][:], 1e-5, None, op0=ALU.add))
    q.barrier(engs_from=("vector",), engs_to=("scalar",))
    q.op("scalar", lambda e: e.activation(out=cn["rB"][:], in_=cn["rB"][:], func=AF.Sqrt))
    q.barrier(engs_from=("scalar",), engs_to=("vector",))
    q.op("vector", lambda e: e.reciprocal(out=cn["rB"][:], in_=cn["rB"][:]), inc=("ev", 1))
    q.wait("vector", "ev", q.cnt["ev"])
    q.op("vector", lambda e: e.tensor_mul(cn["rC"][:], cn["rA"][:], cn["rB"][:]))
    q.barrier(engs_from=("vector",), engs_to=("tensor",))


def emit_ln_cm(q, xt, xb, cn, T, mutate_ok=True):
    nch = T // 512
    for t in range(NDT):
        q.op("vector", lambda e, a=xb[t], b=xt[t]: e.tensor_copy(out=a[:], in_=b[:]))
    q.barrier(engs_from=("vector",), engs_to=("scalar", "tensor"))
    for ch in range(nch):
        sl = slice(ch * 512, (ch + 1) * 512)
        for t in range(NDT):
            sq = cn["sqs"][t % 2]
            if ch * NDT + t >= 2:
                q.wait("scalar", "mm", q.cnt["mm"] - 1)
            q.op("scalar", lambda e, a=sq, b=xb[t], s=sl: e.activation(out=a[:], in_=b[:, s], func=AF.Square), inc=("ac", 1))
            q.op("tensor", lambda e, r=xb[t], s=sl, st=(t == 0), sp=(t == NDT - 1), ps=cn["ps_st"][0]:
                 e.matmul(ps[:], cn["onesc_b"][:], r[:, s], start=st, stop=sp))
            q.wait("tensor", "ac", q.cnt["ac"])
            q.op("tensor", lambda e, r=sq, st=(t == 0), sp=(t == NDT - 1), ps=cn["ps_st"][1]:
                 e.matmul(ps[:], cn["onesc_b"][:], r[:], start=st, stop=sp), inc=("mm", 1))
        q.barrier(engs_from=("tensor",), engs_to=("vector",))
        q.op("vector", lambda e, s=sl: e.tensor_copy(out=cn["ssum"][:, s], in_=cn["ps_st"][0][:]))
        q.op("vector", lambda e, s=sl: e.tensor_copy(out=cn["sssq"][:, s], in_=cn["ps_st"][1][:]), inc=("ev", 1))
        q.barrier(engs_from=("vector",), engs_to=("tensor",))
    emit_ln_finalize(q, cn)
    for ch in range(nch):
        sl = slice(ch * 512, (ch + 1) * 512)
        q.op("tensor", lambda e, s=sl: e.matmul(cn["ps_bc"][0][:], cn["onesr_f"][:], cn["rstd"][:, s], start=True, stop=True))
        q.op("tensor", lambda e, s=sl: e.matmul(cn["ps_bc"][1][:], cn["onesr_f"][:], cn["s2"][:, s], start=True, stop=True), inc=("mm", 1))
        q.barrier(engs_from=("tensor",), engs_to=("vector",))
        q.op("vector", lambda e, s=sl: e.tensor_copy(out=cn["bcs"][0][:, s], in_=cn["ps_bc"][0][:]))
        q.op("vector", lambda e, s=sl: e.tensor_copy(out=cn["bcs"][1][:, s], in_=cn["ps_bc"][1][:]), inc=("ev", 1))
        q.barrier(engs_from=("vector",), engs_to=("tensor",))
    for t in range(NDT):
        q.op("vector", lambda e, a=xt[t]: e.tensor_mul(a[:], a[:], cn["bcs"][0][:, 0:T]))
        q.op("vector", lambda e, a=xt[t]: e.tensor_add(a[:], a[:], cn["bcs"][1][:, 0:T]))
        q.op("vector", lambda e, a=xt[t], o=xb[t]: e.tensor_copy(out=o[:], in_=a[:]))


def alloc_ln_common(q, T, ps_pool=None):
    cn = {}
    cn["sqs"] = [q.sbuf(f"lnsq{i}", [128, 512], BF16) for i in range(2)]
    for n in ("rA", "rB", "rC"):
        cn[n] = q.sbuf("ln_" + n, [1, T], F32)
    cn["ssum"] = cn["mun"] = cn["rA"]
    cn["sssq"] = cn["m2"] = cn["var"] = cn["rstd"] = cn["rB"]
    cn["s2"] = cn["rC"]
    cn["bcs"] = (q.sbuf("ln_bc0", [128, T], F32), q.sbuf("ln_bc1", [128, T], F32))
    cn["onesc_b"] = q.sbuf("onescb", [128, 1], BF16)
    cn["onesr_f"] = q.sbuf("onesrf", [1, 128], F32)
    if ps_pool is None:
        cn["ps_st"] = [q.psum(f"ps_st{i}", [1, 512]) for i in range(2)]
        cn["ps_bc"] = [q.psum(f"ps_bc{i}", [128, 512]) for i in range(2)]
    else:
        cn["ps_st"] = [ps_pool[0][0:1, :], ps_pool[1][0:1, :]]
        cn["ps_bc"] = [ps_pool[2][:], ps_pool[3][:]]
    return cn


def load_ln_consts(q, cn, onesc_ap, onesr_ap):
    q.dma("gpsimd", cn["onesc_b"][:], onesc_ap[:])
    q.dma("gpsimd", cn["onesr_f"][:], onesr_ap[:])



def emit_proj(q, wtiles, rhs_tiles, pp, evac, n_out_tiles=NDT, T=TL, kin_tiles=None):
    kt = kin_tiles if kin_tiles is not None else len(rhs_tiles)
    nch = T // 512
    hist = []
    q.barrier(engs_from=("vector", "scalar"), engs_to=("tensor",))
    for ot in range(n_out_tiles):
        for ch in range(nch):
            gi = ot * nch + ch
            sl = slice(ch * 512, (ch + 1) * 512)
            ps = pp[gi % 2]
            if gi >= 2:
                for sem, val in hist[gi - 2]:
                    q.wait("tensor", sem, val)
            for kin in range(kt):
                inc = ("mm", 1) if kin == kt - 1 else None
                q.op("tensor", lambda e, w=wtiles[kin], o=ot, s=sl, r=rhs_tiles[kin], ps=ps,
                     st=(kin == 0), sp=(kin == kt - 1):
                     e.matmul(ps[:], w[:, o * 128:(o + 1) * 128], r[:, s], start=st, stop=sp), inc=inc)
            covers = evac(ot, sl, ps, q.cnt["mm"])
            hist.append(covers)



def build_A1():
    q = Q()
    nc = q.nc
    x = nc.dram_tensor("x", [D, TL], F32, kind="ExternalInput").ap()
    wk = nc.dram_tensor("wk", [D, D], F32, kind="ExternalInput").ap()
    wv = nc.dram_tensor("wv", [D, D], F32, kind="ExternalInput").ap()
    wr = nc.dram_tensor("wr", [D, D], F32, kind="ExternalInput").ap()
    onesc = nc.dram_tensor("onesc", [128, 1], F32, kind="ExternalInput").ap()
    onesr = nc.dram_tensor("onesr", [1, 128], F32, kind="ExternalInput").ap()
    kv_o = nc.dram_tensor("kv", [D, TL], F32, kind="ExternalOutput").ap()
    r_o = nc.dram_tensor("r", [D, TL], BF16, kind="ExternalOutput").ap()

    xt = [q.sbuf(f"xt{t}", [128, TL], F32) for t in range(NDT)]
    xb = [q.sbuf(f"xb{t}", [128, TL], BF16) for t in range(NDT)]
    rt = [q.sbuf(f"rt{t}", [128, TL], BF16) for t in range(NDT)]
    wA = [q.sbuf(f"wA{t}", [128, D], BF16) for t in range(NDT)]
    wB = [q.sbuf(f"wB{t}", [128, D], BF16) for t in range(NDT)]
    kvt = xt
    cn = alloc_ln_common(q, TL)
    pp = (q.psum("ps_p0", [128, 512]), q.psum("ps_p1", [128, 512]))

    for t in range(NDT):
        q.dma("sync", xt[t][:], x[t * 128:(t + 1) * 128, :])
    for t in range(NDT):
        q.dma("gpsimd", wA[t][:], wk[t * 128:(t + 1) * 128, :])
    for t in range(NDT):
        q.dma("gpsimd", wB[t][:], wv[t * 128:(t + 1) * 128, :])
    load_ln_consts(q, cn, onesc, onesr)
    q.wait_loads("vector", "scalar", "tensor")

    emit_ln_cm(q, xt, xb, cn, TL)
    q.barrier(engs_from=("vector",), engs_to=("tensor", "scalar"))

    def evac_k(ot, sl, ps, mmv):
        q.wait("vector", "mm", mmv)
        q.op("vector", lambda e, o=ot, s=sl, ps=ps: e.tensor_copy(out=kvt[o][:, s], in_=ps[:]), inc=("ev", 1))
        return [("ev", q.cnt["ev"])]

    emit_proj(q, wA, xb, pp, evac_k)
    mm_after_k = q.cnt["mm"]
    q.wait("gpsimd", "mm", mm_after_k)
    for t in range(NDT):
        q.dma("gpsimd", wA[t][:], wr[t * 128:(t + 1) * 128, :])
    ld_r = q.cnt["ldg"]

    def evac_v(ot, sl, ps, mmv):
        q.wait("vector", "mm", mmv)
        q.op("vector", lambda e, o=ot, s=sl, ps=ps: e.tensor_mul(kvt[o][:, s], kvt[o][:, s], ps[:]), inc=("ev", 1))
        return [("ev", q.cnt["ev"])]

    emit_proj(q, wB, xb, pp, evac_v)
    q.wait("tensor", "ldg", ld_r)

    def evac_r(ot, sl, ps, mmv):
        q.wait("scalar", "mm", mmv)
        q.op("scalar", lambda e, o=ot, s=sl, ps=ps: e.activation(out=rt[o][:, s], in_=ps[:], func=AF.Sigmoid), inc=("ac", 1))
        return [("ac", q.cnt["ac"])]

    emit_proj(q, wA, xb, pp, evac_r)
    q.barrier(engs_from=("vector", "scalar"), engs_to=("gpsimd", "sync"))
    for t in range(NDT):
        q.dma("gpsimd", kv_o[t * 128:(t + 1) * 128, :], kvt[t][:], store=True)
        q.dma("sync", r_o[t * 128:(t + 1) * 128, :], rt[t][:], store=True)
    q.wait("gpsimd", "stg", q.cnt["stg"])
    q.wait("gpsimd", "sts", q.cnt["sts"])
    q._flush_waits("gpsimd")
    return q.finish()



def build_A2():
    q = Q()
    nc = q.nc
    kv = nc.dram_tensor("kv", [128, B * S], F32, kind="ExternalInput").ap()
    r = nc.dram_tensor("r", [128, B * S], BF16, kind="ExternalInput").ap()
    td = nc.dram_tensor("td", [128, 1], F32, kind="ExternalInput").ap()
    tf = nc.dram_tensor("tf", [128, 1], F32, kind="ExternalInput").ap()
    out = nc.dram_tensor("out", [128, B * S], BF16, kind="ExternalOutput").ap()

    kvb = [q.sbuf(f"kv{i}", [128, S], F32) for i in range(2)]
    rb = [q.sbuf(f"r{i}", [128, S], BF16) for i in range(2)]
    sb = q.sbuf("s", [128, S], F32)
    eb = q.sbuf("e", [128, S], F32)
    ob = [q.sbuf(f"o{i}", [128, S], BF16) for i in range(2)]
    wcol = q.sbuf("wcol", [128, 1], F32)
    dcol = q.sbuf("dcol", [128, 1], F32)
    ucol = q.sbuf("ucol", [128, 1], F32)
    tdc = q.sbuf("tdc", [128, 1], F32)
    tfc = q.sbuf("tfc", [128, 1], F32)

    q.dma("sync", tdc[:], td[:])
    q.dma("sync", tfc[:], tf[:])
    q.wait("scalar", "lds", 32)
    q.op("scalar", lambda e: e.activation(out=wcol[:], in_=tdc[:], func=AF.Exp))
    q.barrier(engs_from=("scalar",), engs_to=("vector",))
    q.op("vector", lambda e: e.tensor_scalar(wcol[:], wcol[:], -1.0, None, op0=ALU.mult))
    q.barrier(engs_from=("vector",), engs_to=("scalar",))
    q.op("scalar", lambda e: e.activation(out=dcol[:], in_=wcol[:], func=AF.Exp))
    q.op("scalar", lambda e: e.activation(out=ucol[:], in_=tfc[:], func=AF.Exp))
    q.barrier(engs_from=("scalar",), engs_to=("vector",))

    evdone = []
    for b in range(B):
        i = b % 2
        if b >= 2:
            q.wait("sync", "ev", evdone[b - 2])
        q.dma("sync", kvb[i][:], kv[:, b * S:(b + 1) * S])
        q.dma("sync", rb[i][:], r[:, b * S:(b + 1) * S])
        q.wait("vector", "lds", 32 + 32 * (b + 1))
        if b >= 2:
            q.wait("vector", "stg", 16 * (b - 1))
        q.op("vector", lambda e, k=kvb[i]: e.tensor_tensor_scan(
            out=sb[:], data0=dcol[:].to_broadcast([128, S]), data1=k[:],
            initial=0.0, op0=ALU.mult, op1=ALU.add))
        q.op("vector", lambda e, k=kvb[i]: e.tensor_scalar(k[:], k[:], ucol[:], None, op0=ALU.mult), inc=("ev", 1))
        q.wait("vector", "ev", q.cnt["ev"])
        q.op("vector", lambda e, k=kvb[i]: e.tensor_copy(out=eb[:, 0:1], in_=k[:, 0:1]))
        q.op("vector", lambda e, k=kvb[i]: e.tensor_add(eb[:, 1:S], sb[:, 0:S - 1], k[:, 1:S]))
        q.op("vector", lambda e, rr=rb[i], o=ob[i]: e.tensor_mul(o[:], rr[:], eb[:]), inc=("ev", 1))
        evdone.append(q.cnt["ev"])
        q.wait("gpsimd", "ev", q.cnt["ev"])
        q.dma("gpsimd", out[:, b * S:(b + 1) * S], ob[i][:], store=True)
    q.wait("gpsimd", "stg", q.cnt["stg"])
    q._flush_waits("gpsimd")
    return q.finish()



def build_A3(dbg=0):
    q = Q()
    nc = q.nc
    rw = nc.dram_tensor("rw", [D, TL], BF16, kind="ExternalInput").ap()
    xr = nc.dram_tensor("xr", [D, TL], F32, kind="ExternalInput").ap()
    wo = nc.dram_tensor("wo", [D, D], F32, kind="ExternalInput").ap()
    onesc = nc.dram_tensor("onesc", [128, 1], F32, kind="ExternalInput").ap()
    onesr = nc.dram_tensor("onesr", [1, 128], F32, kind="ExternalInput").ap()
    x1_o = nc.dram_tensor("x1", [D, TL], F32, kind="ExternalOutput").ap()
    ln1_o = nc.dram_tensor("ln1", [D, TL], BF16, kind="ExternalOutput").ap()

    rwt = [q.sbuf(f"rw{t}", [128, TL], BF16) for t in range(NDT)]
    xt = [q.sbuf(f"xs{t}", [128, TL], F32) for t in range(NDT)]
    ab = [q.sbuf(f"ab{t}", [128, TL], BF16) for t in range(NDT)]
    wsb = [q.sbuf(f"wo{t}", [128, D], BF16) for t in range(NDT)]
    tmp = q.sbuf("tmp", [128, TL], F32)
    cn = alloc_ln_common(q, TL)
    pp = (q.psum("ps_p0", [128, 512]), q.psum("ps_p1", [128, 512]))

    for t in range(NDT):
        q.dma("sync", rwt[t][:], rw[t * 128:(t + 1) * 128, :])
        q.dma("sync", xt[t][:], xr[t * 128:(t + 1) * 128, :])
    for t in range(NDT):
        q.dma("gpsimd", wsb[t][:], wo[t * 128:(t + 1) * 128, :])
    load_ln_consts(q, cn, onesc, onesr)
    q.wait_loads("vector", "scalar", "tensor")

    if dbg == 4:
        q.wait_loads("gpsimd")
        q.barrier()
        for t in range(NDT):
            q.dma("gpsimd", ln1_o[t * 128:(t + 1) * 128, :], rwt[t][:], store=True)
            q.dma("sync", x1_o[t * 128:(t + 1) * 128, :], xt[t][:], store=True)
        q.wait("gpsimd", "stg", q.cnt["stg"])
        q.wait("gpsimd", "sts", q.cnt["sts"])
        q._flush_waits("gpsimd")
        return q.finish()
    if dbg == 5:
        def evac5(ot, sl, ps, mmv):
            q.wait("vector", "mm", mmv)
            q.op("vector", lambda e, o=ot, s=sl, ps=ps: e.tensor_copy(out=ab[o][:, s], in_=ps[:]), inc=("ev", 1))
            return [("ev", q.cnt["ev"])]
        emit_proj(q, wsb, rwt, pp, evac5)
        q.barrier()
        for t in range(NDT):
            q.dma("gpsimd", ln1_o[t * 128:(t + 1) * 128, :], ab[t][:], store=True)
            q.dma("sync", x1_o[t * 128:(t + 1) * 128, :], xt[t][:], store=True)
        q.wait("gpsimd", "stg", q.cnt["stg"])
        q.wait("gpsimd", "sts", q.cnt["sts"])
        q._flush_waits("gpsimd")
        return q.finish()
    evdone = []
    mmstat = []
    for ch in range(4):
        sl = slice(ch * 512, (ch + 1) * 512)
        for ot in range(NDT):
            gi = ch * NDT + ot
            ps = pp[gi % 2]
            if gi >= 2:
                for sem, val in evdone[gi - 2]:
                    q.wait("tensor", sem, val)
            for kin in range(NDT):
                inc = ("mm", 1) if kin == NDT - 1 else None
                q.op("tensor", lambda e, w=wsb[kin], o=ot, s=sl, rr=rwt[kin], ps=ps,
                     st=(kin == 0), sp=(kin == NDT - 1):
                     e.matmul(ps[:], w[:, o * 128:(o + 1) * 128], rr[:, s], start=st, stop=sp), inc=inc)
            mmv = q.cnt["mm"]
            q.wait("vector", "mm", mmv)
            q.wait("scalar", "mm", mmv)
            q.op("vector", lambda e, o=ot, s=sl, ps=ps: e.tensor_copy(out=ab[o][:, s], in_=ps[:]), inc=("ev", 1))
            sq = cn["sqs"][gi % 2]
            if gi >= 2:
                q.wait("scalar", "mm", mmstat[gi - 2])
            q.wait("scalar", "ev", q.cnt["ev"])
            q.op("scalar", lambda e, o=ot, s=sl, a=sq: e.activation(out=a[:], in_=ab[o][:, s], func=AF.Square), inc=("ac", 1))
            evdone.append([("ev", q.cnt["ev"]), ("ac", q.cnt["ac"])])
            if dbg != 3:
                q.wait("tensor", "ev", q.cnt["ev"])
                q.wait("tensor", "ac", q.cnt["ac"])
                q.op("tensor", lambda e, o=ot, s=sl, st=(ot == 0), sp=(ot == NDT - 1), ps=cn["ps_st"][0], a=ab[ot]:
                     e.matmul(ps[:], cn["onesc_b"][:], a[:, s], start=st, stop=sp))
                q.op("tensor", lambda e, st=(ot == 0), sp=(ot == NDT - 1), ps=cn["ps_st"][1], a=sq:
                     e.matmul(ps[:], cn["onesc_b"][:], a[:], start=st, stop=sp), inc=("mm", 1))
            mmstat.append(q.cnt["mm"])
        q.barrier(engs_from=("tensor",), engs_to=("vector",))
        q.op("vector", lambda e, s=sl: e.tensor_copy(out=cn["ssum"][:, s], in_=cn["ps_st"][0][:]))
        q.op("vector", lambda e, s=sl: e.tensor_copy(out=cn["sssq"][:, s], in_=cn["ps_st"][1][:]), inc=("ev", 1))
        q.barrier(engs_from=("vector",), engs_to=("tensor",))
    if dbg in (1, 3):
        q.barrier()
        for t in range(NDT):
            q.dma("gpsimd", ln1_o[t * 128:(t + 1) * 128, :], ab[t][:], store=True)
            q.dma("sync", x1_o[t * 128:(t + 1) * 128, :], xt[t][:], store=True)
        q.wait("gpsimd", "stg", q.cnt["stg"])
        q.wait("gpsimd", "sts", q.cnt["sts"])
        q._flush_waits("gpsimd")
        return q.finish()
    emit_ln_finalize(q, cn)
    for ch in range(4):
        sl = slice(ch * 512, (ch + 1) * 512)
        q.op("tensor", lambda e, s=sl: e.matmul(cn["ps_bc"][0][:], cn["onesr_f"][:], cn["rstd"][:, s], start=True, stop=True))
        q.op("tensor", lambda e, s=sl: e.matmul(cn["ps_bc"][1][:], cn["onesr_f"][:], cn["s2"][:, s], start=True, stop=True), inc=("mm", 1))
        q.barrier(engs_from=("tensor",), engs_to=("vector",))
        q.op("vector", lambda e, s=sl: e.tensor_copy(out=cn["bcs"][0][:, s], in_=cn["ps_bc"][0][:]))
        q.op("vector", lambda e, s=sl: e.tensor_copy(out=cn["bcs"][1][:, s], in_=cn["ps_bc"][1][:]), inc=("ev", 1))
        q.barrier(engs_from=("vector",), engs_to=("tensor",))
    for t in range(NDT):
        q.op("vector", lambda e, x=xt[t]: e.tensor_add(x[:], x[:], cn["bcs"][1][:]))
        q.op("vector", lambda e, a=ab[t]: e.tensor_mul(tmp[:], a[:], cn["bcs"][0][:]))
        q.op("vector", lambda e, x=xt[t]: e.tensor_add(x[:], x[:], tmp[:]), inc=("ev", 1))
        q.wait("sync", "ev", q.cnt["ev"])
        q.dma("sync", x1_o[t * 128:(t + 1) * 128, :], xt[t][:], store=True)
    if dbg == 2:
        q.barrier()
        for t in range(NDT):
            q.dma("gpsimd", ln1_o[t * 128:(t + 1) * 128, :], ab[t][:], store=True)
        q.wait("gpsimd", "stg", q.cnt["stg"])
        q.wait("gpsimd", "sts", q.cnt["sts"])
        q._flush_waits("gpsimd")
        return q.finish()
    q.wait("vector", "sts", q.cnt["sts"])
    emit_ln_cm(q, xt, ab, cn, TL)
    q.barrier(engs_from=("vector",), engs_to=("gpsimd",))
    for t in range(NDT):
        q.dma("gpsimd", ln1_o[t * 128:(t + 1) * 128, :], ab[t][:], store=True)
    q.wait("gpsimd", "stg", q.cnt["stg"])
    q.wait("gpsimd", "sts", q.cnt["sts"])
    q._flush_waits("gpsimd")
    return q.finish()



def _cm(x_bsd, c):
    sl = x_bsd[:, c * SL:(c + 1) * SL, :]
    return np.ascontiguousarray(sl.transpose(2, 0, 1).reshape(D, TL))


_ONESC = np.ones((128, 1), np.float32)
_ONESR = np.ones((1, 128), np.float32)


import os
_TRACE = bool(os.environ.get("KTRACE"))
_TIMES = []


def _run(nc, in_maps):
    r = run_bass_kernel_spmd(nc, in_maps, core_ids=list(range(NCORE)), trace=_TRACE)
    if _TRACE:
        _TIMES.append(r.exec_time_ns)
    return r


def stage_l0attn(x_cm, Wk, Wv, Wr, Wo, td, tf):
    wkT = np.ascontiguousarray(Wk.T)
    wvT = np.ascontiguousarray(Wv.T)
    wrT = np.ascontiguousarray(Wr.T)
    woT = np.ascontiguousarray(Wo.T)

    nc1 = _prog("A1", build_A1)
    maps = [{"x": x_cm[c], "wk": wkT, "wv": wvT, "wr": wrT, "onesc": _ONESC, "onesr": _ONESR}
            for c in range(NCORE)]
    r1 = _run(nc1, maps).results

    kv_full = np.empty((D, B, S), np.float32)
    r_full = np.empty((D, B, S), BF)
    for c in range(NCORE):
        kv_full[:, :, c * SL:(c + 1) * SL] = r1[c]["kv"].reshape(D, B, SL)
        r_full[:, :, c * SL:(c + 1) * SL] = r1[c]["r"].reshape(D, B, SL)
    kv_full = kv_full.reshape(D, B * S)
    r_full = r_full.reshape(D, B * S)
    td8 = td.reshape(NCORE, 128, 1).astype(np.float32)
    tf8 = tf.reshape(NCORE, 128, 1).astype(np.float32)

    nc2 = _prog("A2", build_A2)
    maps = [{"kv": np.ascontiguousarray(kv_full[c * 128:(c + 1) * 128]),
             "r": np.ascontiguousarray(r_full[c * 128:(c + 1) * 128]),
             "td": td8[c], "tf": tf8[c]} for c in range(NCORE)]
    r2 = _run(nc2, maps).results

    rw_full = np.concatenate([r2[c]["out"].reshape(128, B, S) for c in range(NCORE)], axis=0)
    nc3 = _prog("A3", build_A3)
    maps = []
    for c in range(NCORE):
        rw_c = np.ascontiguousarray(rw_full[:, :, c * SL:(c + 1) * SL].reshape(D, TL))
        maps.append({"rw": rw_c, "xr": x_cm[c], "wo": woT, "onesc": _ONESC, "onesr": _ONESR})
    r3 = _run(nc3, maps).results
    x1 = [r3[c]["x1"] for c in range(NCORE)]
    ln1 = [r3[c]["ln1"] for c in range(NCORE)]
    return x1, ln1


def kernel(**inputs):
    inp = {k: np.asarray(v) for k, v in inputs.items()}
    x = inp["L0_attn_x"].astype(np.float32, copy=False)
    x_cm = [_cm(x, c) for c in range(NCORE)]
    x1, ln1 = stage_l0attn(x_cm, inp["L0_attn_Wk"], inp["L0_attn_Wv"], inp["L0_attn_Wr"],
                           inp["L0_attn_Wo"], inp["L0_attn_time_decay"], inp["L0_attn_time_first"])
    ffn = {k[len("L0_ffn_"):]: v for k, v in inp.items() if k.startswith("L0_ffn_")}
    x2, ln2b, ln2f = stage_l0ffn(ln1, x1, ffn)
    att = {k[len("L1_attn_"):]: v for k, v in inp.items() if k.startswith("L1_attn_")}
    l1o = stage_l1attn(ln2b, att)
    x3, ln3b, ln3f = stage_sandwich(l1o, x2)
    moe = {k[len("L1_ffn_"):]: v for k, v in inp.items() if k.startswith("L1_ffn_")}
    x4, _, _ = stage_l1ffn(ln3b, ln3f, x3, moe)
    out = np.empty((B, S, D), np.float32)
    for c in range(NCORE):
        out[:, c * SL:(c + 1) * SL, :] = np.asarray(x4[c]).reshape(D, B, SL).transpose(1, 2, 0)
    return out


INNER = 512
HEADS = 8
DH = 64
HID = 2730
HIDP = 2816
NJ = 44
NHT = HIDP // 128


def _shift_tiles(q, xf, xprev, hs):
    for i in range(4):
        t = 4 + i
        for b in range(B):
            q.op("vector", lambda e, i=i, t=t, b=b: e.tensor_copy(
                out=hs[i][:, b * SL + 1:(b + 1) * SL], in_=xf[t][:, b * SL:(b + 1) * SL - 1]))
            q.op("vector", lambda e, i=i, t=t, b=b: e.tensor_copy(
                out=hs[i][:, b * SL:b * SL + 1], in_=xprev[t][:, b:b + 1]))


def build_B0():
    q = Q()
    nc = q.nc
    xf = nc.dram_tensor("xf", [D, TL], BF16, kind="ExternalInput").ap()
    xprev = nc.dram_tensor("xprev", [D, B], BF16, kind="ExternalInput").ap()
    wqkv = nc.dram_tensor("wqkv", [D, 3 * INNER], F32, kind="ExternalInput").ap()
    csq = nc.dram_tensor("csq", [128, TL], BF16, kind="ExternalInput").ap()
    ssq = nc.dram_tensor("ssq", [128, TL], BF16, kind="ExternalInput").ap()
    csk = nc.dram_tensor("csk", [128, TL], BF16, kind="ExternalInput").ap()
    ssk = nc.dram_tensor("ssk", [128, TL], BF16, kind="ExternalInput").ap()
    qscol = nc.dram_tensor("qscol", [128, 1], F32, kind="ExternalInput").ap()
    kscol = nc.dram_tensor("kscol", [128, 1], F32, kind="ExternalInput").ap()
    selqk = nc.dram_tensor("selqk", [128, 8 * 16], F32, kind="ExternalInput").ap()
    expd = nc.dram_tensor("expd", [16, 8 * 128], F32, kind="ExternalInput").ap()
    p32 = nc.dram_tensor("p32", [128, 128], F32, kind="ExternalInput").ap()
    idf = nc.dram_tensor("idf", [128, 128], F32, kind="ExternalInput").ap()
    qf_o = nc.dram_tensor("qf", [INNER, TL], BF16, kind="ExternalOutput").ap()
    kf_o = nc.dram_tensor("kf", [INNER, TL], BF16, kind="ExternalOutput").ap()
    vt_o = nc.dram_tensor("vt", [TL, INNER], BF16, kind="ExternalOutput").ap()

    xft = [q.sbuf(f"xf{t}", [128, TL], BF16) for t in range(NDT)]
    xpv = [q.sbuf(f"xpv{t}", [128, B], BF16) for t in range(NDT)]
    hs = [q.sbuf(f"hs{i}", [128, TL], BF16) for i in range(4)]
    wt = [q.sbuf(f"w{t}", [128, 3 * INNER], BF16) for t in range(NDT)]
    qr = [q.sbuf(f"qr{t}", [128, TL], BF16) for t in range(4)]
    kr = [q.sbuf(f"kr{t}", [128, TL], BF16) for t in range(4)]
    vr = [q.sbuf(f"vr{t}", [128, TL], BF16) for t in range(4)]
    qf = [q.sbuf(f"qfo{t}", [128, TL], BF16) for t in range(4)]
    kf = [q.sbuf(f"kfo{t}", [128, TL], BF16) for t in range(4)]
    vtok = [q.sbuf(f"vt{t}", [128, INNER], BF16) for t in range(16)]
    tq1 = q.sbuf("tq1", [128, TL], BF16)
    tq2 = q.sbuf("tq2", [128, TL], BF16)
    tk1 = q.sbuf("tk1", [128, TL], BF16)
    tk2 = q.sbuf("tk2", [128, TL], BF16)
    qs8 = q.sbuf("qs8", [128, 1], F32)
    qs8w = q.sbuf("qs8w", [128, 1], F32)
    ksc = q.sbuf("ksc", [128, 1], F32)
    kscw = q.sbuf("kscw", [128, 1], F32)
    selb = q.sbuf("selb", [128, 8 * 16], BF16)
    expb = q.sbuf("expb", [16, 8 * 128], F32)
    p32b = q.sbuf("p32b", [128, 128], BF16)
    idb = q.sbuf("idb", [128, 128], BF16)
    sqs = [q.sbuf(f"sq{i}", [128, 512], BF16) for i in range(2)]
    nrm = q.sbuf("nrm", [16, TL], F32)
    tA = q.sbuf("tA", [128, 512], F32)
    tB = q.sbuf("tB", [128, 512], F32)
    tC = q.sbuf("tC", [128, 512], F32)
    ps_p = [q.psum(f"psp{i}", [128, 512]) for i in range(2)]
    ps_n = q.psum("psn", [16, 512])
    ps_sw = [q.psum(f"pssw{i}", [128, 512]) for i in range(2)]
    ps_bc = [q.psum(f"psbc{i}", [128, 512]) for i in range(2)]
    ps_tr = q.psum("pstr", [128, 128], BF16)

    for t in range(NDT):
        q.dma("sync", xft[t][:], xf[t * 128:(t + 1) * 128, :])
        q.dma("sync", xpv[t][:], xprev[t * 128:(t + 1) * 128, :])
        q.dma("gpsimd", wt[t][:], wqkv[t * 128:(t + 1) * 128, :])
    for dst, src in ((tq1, csq), (tq2, ssq), (tk1, csk), (tk2, ssk)):
        q.dma("sync", dst[:], src[:])
    for dst, src in ((qs8, qscol), (ksc, kscol)):
        q.dma("sync", dst[:], src[:])
    q.dma("gpsimd", selb[:], selqk[:])
    q.dma("sync", expb[:], expd[:])
    q.dma("gpsimd", p32b[:], p32[:])
    q.dma("gpsimd", idb[:], idf[:])
    q.wait_loads("vector", "scalar", "tensor")

    _shift_tiles(q, xft, xpv, hs)
    q.op("vector", lambda e: e.tensor_scalar(qs8[:], qs8[:], 8.0, None, op0=ALU.mult), inc=("ev", 1))
    q.wait("vector", "ev", q.cnt["ev"])
    for base in (0, 64):
        q.op("vector", lambda e, b=base: e.tensor_copy(out=qs8w[b:b + 32, :], in_=qs8[b + 32:b + 64, :]))
        q.op("vector", lambda e, b=base: e.tensor_copy(out=qs8w[b + 32:b + 64, :], in_=qs8[b:b + 32, :]))
        q.op("vector", lambda e, b=base: e.tensor_copy(out=kscw[b:b + 32, :], in_=ksc[b + 32:b + 64, :]))
        inc2 = ("ev", 1) if base == 64 else None
        q.op("vector", lambda e, b=base: e.tensor_copy(out=kscw[b + 32:b + 64, :], in_=ksc[b:b + 32, :]), inc=inc2)
    q.wait("vector", "ev", q.cnt["ev"])
    q.op("vector", lambda e: e.tensor_scalar(tq1[:], tq1[:], qs8[:], None, op0=ALU.mult))
    q.op("vector", lambda e: e.tensor_scalar(tq2[:], tq2[:], qs8w[:], None, op0=ALU.mult))
    q.op("vector", lambda e: e.tensor_scalar(tk1[:], tk1[:], ksc[:], None, op0=ALU.mult))
    q.op("vector", lambda e: e.tensor_scalar(tk2[:], tk2[:], kscw[:], None, op0=ALU.mult))
    q.barrier(engs_from=("vector",), engs_to=("tensor", "scalar"))

    rhs = xft[:4] + hs

    outs = qr + kr + vr

    def evac_qkv(ot, sl, ps, mmv):
        q.wait("vector", "mm", mmv)
        q.op("vector", lambda e, o=outs[ot], s=sl, ps=ps: e.tensor_copy(out=o[:, s], in_=ps[:]), inc=("ev", 1))
        return [("ev", q.cnt["ev"])]

    emit_proj(q, wt, rhs, ps_p, evac_qkv, n_out_tiles=12)
    q.barrier()

    tr = 0
    for vti in range(16):
        for cj in range(4):
            q.op("tensor", lambda e, v=vr[cj], c=vti: e.transpose(ps_tr[:], v[:, c * 128:(c + 1) * 128], idb[:]), inc=("mm", 1))
            mmv = q.cnt["mm"]
            q.wait("vector", "mm", mmv)
            q.op("vector", lambda e, t=vti, c=cj: e.tensor_copy(out=vtok[t][:, c * 128:(c + 1) * 128], in_=ps_tr[:]), inc=("ev", 1))
            q.wait("tensor", "ev", q.cnt["ev"])
            tr += 1
    q.barrier()

    srcs = qr + kr
    for ch in range(4):
        sl = slice(ch * 512, (ch + 1) * 512)
        for t in range(8):
            sq = sqs[t % 2]
            gi = ch * 8 + t
            if gi >= 2:
                q.wait("scalar", "mm", q.cnt["mm"] - 1)
            q.op("scalar", lambda e, a=sq, b=srcs[t], s=sl: e.activation(out=a[:], in_=b[:, s], func=AF.Square), inc=("ac", 1))
            q.wait("tensor", "ac", q.cnt["ac"])
            q.op("tensor", lambda e, a=sq, t=t, st=(t == 0), sp=(t == 7):
                 e.matmul(ps_n[:], selb[:, t * 16:(t + 1) * 16], a[:], start=st, stop=sp), inc=("mm", 1))
        q.barrier(engs_from=("tensor",), engs_to=("vector",))
        q.op("vector", lambda e, s=sl: e.tensor_copy(out=nrm[:, s], in_=ps_n[:]), inc=("ev", 1))
        q.barrier(engs_from=("vector",), engs_to=("tensor",))
    q.barrier(engs_from=("vector",), engs_to=("scalar",))
    q.op("scalar", lambda e: e.activation(out=nrm[:], in_=nrm[:], func=AF.Sqrt))
    q.barrier(engs_from=("scalar",), engs_to=("vector",))
    q.op("vector", lambda e: e.tensor_scalar(nrm[:], nrm[:], 1e-12, None, op0=ALU.max))
    q.op("vector", lambda e: e.reciprocal(out=nrm[:], in_=nrm[:]))
    q.barrier(engs_from=("vector",), engs_to=("tensor",))

    tabs1 = [tq1] * 4 + [tk1] * 4
    tabs2 = [tq2] * 4 + [tk2] * 4
    fins = qf + kf
    rothist = []
    for t in range(8):
        for ch in range(4):
            sl = slice(ch * 512, (ch + 1) * 512)
            gi = t * 4 + ch
            pb = ps_bc[gi % 2]
            pw = ps_sw[gi % 2]
            if gi >= 2:
                q.wait("tensor", "ev", rothist[gi - 2])
            q.op("tensor", lambda e, t=t, s=sl, pb=pb: e.matmul(pb[:], expb[:, t * 128:(t + 1) * 128], nrm[:, s], start=True, stop=True))
            q.op("tensor", lambda e, t=t, s=sl, pw=pw: e.matmul(pw[:], p32b[:], srcs[t][:, s], start=True, stop=True), inc=("mm", 1))
            mmv = q.cnt["mm"]
            q.wait("vector", "mm", mmv)
            q.op("vector", lambda e, pb=pb: e.tensor_copy(out=tC[:], in_=pb[:]))
            q.op("vector", lambda e, pw=pw: e.tensor_copy(out=tB[:], in_=pw[:]))
            q.op("vector", lambda e, t=t, s=sl: e.tensor_mul(tA[:], srcs[t][:, s], tC[:]))
            q.op("vector", lambda e: e.tensor_mul(tB[:], tB[:], tC[:]))
            q.op("vector", lambda e, t=t, s=sl: e.tensor_mul(tA[:], tA[:], tabs1[t][:, s]))
            q.op("vector", lambda e, t=t, s=sl: e.tensor_mul(tB[:], tB[:], tabs2[t][:, s]))
            q.op("vector", lambda e, t=t, s=sl: e.tensor_add(fins[t][:, s], tA[:], tB[:]), inc=("ev", 1))
            rothist.append(q.cnt["ev"])
    q.barrier()
    for t in range(4):
        q.dma("gpsimd", qf_o[t * 128:(t + 1) * 128, :], qf[t][:], store=True)
        q.dma("gpsimd", kf_o[t * 128:(t + 1) * 128, :], kf[t][:], store=True)
    for t in range(16):
        q.dma("sync", vt_o[t * 128:(t + 1) * 128, :], vtok[t][:], store=True)
    q.wait("gpsimd", "stg", q.cnt["stg"])
    q.wait("gpsimd", "sts", q.cnt["sts"])
    q._flush_waits("gpsimd")
    return q.finish()


def build_B1():
    q = Q()
    nc = q.nc
    qf = nc.dram_tensor("qf", [INNER, TL], BF16, kind="ExternalInput").ap()
    kfx = nc.dram_tensor("kfx", [INNER, 2 * TL], BF16, kind="ExternalInput").ap()
    vtx = nc.dram_tensor("vtx", [2 * TL, INNER], BF16, kind="ExternalInput").ap()
    xfb = nc.dram_tensor("xfb", [D, TL], BF16, kind="ExternalInput").ap()
    wo = nc.dram_tensor("wo", [INNER, D], F32, kind="ExternalInput").ap()
    mkm = nc.dram_tensor("mkm", [128, 8 * 512], BF16, kind="ExternalInput").ap()
    idf = nc.dram_tensor("idf", [128, 128], F32, kind="ExternalInput").ap()
    onesc = nc.dram_tensor("onesc", [128, 1], F32, kind="ExternalInput").ap()
    onesr = nc.dram_tensor("onesr", [1, 128], F32, kind="ExternalInput").ap()
    xm_o = nc.dram_tensor("xm", [D, TL], F32, kind="ExternalOutput").ap()
    xmb_o = nc.dram_tensor("xmb", [D, TL], BF16, kind="ExternalOutput").ap()

    qft = [q.sbuf(f"qf{t}", [128, TL], BF16) for t in range(4)]
    kft = [q.sbuf(f"kf{t}", [128, 2 * TL], BF16) for t in range(4)]
    vtt = [q.sbuf(f"vt{t}", [128, INNER], BF16) for t in range(32)]
    xft = [q.sbuf(f"xf{t}", [128, TL], BF16) for t in range(NDT)]
    wt = [q.sbuf(f"wo{t}", [128, D], BF16) for t in range(4)]
    mkmb = q.sbuf("mkmb", [128, 8 * 512], BF16)
    idb = q.sbuf("idb", [128, 128], BF16)
    onescb = q.sbuf("onescb", [128, 1], BF16)
    onesrf = q.sbuf("onesrf", [1, 128], F32)
    obuf = [q.sbuf(f"ob{t}", [128, TL], BF16) for t in range(4)]
    pT = q.sbuf("pT", [128, 8 * 512], BF16)
    zrow = q.sbuf("zrow", [1, 512], F32)
    zb = q.sbuf("zb", [64, 512], F32)
    xstg = [q.sbuf(f"xstg{i}", [128, 512], F32) for i in range(2)]
    xstgb = [q.sbuf(f"xstgb{i}", [128, 512], BF16) for i in range(2)]
    ps_T = [q.psum(f"ps_T{i}", [128, 512]) for i in range(2)]
    ps_z = q.psum("ps_z", [1, 512])
    ps_o = q.psum("ps_o", [64, 512])
    ps_zb = q.psum("ps_zb", [64, 512])

    for t in range(4):
        q.dma("sync", qft[t][:], qf[t * 128:(t + 1) * 128, :])
        q.dma("sync", kft[t][:], kfx[t * 128:(t + 1) * 128, :])
    for t in range(32):
        q.dma("sync", vtt[t][:], vtx[t * 128:(t + 1) * 128, :])
    for t in range(NDT):
        q.dma("sync", xft[t][:], xfb[t * 128:(t + 1) * 128, :])
    for t in range(4):
        q.dma("gpsimd", wt[t][:], wo[t * 128:(t + 1) * 128, :])
    q.dma("sync", mkmb[:], mkm[:])
    q.dma("gpsimd", idb[:], idf[:])
    q.dma("gpsimd", onescb[:], onesc[:])
    q.dma("gpsimd", onesrf[:], onesr[:])
    q.wait_loads("vector", "scalar", "tensor")

    for b in range(B):
        for h in range(HEADS):
            ht, hr = h // 2, (h % 2) * 64
            for kt in range(8):
                ps = ps_T[kt % 2]
                q.op("tensor", lambda e, kt=kt, ps=ps: e.matmul(
                    ps[:], idb[:], mkmb[:, kt * 512:(kt + 1) * 512], start=True, stop=False))
                q.op("tensor", lambda e, b=b, kt=kt, ps=ps, ht=ht, hr=hr: e.matmul(
                    ps[:], kft[ht][hr:hr + 64, b * 1024 + kt * 128: b * 1024 + (kt + 1) * 128],
                    qft[ht][hr:hr + 64, b * 512:(b + 1) * 512],
                    start=False, stop=True), inc=("mm", 1))
                q.wait("scalar", "mm", q.cnt["mm"])
                q.op("scalar", lambda e, kt=kt, ps=ps: e.activation(
                    out=pT[:, kt * 512:(kt + 1) * 512], in_=ps[:], func=AF.Exp), inc=("ac", 1))
                if kt >= 1:
                    q.wait("tensor", "ac", q.cnt["ac"] - 1)
                    q.op("tensor", lambda e, kt=kt, st=(kt == 1): e.matmul(
                        ps_z[:], onescb[:], pT[:, (kt - 1) * 512:kt * 512], start=st, stop=False))
                    q.op("tensor", lambda e, b=b, kt=kt, h=h, st=(kt == 1): e.matmul(
                        ps_o[:], vtt[b * 8 + kt - 1][:, h * 64:(h + 1) * 64],
                        pT[:, (kt - 1) * 512:kt * 512], start=st, stop=False))
            q.wait("tensor", "ac", q.cnt["ac"])
            q.op("tensor", lambda e: e.matmul(
                ps_z[:], onescb[:], pT[:, 7 * 512:8 * 512], start=False, stop=True))
            q.op("tensor", lambda e, b=b, h=h: e.matmul(
                ps_o[:], vtt[b * 8 + 7][:, h * 64:(h + 1) * 64], pT[:, 7 * 512:8 * 512],
                start=False, stop=True), inc=("mm", 1))
            q.wait("vector", "mm", q.cnt["mm"])
            q.op("vector", lambda e: e.tensor_copy(out=zrow[:], in_=ps_z[:]), inc=("ev", 1))
            q.wait("vector", "ev", q.cnt["ev"])
            q.op("vector", lambda e: e.reciprocal(out=zrow[:], in_=zrow[:]), inc=("ev", 1))
            q.wait("vector", "ev", q.cnt["ev"])
            q.barrier(engs_from=("vector",), engs_to=("tensor",))
            q.op("tensor", lambda e: e.matmul(ps_zb[:], onesrf[:, 0:64], zrow[:], start=True, stop=True), inc=("mm", 1))
            q.wait("vector", "mm", q.cnt["mm"])
            q.op("vector", lambda e: e.tensor_copy(out=zb[:], in_=ps_zb[:]))
            q.op("vector", lambda e, b=b, ht=ht, hr=hr: e.tensor_mul(
                obuf[ht][hr:hr + 64, b * 512:(b + 1) * 512], ps_o[:], zb[:]), inc=("ev", 1))
            q.barrier(engs_from=("vector",), engs_to=("tensor",))
    q.barrier()

    for ot in range(NDT):
        for ch in range(4):
            sl = slice(ch * 512, (ch + 1) * 512)
            gi = ot * 4 + ch
            ps = ps_T[gi % 2]
            ii = gi % 2
            if gi >= 2:
                q.wait("tensor", "stg", 16 * (2 * (gi - 1)))
            for kin in range(4):
                inc = ("mm", 1) if kin == 3 else None
                q.op("tensor", lambda e, kin=kin, ot=ot, s=sl, ps=ps, st=(kin == 0), sp=(kin == 3): e.matmul(
                    ps[:], wt[kin][:, ot * 128:(ot + 1) * 128], obuf[kin][:, s], start=st, stop=sp), inc=inc)
            q.wait("vector", "mm", q.cnt["mm"])
            q.op("vector", lambda e, ot=ot, s=sl, ps=ps, ii=ii: e.tensor_add(xstg[ii][:], xft[ot][:, s], ps[:]))
            q.op("vector", lambda e, ii=ii: e.tensor_copy(out=xstgb[ii][:], in_=xstg[ii][:]), inc=("ev", 1))
            q.wait("gpsimd", "ev", q.cnt["ev"])
            q.dma("gpsimd", xm_o[ot * 128:(ot + 1) * 128, sl], xstg[ii][:], store=True)
            q.dma("gpsimd", xmb_o[ot * 128:(ot + 1) * 128, sl], xstgb[ii][:], store=True)
    q.wait("gpsimd", "stg", q.cnt["stg"])
    q._flush_waits("gpsimd")
    return q.finish()



def _rot_tables(c, for_q):
    i = np.arange(32)
    freq = 10000.0 ** (-(2 * i) / 64.0)
    base = (2 * i + 0.4 * 64) / (1.4 * 64)
    t_glob = (512 * c + np.arange(SL)).astype(np.float64)
    theta = t_glob[None, :] * freq[:, None]
    power = (t_glob - 2048.0) / 256.0
    scale = base[:, None] ** power[None, :]
    if not for_q:
        scale = 1.0 / scale
    cos = np.cos(theta) * scale
    sin = np.sin(theta) * scale
    cs64 = np.concatenate([cos, cos], 0)
    ss64 = np.concatenate([-sin, sin], 0)
    CS = np.tile(np.tile(cs64, (2, 1)), (1, B)).astype(BF)
    SS = np.tile(np.tile(ss64, (2, 1)), (1, B)).astype(BF)
    return CS, SS


def _sel_exp_consts():
    selqk = np.zeros((128, 8 * 16), np.float32)
    expd = np.zeros((16, 8 * 128), np.float32)
    for t in range(8):
        for half in range(2):
            row = (2 * t + half) if t < 4 else (8 + 2 * (t - 4) + half)
            selqk[half * 64:(half + 1) * 64, t * 16 + row] = 1.0
            expd[row, t * 128 + half * 64: t * 128 + (half + 1) * 64] = 1.0
    p32 = np.zeros((128, 128), np.float32)
    for r in range(128):
        k = r + 32 if (r % 64) < 32 else r - 32
        p32[k, r] = 1.0
    idf = np.eye(128, dtype=np.float32)
    return selqk, expd, p32, idf


def _ffn_masks(c):
    mkm = np.zeros((128, 8 * 512), np.float32)
    qpos = 512 * c + np.arange(512)
    kpos = 512 * (c - 1) + np.arange(1024)
    kvalid = (kpos >= 0)
    allowed = (kpos[None, :] <= qpos[:, None]) & (kpos[None, :] >= qpos[:, None] - 512) & kvalid[None, :]
    neg = -1e30
    for kt in range(8):
        blk = np.where(allowed[:, kt * 128:(kt + 1) * 128], -20.0, neg).T
        mkm[:, kt * 512:(kt + 1) * 512] = blk
    return mkm.astype(BF)


def _qs_col(qs):
    return np.tile(np.asarray(qs, np.float32), 2).reshape(128, 1)


def stage_l0ffn_B0(xf_cm, wqkv_l, q_scale_l, k_scale_l):
    selqk, expd, p32, idf = _sel_exp_consts()
    nc = _prog("B0", build_B0)
    maps = []
    for c in range(NCORE):
        xprev = np.zeros((D, B), BF) if c == 0 else np.ascontiguousarray(
            xf_cm[c - 1][:, SL - 1::SL])
        csq, ssq = _rot_tables(c, True)
        csk, ssk = _rot_tables(c, False)
        maps.append({
            "xf": xf_cm[c], "xprev": xprev, "wqkv": np.ascontiguousarray(wqkv_l, np.float32),
            "csq": csq, "ssq": ssq, "csk": csk, "ssk": ssk,
            "qscol": _qs_col(q_scale_l), "kscol": _qs_col(k_scale_l),
            "selqk": selqk, "expd": expd, "p32": p32, "idf": idf,
        })
    r = _run(nc, maps).results
    return ([r[c]["qf"] for c in range(NCORE)], [r[c]["kf"] for c in range(NCORE)],
            [r[c]["vt"] for c in range(NCORE)])


def stage_l0ffn_B1(qf, kf, vt, xf_cm, wo_l):
    selqk, expd, p32, idf = _sel_exp_consts()
    nc = _prog("B1", build_B1)
    maps = []
    for c in range(NCORE):
        kfx = np.zeros((INNER, 2 * TL), BF)
        vtx = np.zeros((2 * TL, INNER), BF)
        for b in range(B):
            if c > 0:
                kfx[:, b * 1024: b * 1024 + 512] = kf[c - 1][:, b * 512:(b + 1) * 512]
                vtx[b * 1024: b * 1024 + 512, :] = vt[c - 1][b * 512:(b + 1) * 512, :]
            kfx[:, b * 1024 + 512:(b + 1) * 1024] = kf[c][:, b * 512:(b + 1) * 512]
            vtx[b * 1024 + 512:(b + 1) * 1024, :] = vt[c][b * 512:(b + 1) * 512, :]
        mkm = _ffn_masks(c)
        maps.append({
            "qf": qf[c], "kfx": kfx, "vtx": vtx, "xfb": xf_cm[c],
            "wo": np.ascontiguousarray(wo_l, np.float32),
            "mkm": mkm, "idf": idf, "onesc": _ONESC, "onesr": _ONESR,
        })
    r = _run(nc, maps).results
    return [r[c]["xm"] for c in range(NCORE)], [r[c]["xmb"] for c in range(NCORE)]


def build_B2(dbg=0):
    q = Q()
    nc = q.nc
    xmb = nc.dram_tensor("xmb", [D, TL], BF16, kind="ExternalInput").ap()
    xmprev = nc.dram_tensor("xmprev", [D, B], BF16, kind="ExternalInput").ap()
    xm = nc.dram_tensor("xm", [D, TL], F32, kind="ExternalInput").ap()
    lngb = nc.dram_tensor("lngb", [128, 16], F32, kind="ExternalInput").ap()
    w1r = nc.dram_tensor("w1r", [NJ * 128, D], F32, kind="ExternalInput").ap()
    b1r = nc.dram_tensor("b1r", [128, NJ], F32, kind="ExternalInput").ap()
    w2r = nc.dram_tensor("w2r", [HIDP, D], F32, kind="ExternalInput").ap()
    b2c = nc.dram_tensor("b2c", [128, NDT], F32, kind="ExternalInput").ap()
    onesc = nc.dram_tensor("onesc", [128, 1], F32, kind="ExternalInput").ap()
    onesr = nc.dram_tensor("onesr", [1, 128], F32, kind="ExternalInput").ap()
    xf2_o = nc.dram_tensor("xf2", [D, TL], F32, kind="ExternalOutput").ap()
    xf2b_o = nc.dram_tensor("xf2b", [D, TL], BF16, kind="ExternalOutput").ap()
    hdbg_o = nc.dram_tensor("hdbg", [HIDP, TL], BF16, kind="ExternalOutput").ap() if dbg == 2 else None

    xmt = [q.sbuf(f"xmb{t}", [128, TL], BF16) for t in range(NDT)]
    xpv = [q.sbuf(f"xpv{t}", [128, B], BF16) for t in range(NDT)]
    sh = [q.sbuf(f"sh{i}", [128, TL], BF16) for i in range(4)]
    hbuf = [q.sbuf(f"hb{j}", [128, TL], BF16) for j in range(NHT)]
    w1t = [q.sbuf(f"w1t{i}", [128, D], BF16) for i in range(2)]
    lngbs = q.sbuf("lngbs", [128, 16], F32)
    b1s = q.sbuf("b1s", [128, NJ], F32)
    b2s = q.sbuf("b2s", [128, NDT], F32)
    gl = [q.sbuf(f"gl{i}", [64, 512], F32) for i in range(2)]
    asb = q.sbuf("asb", [64, 512], F32)
    tmpf = q.sbuf("tmpf", [128, 512], F32)
    xmf = [q.sbuf(f"xmf{i}", [128, 512], F32) for i in range(2)]
    stg = [q.sbuf(f"stg{i}", [128, 512], F32) for i in range(2)]
    stgb = [q.sbuf(f"stgb{i}", [128, 512], BF16) for i in range(2)]
    cn = alloc_ln_common(q, TL)
    pp = (q.psum("ps_p0", [128, 512]), q.psum("ps_p1", [128, 512]))

    for t in range(NDT):
        q.dma("sync", xmt[t][:], xmb[t * 128:(t + 1) * 128, :])
        q.dma("sync", xpv[t][:], xmprev[t * 128:(t + 1) * 128, :])
    q.dma("sync", lngbs[:], lngb[:])
    q.dma("sync", b1s[:], b1r[:])
    q.dma("sync", b2s[:], b2c[:])
    load_ln_consts(q, cn, onesc, onesr)
    q.wait_loads("vector", "scalar", "tensor")

    _shift_tiles(q, xmt, xpv, sh)
    q.barrier(engs_from=("vector",), engs_to=("scalar", "tensor", "vector"))

    srcs = xmt[:4] + sh
    lnx = srcs
    nch = 4
    for ch in range(nch):
        sl = slice(ch * 512, (ch + 1) * 512)
        for t in range(NDT):
            sq = cn["sqs"][t % 2]
            if ch * NDT + t >= 2:
                q.wait("scalar", "mm", q.cnt["mm"] - 1)
            q.op("scalar", lambda e, a=sq, b=srcs[t], s=sl: e.activation(out=a[:], in_=b[:, s], func=AF.Square), inc=("ac", 1))
            q.op("tensor", lambda e, r=srcs[t], s=sl, st=(t == 0), sp=(t == NDT - 1), ps=cn["ps_st"][0]:
                 e.matmul(ps[:], cn["onesc_b"][:], r[:, s], start=st, stop=sp))
            q.wait("tensor", "ac", q.cnt["ac"])
            q.op("tensor", lambda e, r=sq, st=(t == 0), sp=(t == NDT - 1), ps=cn["ps_st"][1]:
                 e.matmul(ps[:], cn["onesc_b"][:], r[:], start=st, stop=sp), inc=("mm", 1))
        q.barrier(engs_from=("tensor",), engs_to=("vector",))
        q.op("vector", lambda e, s=sl: e.tensor_copy(out=cn["ssum"][:, s], in_=cn["ps_st"][0][:]))
        q.op("vector", lambda e, s=sl: e.tensor_copy(out=cn["sssq"][:, s], in_=cn["ps_st"][1][:]), inc=("ev", 1))
        q.barrier(engs_from=("vector",), engs_to=("tensor",))
    emit_ln_finalize(q, cn)
    for ch in range(nch):
        sl = slice(ch * 512, (ch + 1) * 512)
        q.op("tensor", lambda e, s=sl: e.matmul(cn["ps_bc"][0][:], cn["onesr_f"][:], cn["rB"][:, s], start=True, stop=True))
        q.op("tensor", lambda e, s=sl: e.matmul(cn["ps_bc"][1][:], cn["onesr_f"][:], cn["rC"][:, s], start=True, stop=True), inc=("mm", 1))
        q.barrier(engs_from=("tensor",), engs_to=("vector",))
        q.op("vector", lambda e, s=sl: e.tensor_copy(out=cn["bcs"][0][:, s], in_=cn["ps_bc"][0][:]))
        q.op("vector", lambda e, s=sl: e.tensor_copy(out=cn["bcs"][1][:, s], in_=cn["ps_bc"][1][:]), inc=("ev", 1))
        q.barrier(engs_from=("vector",), engs_to=("tensor",))
    for t in range(NDT):
        for ch in range(4):
            sl = slice(ch * 512, (ch + 1) * 512)
            q.op("vector", lambda e, a=srcs[t], s=sl: e.tensor_mul(tmpf[:], a[:, s], cn["bcs"][0][:, s]))
            q.op("vector", lambda e, s=sl: e.tensor_add(tmpf[:], tmpf[:], cn["bcs"][1][:, s]))
            q.op("vector", lambda e, t=t, o=lnx[t], s=sl: e.tensor_scalar(
                o[:, s], tmpf[:], lngbs[:, t:t + 1], lngbs[:, NDT + t:NDT + t + 1], op0=ALU.mult, op1=ALU.add), inc=("ev", 1))
    q.barrier()
    if dbg == 1:
        for t in range(NDT):
            q.dma("gpsimd", xf2b_o[t * 128:(t + 1) * 128, :], lnx[t][:], store=True)
        q.dma("gpsimd", xf2_o[:, :], xm[:, :], store=True)
        q.wait("gpsimd", "stg", q.cnt["stg"])
        q._flush_waits("gpsimd")
        return q.finish()

    for j in range(NJ):
        wb = w1t[j % 2]
        if j >= 2:
            q.wait("gpsimd", "mm", q.cnt["mm"] - 4)
        q.dma("gpsimd", wb[:], w1r[j * 128:(j + 1) * 128, :])
        ldg_j = q.cnt["ldg"]
        jj, half = j // 2, (j % 2) * 64
        for ch in range(4):
            sl = slice(ch * 512, (ch + 1) * 512)
            gi = j * 4 + ch
            ps = pp[gi % 2]
            if gi >= 2:
                q.wait("tensor", "ev", q.cnt["ev"] - 1)
            q.wait("tensor", "ldg", ldg_j)
            for kin in range(NDT):
                inc = ("mm", 1) if kin == NDT - 1 else None
                q.op("tensor", lambda e, kin=kin, s=sl, ps=ps, wb=wb, st=(kin == 0), sp=(kin == NDT - 1):
                     e.matmul(ps[:], wb[:, kin * 128:(kin + 1) * 128], lnx[kin][:, s], start=st, stop=sp), inc=inc)
            mmv = q.cnt["mm"]
            q.wait("scalar", "mm", mmv)
            if gi >= 2:
                q.wait("scalar", "ev", q.cnt["ev"] - 1)
            gli = gl[gi % 2]
            q.op("scalar", lambda e, ps=ps, j=j, g=gli: e.activation(
                out=g[:], in_=ps[64:128, :], func=AF.Gelu, bias=b1s[64:128, j:j + 1]), inc=("ac", 1))
            q.wait("vector", "ac", q.cnt["ac"])
            q.op("vector", lambda e, ps=ps, j=j: e.tensor_scalar(asb[:], ps[0:64, :], b1s[0:64, j:j + 1], None, op0=ALU.add))
            q.op("vector", lambda e, jj=jj, half=half, s=sl, g=gli: e.tensor_mul(hbuf[jj][half:half + 64, s], asb[:], g[:]), inc=("ev", 1))
    q.barrier()

    if dbg == 2:
        q.barrier()
        for jj in range(NHT):
            q.dma("gpsimd", hdbg_o[jj * 128:(jj + 1) * 128, :], hbuf[jj][:], store=True)
        q.dma("gpsimd", xf2_o[:, :], xm[:, :], store=True)
        for t in range(NDT):
            q.dma("sync", xf2b_o[t * 128:(t + 1) * 128, :], lnx[t][:], store=True)
        q.wait("gpsimd", "stg", q.cnt["stg"])
        q.wait("gpsimd", "sts", q.cnt["sts"])
        q._flush_waits("gpsimd")
        return q.finish()

    w2t = []
    for jj in range(NHT):
        if jj < 16:
            view = xmt[jj // 2][:, (jj % 2) * D:(jj % 2) * D + D]
        else:
            k = jj - 16
            view = sh[k // 2][:, (k % 2) * D:(k % 2) * D + D]
        w2t.append(view)
        q.dma("gpsimd", view, w2r[jj * 128:(jj + 1) * 128, :])
    q.wait_loads("tensor", "vector")
    hist3 = []
    lds0 = q.cnt["lds"]
    for ot in range(NDT):
        for ch in range(4):
            sl = slice(ch * 512, (ch + 1) * 512)
            gi = ot * 4 + ch
            ps = pp[gi % 2]
            if gi >= 2:
                q.wait("sync", "ev", hist3[gi - 2])
            q.dma("sync", xmf[gi % 2][:], xm[ot * 128:(ot + 1) * 128, sl])
            if gi >= 2:
                q.wait("tensor", "ev", hist3[gi - 2])
            for jj in range(NHT):
                inc = ("mm", 1) if jj == NHT - 1 else None
                q.op("tensor", lambda e, jj=jj, ot=ot, s=sl, ps=ps, st=(jj == 0), sp=(jj == NHT - 1):
                     e.matmul(ps[:], w2t[jj][:, ot * 128:(ot + 1) * 128], hbuf[jj][:, s], start=st, stop=sp), inc=inc)
            q.wait("vector", "mm", q.cnt["mm"])
            q.wait("vector", "lds", q.cnt["lds"])
            if gi >= 2:
                q.wait("vector", "stg", 16 * (2 * (gi - 1)))
            ii = gi % 2
            q.op("vector", lambda e, ps=ps, ot=ot, ii=ii: e.scalar_tensor_tensor(
                out=stg[ii][:], in0=ps[:], scalar=b2s[:, ot:ot + 1], in1=xmf[ii][:],
                op0=ALU.add, op1=ALU.add))
            q.op("vector", lambda e, ii=ii: e.tensor_copy(out=stgb[ii][:], in_=stg[ii][:]), inc=("ev", 1))
            hist3.append(q.cnt["ev"])
            q.wait("gpsimd", "ev", q.cnt["ev"])
            q.dma("gpsimd", xf2_o[ot * 128:(ot + 1) * 128, sl], stg[ii][:], store=True)
            q.dma("gpsimd", xf2b_o[ot * 128:(ot + 1) * 128, sl], stgb[ii][:], store=True)
    q.wait("gpsimd", "stg", q.cnt["stg"])
    q._flush_waits("gpsimd")
    return q.finish()


def _w1_prep(w1_l, b1_l):
    w1 = np.asarray(w1_l, np.float32)
    b1 = np.asarray(b1_l, np.float32)
    a, g = w1[:, :HID], w1[:, HID:]
    ba, bg = b1[:HID], b1[HID:]
    ap = np.zeros((D, HIDP), np.float32); ap[:, :HID] = a
    gp = np.zeros((D, HIDP), np.float32); gp[:, :HID] = g
    bap = np.zeros(HIDP, np.float32); bap[:HID] = ba
    bgp = np.zeros(HIDP, np.float32); bgp[:HID] = bg
    w1r = np.zeros((NJ * 128, D), np.float32)
    b1r = np.zeros((128, NJ), np.float32)
    for j in range(NJ):
        cols_a = ap[:, j * 64:(j + 1) * 64]
        cols_g = gp[:, j * 64:(j + 1) * 64]
        w128 = np.concatenate([cols_a, cols_g], axis=1)
        w1r[j * 128:(j + 1) * 128] = w128.reshape(NDT, 128, 128).transpose(1, 0, 2).reshape(128, D)
        b1r[0:64, j] = bap[j * 64:(j + 1) * 64]
        b1r[64:128, j] = bgp[j * 64:(j + 1) * 64]
    return w1r, b1r


def _w2_prep(w2_l, b2_l):
    w2 = np.zeros((HIDP, D), np.float32)
    w2[:HID] = np.asarray(w2_l, np.float32)
    b2c = np.asarray(b2_l, np.float32).reshape(NDT, 128).T.copy()
    return w2, b2c


def stage_l0ffn_B2(xm, xmb, w1_l, b1_l, w2_l, b2_l, lng_l, lnb_l):
    w1r, b1r = _w1_prep(w1_l, b1_l)
    w2r, b2c = _w2_prep(w2_l, b2_l)
    lngb = np.stack([np.asarray(lng_l, np.float32).reshape(NDT, 128).T,
                     np.asarray(lnb_l, np.float32).reshape(NDT, 128).T], axis=1).reshape(128, 16)
    lngb = np.concatenate([np.asarray(lng_l, np.float32).reshape(NDT, 128).T,
                           np.asarray(lnb_l, np.float32).reshape(NDT, 128).T], axis=1)
    nc = _prog("B2", build_B2)
    maps = []
    for c in range(NCORE):
        xmprev = np.zeros((D, B), BF) if c == 0 else np.ascontiguousarray(xmb[c - 1][:, SL - 1::SL])
        maps.append({
            "xmb": xmb[c], "xmprev": xmprev, "xm": xm[c], "lngb": lngb,
            "w1r": w1r, "b1r": b1r, "w2r": w2r, "b2c": b2c,
            "onesc": _ONESC, "onesr": _ONESR,
        })
    r = _run(nc, maps).results
    return [r[c]["xf2"] for c in range(NCORE)], [r[c]["xf2b"] for c in range(NCORE)]


def build_SW(mode="full"):
    q = Q()
    nc = q.nc
    y = nc.dram_tensor("y", [D, TL], F32, kind="ExternalInput").ap()
    res = nc.dram_tensor("res", [D, TL], F32, kind="ExternalInput").ap()
    onesc = nc.dram_tensor("onesc", [128, 1], F32, kind="ExternalInput").ap()
    onesr = nc.dram_tensor("onesr", [1, 128], F32, kind="ExternalInput").ap()
    x_o = nc.dram_tensor("x", [D, TL], F32, kind="ExternalOutput").ap()
    ln_o = nc.dram_tensor("ln", [D, TL], BF16, kind="ExternalOutput").ap() if mode != "x" else None
    lnf_o = nc.dram_tensor("lnf", [D, TL], F32, kind="ExternalOutput").ap() if mode == "full" else None

    yt = [q.sbuf(f"y{t}", [128, TL], F32) for t in range(NDT)]
    rt = [q.sbuf(f"r{t}", [128, TL], F32) for t in range(NDT)]
    xb = [q.sbuf(f"b{t}", [128, TL], BF16) for t in range(NDT)]
    cn = alloc_ln_common(q, TL)
    for t in range(NDT):
        q.dma("sync", yt[t][:], y[t * 128:(t + 1) * 128, :])
        q.dma("sync", rt[t][:], res[t * 128:(t + 1) * 128, :])
    load_ln_consts(q, cn, onesc, onesr)
    q.wait_loads("vector", "scalar", "tensor")
    emit_ln_cm(q, yt, xb, cn, TL)
    for t in range(NDT):
        q.op("vector", lambda e, t=t: e.tensor_add(rt[t][:], rt[t][:], yt[t][:]), inc=("ev", 1))
        q.wait("sync", "ev", q.cnt["ev"])
        q.dma("sync", x_o[t * 128:(t + 1) * 128, :], rt[t][:], store=True)
    if mode == "x":
        q.wait("sync", "sts", q.cnt["sts"])
        q._flush_waits("sync")
        return q.finish()
    q.wait("vector", "sts", q.cnt["sts"])
    emit_ln_cm(q, rt, xb, cn, TL)
    q.barrier(engs_from=("vector",), engs_to=("gpsimd", "sync"))
    for t in range(NDT):
        q.dma("gpsimd", ln_o[t * 128:(t + 1) * 128, :], xb[t][:], store=True)
        if mode == "full":
            q.dma("sync", lnf_o[t * 128:(t + 1) * 128, :], rt[t][:], store=True)
    q.wait("gpsimd", "stg", q.cnt["stg"])
    if mode == "full":
        q.wait("gpsimd", "sts", q.cnt["sts"])
    q._flush_waits("gpsimd")
    return q.finish()


_PROG = {}


def _prog(name, builder):
    if name not in _PROG:
        _PROG[name] = builder()
    return _PROG[name]


def stage_sandwich(y_cm, res_cm, mode="full"):
    nc = _prog("SW_" + mode, lambda: build_SW(mode))
    maps = [{"y": y_cm[c], "res": res_cm[c], "onesc": _ONESC, "onesr": _ONESR}
            for c in range(NCORE)]
    r = _run(nc, maps).results
    return ([r[c]["x"] for c in range(NCORE)],
            [r[c]["ln"] for c in range(NCORE)] if mode != "x" else None,
            [r[c]["lnf"] for c in range(NCORE)] if mode == "full" else None)


def stage_l0ffn(ln1_cm, x1_cm, ffn):
    xf = ln1_cm
    xf_f32 = None
    for l in range(2):
        qf, kf, vt = stage_l0ffn_B0(xf, ffn["wqkv"][l], ffn["q_scale"][l], ffn["k_scale"][l])
        xm, xmb = stage_l0ffn_B1(qf, kf, vt, xf, ffn["wo"][l])
        xf2, xf2b = stage_l0ffn_B2(xm, xmb, ffn["w1"][l], ffn["b1"][l], ffn["w2"][l], ffn["b2"][l],
                                   ffn["ln_g"][l], ffn["ln_b"][l])
        xf = xf2b
        xf_f32 = xf2
    return stage_sandwich(xf_f32, x1_cm, mode="xln")


H16 = 16
DK = 32
DV = 64
PAIRS = [(t, s) for t in range(B) for s in range(t + 1)]


def build_D(dbg=0):
    q = Q()
    nc = q.nc
    xn_i = nc.dram_tensor("xn", [D, TL], BF16, kind="ExternalInput").ap()
    convw = nc.dram_tensor("convw", [128, 32], F32, kind="ExternalInput").ap()
    wq = nc.dram_tensor("wq", [D, 512], F32, kind="ExternalInput").ap()
    wkg = nc.dram_tensor("wkg", [D, 512], F32, kind="ExternalInput").ap()
    wv = nc.dram_tensor("wv", [D, D], F32, kind="ExternalInput").ap()
    wg = nc.dram_tensor("wg", [D, D], F32, kind="ExternalInput").ap()
    wout = nc.dram_tensor("wout", [D, D], F32, kind="ExternalInput").ap()
    bgc = nc.dram_tensor("bgc", [128, NDT], F32, kind="ExternalInput").ap()
    augc = nc.dram_tensor("augc", [128, 4], F32, kind="ExternalInput").ap()
    selH = nc.dram_tensor("selH", [128, 64], F32, kind="ExternalInput").ap()
    selV = nc.dram_tensor("selV", [128, 128], F32, kind="ExternalInput").ap()
    expH = nc.dram_tensor("expH", [16, 8 * 128], F32, kind="ExternalInput").ap()
    l1o_o = nc.dram_tensor("l1o", [D, TL], F32, kind="ExternalOutput").ap()
    gsd = nc.dram_tensor("gsd", [D, TL], BF16, kind="Internal").ap()
    dbg_o = nc.dram_tensor("dbg", [D, TL], BF16, kind="ExternalOutput").ap() if dbg else None

    xnb = [q.sbuf(f"xn{t}", [128, TL], BF16) for t in range(NDT)]
    wts = {}
    for nm, wd in (("wq", 4), ("wkg", 4), ("wv", 8), ("wg", 8)):
        wts[nm] = [q.sbuf(f"{nm}{t}", [128, wd * 128], BF16) for t in range(NDT)]
    qb = [q.sbuf(f"qb{t}", [128, TL], BF16) for t in range(4)]
    gkf = [q.sbuf(f"gk{t}", [128, TL], F32) for t in range(4)]
    k1b = [q.sbuf(f"k1{t}", [128, TL], BF16) for t in range(4)]
    vb = [q.sbuf(f"vb{t}", [128, TL], BF16) for t in range(NDT)]
    gss = [q.sbuf(f"gss{i}", [128, 512], BF16) for i in range(2)]
    gsl = [q.sbuf(f"gsl{i}", [128, 512], BF16) for i in range(2)]
    wob = [qb[t][:, 0:D] for t in range(4)] + [k1b[t][:, 0:D] for t in range(4)]
    ob = [gkf[dt // 4][:, (dt % 4) * 512:(dt % 4 + 1) * 512] for dt in range(NDT)]
    obb = [gkf[2][:].bitcast(BF16)[:, dt * 512:(dt + 1) * 512] for dt in range(NDT)]
    gob = [gkf[3][:].bitcast(BF16)[:, dt * 512:(dt + 1) * 512] for dt in range(NDT)]
    abuf = q.sbuf("abuf", [16, 10 * 512], BF16)
    augw = q.sbuf("augw", [16, 4 * 512], BF16)
    cw = q.sbuf("cw", [128, 32], F32)
    bgs = q.sbuf("bgs", [128, NDT], F32)
    augs = q.sbuf("augs", [128, 4], F32)
    selHb = q.sbuf("selHb", [128, 64], BF16)
    selAb = q.sbuf("selAb", [128, 64], BF16)
    selVb = q.sbuf("selVb", [128, 128], BF16)
    selVf = q.sbuf("selVf", [128, 128], F32)
    expHf = q.sbuf("expHf", [16, 8 * 128], F32)
    expHb = q.sbuf("expHb", [16, 8 * 128], BF16)
    scr = [q.sbuf(f"scr{i}", [128, 512], F32) for i in range(3)]
    scrb = [gss[0], gss[1], gsl[0], gsl[1]]
    escr = scr[2]
    nrA = wts["wg"][2][0:16, :].bitcast(F32)
    nrB = wts["wg"][3][0:16, :].bitcast(F32)
    nrC = wts["wg"][4][0:16, :].bitcast(F32)
    stg = [wts["wg"][0][:].bitcast(F32), wts["wg"][1][:].bitcast(F32)]
    ps_p = [q.psum(f"psp{i}", [128, 512]) for i in range(2)]
    ps_A = [q.psum(f"psA{i}", [16, 512]) for i in range(2)]
    ps_B = [q.psum(f"psB{i}", [128, 512]) for i in range(2)]
    ps_s = [q.psum(f"pss{i}", [16, 512]) for i in range(2)]

    for t in range(NDT):
        q.dma("sync", xnb[t][:], xn_i[t * 128:(t + 1) * 128, :])
        q.dma("gpsimd", wts["wq"][t][:], wq[t * 128:(t + 1) * 128, :])
        q.dma("gpsimd", wts["wkg"][t][:], wkg[t * 128:(t + 1) * 128, :])
        q.dma("gpsimd", wts["wv"][t][:], wv[t * 128:(t + 1) * 128, :])
        q.dma("gpsimd", wts["wg"][t][:], wg[t * 128:(t + 1) * 128, :])
    q.dma("sync", cw[:], convw[:])
    q.dma("sync", bgs[:], bgc[:])
    q.dma("sync", augs[:], augc[:])
    q.dma("gpsimd", selHb[:], selH[:])
    q.dma("gpsimd", selVb[:], selV[:])
    q.dma("sync", selVf[:], selV[:])
    q.dma("sync", expHf[:], expH[:])
    q.dma("gpsimd", expHb[:], expH[:])
    q.wait_loads("vector", "scalar", "tensor")

    for dt in range(NDT):
        for b in reversed(range(B)):
            terms = [(bp, 3 - (b - bp)) for bp in range(max(0, b - 3), b + 1)]
            first_bp, first_k = terms[0]
            inc0 = ("ev", 1) if len(terms) == 1 else None
            q.op("vector", lambda e, dt=dt, bp=first_bp, k=first_k: e.tensor_scalar(
                scr[0][:], xnb[dt][:, bp * SL:(bp + 1) * SL], cw[:, dt * 4 + k:dt * 4 + k + 1], None, op0=ALU.mult), inc=inc0)
            si = 0
            for ti, (bp, k) in enumerate(terms[1:]):
                inc1 = ("ev", 1) if ti == len(terms) - 2 else None
                q.op("vector", lambda e, dt=dt, bp=bp, k=k, si=si: e.scalar_tensor_tensor(
                    out=scr[1 - si][:], in0=xnb[dt][:, bp * SL:(bp + 1) * SL],
                    scalar=cw[:, dt * 4 + k:dt * 4 + k + 1], in1=scr[si][:], op0=ALU.mult, op1=ALU.add), inc=inc1)
                si = 1 - si
            q.wait("scalar", "ev", q.cnt["ev"])
            q.op("scalar", lambda e, dt=dt, b=b, si=si: e.activation(
                out=xnb[dt][:, b * SL:(b + 1) * SL], in_=scr[si][:], func=AF.Silu), inc=("ac", 1))
            q.wait("vector", "ac", q.cnt["ac"])
    q.barrier()
    if dbg == 1:
        for t in range(NDT):
            q.dma("gpsimd", dbg_o[t * 128:(t + 1) * 128, :], xnb[t][:], store=True)
        q.op("vector", lambda e: e.memset(stg[0], 0.0), inc=("ev", 1))
        q.wait("gpsimd", "ev", q.cnt["ev"])
        for t in range(NDT):
            for ch in range(4):
                q.dma("gpsimd", l1o_o[t * 128:(t + 1) * 128, ch * 512:(ch + 1) * 512], stg[0], store=True)
        q.wait("gpsimd", "stg", q.cnt["stg"])
        q._flush_waits("gpsimd")
        return q.finish()

    def evac_copy(dst):
        def f(ot, sl, ps, mmv):
            q.wait("vector", "mm", mmv)
            q.op("vector", lambda e, o=dst[ot], s=sl, ps=ps: e.tensor_copy(out=o[:, s], in_=ps[:]), inc=("ev", 1))
            return [("ev", q.cnt["ev"])]
        return f

    emit_proj(q, wts["wq"], xnb, ps_p, evac_copy(qb), n_out_tiles=4)

    def evac_kg(ot, sl, ps, mmv):
        q.wait("scalar", "mm", mmv)
        q.wait("scalar", "ev", q.cnt["ev"])
        q.op("scalar", lambda e, ps=ps: e.activation(out=scr[0][:], in_=ps[:], func=AF.Abs), inc=("ac", 1))
        q.wait("vector", "ac", q.cnt["ac"])
        q.op("vector", lambda e, ps=ps: e.tensor_scalar_min(scr[1][:], ps[:], 0.0), inc=("ev", 1))
        q.op("scalar", lambda e: e.activation(out=scr[2][:], in_=scr[0][:], func=AF.Exp, scale=-1.0))
        q.op("scalar", lambda e: e.activation(out=scr[2][:], in_=scr[2][:], func=AF.Ln, bias=1.0), inc=("ac", 1))
        q.wait("vector", "ac", q.cnt["ac"])
        q.op("vector", lambda e: e.tensor_sub(scr[1][:], scr[1][:], scr[2][:]))
        q.op("vector", lambda e, o=gkf[ot], s=sl: e.tensor_scalar(o[:, s], scr[1][:], 1.0 / 16.0, None, op0=ALU.mult), inc=("ev", 1))
        return [("ev", q.cnt["ev"]), ("ac", q.cnt["ac"])]

    emit_proj(q, wts["wkg"], xnb, ps_p, evac_kg, n_out_tiles=4)
    emit_proj(q, wts["wv"], xnb, ps_p, evac_copy(vb), n_out_tiles=8)

    g_ctr = [0]

    def evac_g(ot, sl, ps, mmv):
        gi = g_ctr[0]; g_ctr[0] += 1
        ii = gi % 2
        q.wait("scalar", "mm", mmv)
        if gi >= 2:
            q.wait("scalar", "stg", 16 * (gi - 1))
        q.op("scalar", lambda e, s=sl, ps=ps, ot=ot, ii=ii: e.activation(
            out=gss[ii][:], in_=ps[:], func=AF.Silu, bias=bgs[:, ot:ot + 1]), inc=("ac", 1))
        q.wait("gpsimd", "ac", q.cnt["ac"])
        q.dma("gpsimd", gsd[ot * 128:(ot + 1) * 128, sl], gss[ii][:], store=True)
        return [("ac", q.cnt["ac"])]

    emit_proj(q, wts["wg"], xnb, ps_p, evac_g, n_out_tiles=8)
    q.barrier()
    if dbg == 2:
        for t in range(4):
            q.dma("gpsimd", dbg_o[t * 128:(t + 1) * 128, :], qb[t][:], store=True)
        for t in range(4):
            q.dma("sync", l1o_o[(4 + t) * 128:(5 + t) * 128, :], gkf[t][:], store=True)
        q.op("vector", lambda e: e.memset(stg[0], 0.0), inc=("ev", 1))
        q.wait("gpsimd", "ev", q.cnt["ev"])
        for t in range(4):
            for ch in range(4):
                q.dma("gpsimd", l1o_o[t * 128:(t + 1) * 128, ch * 512:(ch + 1) * 512], stg[0], store=True)
        q.wait("gpsimd", "stg", q.cnt["stg"])
        q.wait("gpsimd", "sts", q.cnt["sts"])
        q._flush_waits("gpsimd")
        return q.finish()

    for dt in range(4):
        for b in range(B):
            bs = slice(b * SL, (b + 1) * SL)
            q.op("scalar", lambda e, dt=dt, bs=bs: e.activation(out=escr[:], in_=gkf[dt][:, bs], func=AF.Exp), inc=("ac", 1))
            q.wait("vector", "ac", q.cnt["ac"])
            q.op("vector", lambda e, dt=dt, bs=bs: e.tensor_scalar(k1b[dt][:, bs], escr[:], -1.0, 1.0, op0=ALU.mult, op1=ALU.add), inc=("ev", 1))
            q.wait("scalar", "ev", q.cnt["ev"])
    for dt in range(4):
        q.op("vector", lambda e, dt=dt: e.tensor_scalar(
            selAb[:, dt * 16:(dt + 1) * 16], selHb[:, dt * 16:(dt + 1) * 16],
            augs[:, dt:dt + 1], None, op0=ALU.mult), inc=("ev", 1))
    q.barrier(engs_from=("vector",), engs_to=("tensor",))
    for t in range(B):
        pa = ps_A[t % 2]
        if t >= 2:
            q.wait("tensor", "ev", q.cnt["ev"] - 1)
        for dt in range(4):
            q.op("vector", lambda e, dt=dt, t=t: e.tensor_mul(
                scrb[dt][:], qb[dt][:, t * SL:(t + 1) * SL], k1b[dt][:, t * SL:(t + 1) * SL]))
        q.barrier(engs_from=("vector",), engs_to=("tensor",))
        for dt in range(4):
            q.op("tensor", lambda e, dt=dt, pa=pa, st=(dt == 0), sp=(dt == 3): e.matmul(
                pa[:], selAb[:, dt * 16:(dt + 1) * 16], scrb[dt][:], start=st, stop=sp),
                inc=("mm", 1) if dt == 3 else None)
        q.wait("vector", "mm", q.cnt["mm"])
        q.op("vector", lambda e, t=t, pa=pa: e.tensor_copy(out=augw[:, t * 512:(t + 1) * 512], in_=pa[:]), inc=("ev", 1))
    q.barrier()
    for dt in range(4):
        for b in range(1, B):
            q.op("vector", lambda e, dt=dt, b=b: e.tensor_add(
                gkf[dt][:, b * SL:(b + 1) * SL], gkf[dt][:, b * SL:(b + 1) * SL],
                gkf[dt][:, (b - 1) * SL:b * SL]))
    q.barrier(engs_from=("vector",), engs_to=("scalar",))
    for dt in range(4):
        for b in range(B):
            bs = slice(b * SL, (b + 1) * SL)
            q.op("scalar", lambda e, dt=dt, bs=bs: e.activation(out=escr[:], in_=gkf[dt][:, bs], func=AF.Exp), inc=("ac", 1))
            q.wait("vector", "ac", q.cnt["ac"])
            q.op("vector", lambda e, dt=dt, bs=bs: e.tensor_mul(qb[dt][:, bs], qb[dt][:, bs], escr[:]), inc=("ev", 1))
            q.wait("scalar", "ev", q.cnt["ev"])
            q.op("scalar", lambda e, dt=dt, bs=bs: e.activation(out=escr[:], in_=gkf[dt][:, bs], func=AF.Exp, scale=-1.0), inc=("ac", 1))
            q.wait("vector", "ac", q.cnt["ac"])
            q.op("vector", lambda e, dt=dt, bs=bs: e.tensor_mul(k1b[dt][:, bs], k1b[dt][:, bs], escr[:]), inc=("ev", 1))
            q.wait("scalar", "ev", q.cnt["ev"])
    q.barrier()
    for pi, (t, s) in enumerate(PAIRS):
        pa = ps_A[pi % 2]
        if pi >= 2:
            q.wait("tensor", "ev", q.cnt["ev"] - 1)
        for dt in range(4):
            q.op("vector", lambda e, dt=dt, t=t, s=s: e.tensor_mul(
                scrb[dt][:], qb[dt][:, t * SL:(t + 1) * SL], k1b[dt][:, s * SL:(s + 1) * SL]))
        q.barrier(engs_from=("vector",), engs_to=("tensor",))
        for dt in range(4):
            q.op("tensor", lambda e, dt=dt, pa=pa, st=(dt == 0), sp=(dt == 3): e.matmul(
                pa[:], selHb[:, dt * 16:(dt + 1) * 16], scrb[dt][:], start=st, stop=sp),
                inc=("mm", 1) if dt == 3 else None)
        q.wait("vector", "mm", q.cnt["mm"])
        q.op("vector", lambda e, pi=pi, pa=pa: e.tensor_scalar(
            abuf[:, pi * 512:(pi + 1) * 512], pa[:], 0.17677669529663687, None, op0=ALU.mult), inc=("ev", 1))
    q.barrier()
    if dbg == 4:
        q.dma("gpsimd", dbg_o[0:16, 0:2048], abuf[:, 0:2048], store=True)
        q.dma("gpsimd", dbg_o[16:32, 0:2048], abuf[:, 2048:4096], store=True)
        q.dma("gpsimd", dbg_o[32:48, 0:1024], abuf[:, 4096:5120], store=True)
        q.dma("gpsimd", dbg_o[48:64, 0:2048], augw[:], store=True)
        q.op("vector", lambda e: e.memset(stg[0], 0.0), inc=("ev", 1))
        q.wait("gpsimd", "ev", q.cnt["ev"])
        for t in range(NDT):
            for ch in range(4):
                q.dma("gpsimd", l1o_o[t * 128:(t + 1) * 128, ch * 512:(ch + 1) * 512], stg[0], store=True)
        q.wait("gpsimd", "stg", q.cnt["stg"])
        q._flush_waits("gpsimd")
        return q.finish()
    if dbg == 3:
        for t in range(4):
            q.dma("gpsimd", dbg_o[t * 128:(t + 1) * 128, :], qb[t][:], store=True)
            q.dma("gpsimd", dbg_o[(4 + t) * 128:(5 + t) * 128, :], k1b[t][:], store=True)
        q.op("vector", lambda e: e.memset(stg[0], 0.0), inc=("ev", 1))
        q.wait("gpsimd", "ev", q.cnt["ev"])
        for t in range(NDT):
            for ch in range(4):
                q.dma("gpsimd", l1o_o[t * 128:(t + 1) * 128, ch * 512:(ch + 1) * 512], stg[0], store=True)
        q.wait("gpsimd", "stg", q.cnt["stg"])
        q._flush_waits("gpsimd")
        return q.finish()
    for t in range(NDT):
        q.dma("gpsimd", wob[t], wout[t * 128:(t + 1) * 128, :])
    q.wait("tensor", "ldg", q.cnt["ldg"])

    pair_idx = {(t, s): i for i, (t, s) in enumerate(PAIRS)}
    for t in range(B):
        sl = slice(t * SL, (t + 1) * SL)
        for dt in range(NDT):
            for s in range(t + 1):
                pi = pair_idx[(t, s)]
                pb = ps_B[0]
                q.op("tensor", lambda e, dt=dt, pi=pi, pb=pb: e.matmul(
                    pb[:], expHb[:, dt * 128:(dt + 1) * 128], abuf[:, pi * 512:(pi + 1) * 512],
                    start=True, stop=True), inc=("mm", 1))
                q.wait("vector", "mm", q.cnt["mm"])
                if s == 0:
                    q.op("vector", lambda e, dt=dt, s=s, pb=pb: e.tensor_mul(
                        ob[dt][:], vb[dt][:, s * SL:(s + 1) * SL], pb[:]), inc=("ev", 1))
                else:
                    q.op("vector", lambda e, dt=dt, s=s, pb=pb: e.tensor_mul(scr[0][:], vb[dt][:, s * SL:(s + 1) * SL], pb[:]))
                    q.op("vector", lambda e, dt=dt: e.tensor_add(ob[dt][:], ob[dt][:], scr[0][:]), inc=("ev", 1))
                q.wait("tensor", "ev", q.cnt["ev"])
            pb = ps_B[1]
            q.op("tensor", lambda e, dt=dt, t=t, pb=pb: e.matmul(
                pb[:], expHb[:, dt * 128:(dt + 1) * 128], augw[:, t * 512:(t + 1) * 512],
                start=True, stop=True), inc=("mm", 1))
            q.wait("vector", "mm", q.cnt["mm"])
            q.op("vector", lambda e, dt=dt, s=sl, pb=pb: e.tensor_mul(scr[1][:], vb[dt][:, s], pb[:]), inc=("ev", 1))
            q.wait("scalar", "ev", q.cnt["ev"])
            q.op("scalar", lambda e: e.activation(out=scr[2][:], in_=scr[1][:], func=AF.Sigmoid), inc=("ac", 1))
            q.wait("vector", "ac", q.cnt["ac"])
            q.op("vector", lambda e, dt=dt: e.tensor_add(ob[dt][:], ob[dt][:], scr[2][:]))
            q.op("vector", lambda e, dt=dt: e.tensor_copy(out=obb[dt][:], in_=ob[dt][:]), inc=("ev", 1))
            q.wait("tensor", "ev", q.cnt["ev"])
        q.barrier(engs_from=("vector",), engs_to=("tensor", "scalar"))
        if dbg == 5 and t == 1:
            q.barrier()
            for dt in range(NDT):
                q.dma("gpsimd", l1o_o[dt * 128:(dt + 1) * 128, 0:512], ob[dt], store=True)
            q.wait("gpsimd", "stg", q.cnt["stg"])
            q._flush_waits("gpsimd")
            return q.finish()
        for dt in range(NDT):
            q.op("tensor", lambda e, dt=dt, st=(dt == 0), sp=(dt == NDT - 1): e.matmul(
                ps_s[0][:], selVf[:, dt * 16:(dt + 1) * 16], ob[dt], start=st, stop=sp), inc=("mm", 1))
        q.barrier(engs_from=("tensor",), engs_to=("vector",))
        q.op("vector", lambda e: e.tensor_scalar(nrA, ps_s[0][:], -1.0 / DV, None, op0=ALU.mult), inc=("ev", 1))
        q.wait("vector", "ev", q.cnt["ev"])
        q.barrier(engs_from=("vector",), engs_to=("tensor",))
        for dt in range(NDT):
            pb = ps_B[dt % 2]
            if dt >= 2:
                q.wait("tensor", "ev", q.cnt["ev"] - 1)
                q.wait("tensor", "ac", q.cnt["ac"] - 1)
            q.op("tensor", lambda e, dt=dt, pb=pb: e.matmul(
                pb[:], expHf[:, dt * 128:(dt + 1) * 128], nrA, start=True, stop=True), inc=("mm", 1))
            q.wait("vector", "mm", q.cnt["mm"])
            q.op("vector", lambda e, dt=dt, pb=pb: e.tensor_add(ob[dt], ob[dt], pb[:]))
            q.op("vector", lambda e, dt=dt: e.tensor_copy(out=obb[dt], in_=ob[dt]), inc=("ev", 1))
            sq = scrb[dt % 2]
            q.wait("scalar", "ev", q.cnt["ev"])
            if dt >= 2:
                q.wait("scalar", "mm", q.cnt["mm"])
            q.op("scalar", lambda e, dt=dt, sq=sq: e.activation(out=sq[:], in_=obb[dt], func=AF.Square), inc=("ac", 1))
            q.wait("tensor", "ac", q.cnt["ac"])
            q.op("tensor", lambda e, sq=sq, st=(dt == 0), sp=(dt == NDT - 1), dt=dt: e.matmul(
                ps_s[1][:], selVb[:, dt * 16:(dt + 1) * 16], sq[:], start=st, stop=sp), inc=("mm", 1))
        q.barrier(engs_from=("tensor",), engs_to=("vector",))
        q.op("vector", lambda e: e.tensor_scalar(nrB, ps_s[1][:], 1.0 / DV, None, op0=ALU.mult), inc=("ev", 1))
        q.wait("vector", "ev", q.cnt["ev"])
        q.op("vector", lambda e: e.tensor_scalar(nrB, nrB, 1e-5, None, op0=ALU.add), inc=("ev", 1))
        q.barrier(engs_from=("vector",), engs_to=("scalar", "tensor"))
        q.op("scalar", lambda e: e.activation(out=nrB, in_=nrB, func=AF.Sqrt), inc=("ac", 1))
        q.wait("vector", "ac", q.cnt["ac"])
        q.op("vector", lambda e: e.reciprocal(out=nrB, in_=nrB), inc=("ev", 1))
        q.wait("vector", "ev", q.cnt["ev"])
        q.barrier(engs_from=("vector",), engs_to=("tensor",))
        if dbg == 7 and t == 1:
            q.barrier()
            q.dma("gpsimd", l1o_o[0:16, 0:512], nrA, store=True)
            q.dma("gpsimd", l1o_o[16:32, 0:512], nrB, store=True)
            q.wait("gpsimd", "stg", q.cnt["stg"])
            q._flush_waits("gpsimd")
            return q.finish()
        for dt in range(NDT):
            pb = ps_B[dt % 2]
            pa = ps_p[dt % 2]
            if dt >= 2:
                q.wait("tensor", "ev", q.cnt["ev"] - 1)
                q.wait("sync", "ev", q.cnt["ev"] - 1)
            q.dma("sync", gsl[dt % 2][:], gsd[dt * 128:(dt + 1) * 128, sl])
            q.op("tensor", lambda e, dt=dt, pb=pb: e.matmul(
                pb[:], expHf[:, dt * 128:(dt + 1) * 128], nrB, start=True, stop=True), inc=("mm", 1))
            q.wait("vector", "mm", q.cnt["mm"])
            q.wait("vector", "lds", q.cnt["lds"])
            q.op("vector", lambda e, dt=dt, pb=pb: e.tensor_mul(scr[0][:], ob[dt], pb[:]))
            q.op("vector", lambda e, dt=dt: e.tensor_mul(gob[dt], gsl[dt % 2][:], scr[0][:]), inc=("ev", 1))
        q.barrier()
        if dbg == 6 and t == 1:
            for dt in range(NDT):
                q.dma("gpsimd", dbg_o[dt * 128:(dt + 1) * 128, 0:512], gob[dt], store=True)
            q.op("vector", lambda e: e.memset(stg[0], 0.0), inc=("ev", 1))
            q.wait("gpsimd", "ev", q.cnt["ev"])
            for tt in range(NDT):
                for ch in range(4):
                    q.dma("gpsimd", l1o_o[tt * 128:(tt + 1) * 128, ch * 512:(ch + 1) * 512], stg[0], store=True)
            q.wait("gpsimd", "stg", q.cnt["stg"])
            q._flush_waits("gpsimd")
            return q.finish()
        for ot in range(NDT):
            psw = ps_p[ot % 2]
            if ot >= 2:
                q.wait("tensor", "stg", q.cnt["stg"] - 16)
            for kin in range(NDT):
                inc = ("mm", 1) if kin == NDT - 1 else None
                q.op("tensor", lambda e, kin=kin, ot=ot, psw=psw, st=(kin == 0), sp=(kin == NDT - 1):
                     e.matmul(psw[:], wob[kin][:, ot * 128:(ot + 1) * 128], gob[kin], start=st, stop=sp), inc=inc)
            q.wait("vector", "mm", q.cnt["mm"])
            q.op("vector", lambda e, psw=psw, ii=ot % 2: e.tensor_copy(out=stg[ii], in_=psw[:]), inc=("ev", 1))
            q.wait("gpsimd", "ev", q.cnt["ev"])
            q.dma("gpsimd", l1o_o[ot * 128:(ot + 1) * 128, sl], stg[ot % 2], store=True)
        q.barrier()

    q.wait("gpsimd", "stg", q.cnt["stg"])
    q._flush_waits("gpsimd")
    return q.finish()


def _d_consts():
    selH = np.zeros((128, 64), np.float32)
    selV = np.zeros((128, 128), np.float32)
    expH = np.zeros((16, 8 * 128), np.float32)
    for dt in range(4):
        for r4 in range(4):
            h = 4 * dt + r4
            selH[r4 * 32:(r4 + 1) * 32, dt * 16 + h] = 1.0
    for dt in range(8):
        for r2 in range(2):
            h = 2 * dt + r2
            selV[r2 * 64:(r2 + 1) * 64, dt * 16 + h] = 1.0
            expH[h, dt * 128 + r2 * 64: dt * 128 + (r2 + 1) * 64] = 1.0
    return selH, selV, expH


def stage_l1attn(ln2_cm, att):
    selH, selV, expH = _d_consts()
    nc = _prog("D", build_D)
    bgc = np.asarray(att["bg"], np.float32).reshape(NDT, 128).T.copy()
    augc = np.asarray(att["aug_balance"], np.float32).reshape(4, 128).T.copy()
    cwc = np.asarray(att["conv_w"], np.float32).reshape(NDT, 128, 4).transpose(1, 0, 2).reshape(128, 32)
    maps = []
    for c in range(NCORE):
        maps.append({
            "xn": ln2_cm[c], "convw": cwc,
            "wq": np.asarray(att["Wq"], np.float32), "wkg": np.asarray(att["Wkg"], np.float32),
            "wv": np.asarray(att["Wv"], np.float32), "wg": np.asarray(att["Wg"], np.float32),
            "wout": np.asarray(att["Wout"], np.float32),
            "bgc": bgc, "augc": augc, "selH": selH, "selV": selV, "expH": expH,
        })
    r = _run(nc, maps).results
    return [r[c]["l1o"] for c in range(NCORE)]


EXP = 8
HME = 4096
NJ_E = HME // 128
BLK = 1024
NBLK = TL // BLK
NC2 = BLK // 512


def build_E(dbg=0):
    q = Q()
    nc = q.nc
    ln3 = nc.dram_tensor("ln3", [D, TL], BF16, kind="ExternalInput").ap()
    ln3f = nc.dram_tensor("ln3f", [D, TL], F32, kind="ExternalInput").ap()
    gam = nc.dram_tensor("gam", [128, NDT], F32, kind="ExternalInput").ap()
    bet = nc.dram_tensor("bet", [128, NDT], F32, kind="ExternalInput").ap()
    rw = nc.dram_tensor("rw", [128, NDT * EXP], F32, kind="ExternalInput").ap()
    w1r = nc.dram_tensor("w1r", [EXP * (NJ_E // 2) * 128, 2 * D], F32, kind="ExternalInput").ap()
    w2r = nc.dram_tensor("w2r", [EXP * HME, 2 * 512], F32, kind="ExternalInput").ap()
    idf = nc.dram_tensor("idf", [128, 128], F32, kind="ExternalInput").ap()
    onesc = nc.dram_tensor("onesc", [128, 1], F32, kind="ExternalInput").ap()
    onesr = nc.dram_tensor("onesr", [1, 128], F32, kind="ExternalInput").ap()
    ys_o = nc.dram_tensor("ys", [D, TL], F32, kind="ExternalOutput").ap()
    wd_d = nc.dram_tensor("wd_d", [EXP, TL], BF16, kind="Internal").ap()
    dbg_o = nc.dram_tensor("dbgE", [D, TL], BF16, kind="ExternalOutput").ap() if dbg else None

    l3b = [q.sbuf(f"l3{t}", [128, TL], BF16) for t in range(NDT)]
    xnb = [q.sbuf(f"xnb{t}", [128, TL], BF16) for t in range(NDT)]
    hb = [q.sbuf(f"hb{j}", [128, BLK], BF16) for j in range(NJ_E)]
    acc = [q.sbuf(f"acc{t}", [128, BLK], F32) for t in range(NDT)]
    w1t = [q.sbuf(f"w1t{i}", [128, 2 * D], BF16) for i in range(2)]
    w2t = [q.sbuf(f"w2t{i}", [128, 2 * 512], BF16) for i in range(2)]
    rwt = q.sbuf("rwt", [128, EXP * NDT], BF16)
    gamс = q.sbuf("gamc", [128, NDT], F32)
    betc = q.sbuf("betc", [128, NDT], F32)
    idb = q.sbuf("idb", [128, 128], BF16)
    logr = q.sbuf("logr", [8, TL], BF16)
    ltok = [q.sbuf(f"lt{i}", [128, EXP], BF16) for i in range(16)]
    wdr = q.sbuf("wdr", [8, TL], BF16)
    wdBs = [q.sbuf(f"wdB{i}", [128, 512], F32) for i in range(2)]
    wrow = [q.sbuf(f"wrow{i}", [1, 512], BF16) for i in range(2)]
    onesr_b = q.sbuf("onesrb", [1, 128], BF16)
    mcol = q.sbuf("mcol", [128, 1], F32)
    zcol = q.sbuf("zcol", [128, 1], F32)
    srt = q.sbuf("srt", [128, 8], F32)
    prb = q.sbuf("prb", [128, 8], F32)
    msk = q.sbuf("msk", [128, 8], F32)
    wdt = q.sbuf("wdt", [128, 8], BF16)
    tmpf = q.sbuf("tmpE", [128, 512], F32)
    l3s = [q.sbuf(f"l3s{i}", [128, 512], F32) for i in range(2)]
    stg = [q.sbuf(f"stgE{i}", [128, 512], F32) for i in range(2)]
    pp = (q.psum("pse0", [128, 512]), q.psum("pse1", [128, 512]))
    ps_y = [q.psum(f"psy{i}", [128, 512]) for i in range(6)]
    cn = {}
    cn["sqs"] = [q.sbuf(f"lnsq{i}", [128, 512], BF16) for i in range(2)]
    cn["onesc_b"] = q.sbuf("onescb", [128, 1], BF16)
    cn["onesr_f"] = q.sbuf("onesrf", [1, 128], F32)
    cn["ps_st"] = [ps_y[0][0:1, :], ps_y[1][0:1, :]]
    cn["ps_bc"] = [ps_y[2][:], ps_y[3][:]]
    rowA = q.sbuf("rowA", [1, 512], F32)
    rowB = q.sbuf("rowB", [1, 512], F32)
    rowC = q.sbuf("rowC", [1, 512], F32)
    ps_tr = ps_y[4][:].bitcast(BF16)[:, 0:128]
    dbgt = hb[0] if dbg == 3 else None

    for t in range(NDT):
        q.dma("sync", l3b[t][:], ln3[t * 128:(t + 1) * 128, :])
    q.dma("sync", gamс[:], gam[:])
    q.dma("sync", betc[:], bet[:])
    q.dma("gpsimd", rwt[:], rw[:])
    q.dma("gpsimd", idb[:], idf[:])
    q.dma("gpsimd", onesr_b[:], onesr[:])
    load_ln_consts(q, cn, onesc, onesr)
    q.wait_loads("vector", "scalar", "tensor")

    srcs = l3b
    for ch in range(4):
        sl = slice(ch * 512, (ch + 1) * 512)
        for t in range(NDT):
            sq = cn["sqs"][t % 2]
            if t >= 2:
                q.wait("scalar", "mm", q.cnt["mm"] - 1)
            q.op("scalar", lambda e, a=sq, b=srcs[t], s=sl: e.activation(out=a[:], in_=b[:, s], func=AF.Square), inc=("ac", 1))
            q.op("tensor", lambda e, r=srcs[t], s=sl, st=(t == 0), sp=(t == NDT - 1), ps=cn["ps_st"][0]:
                 e.matmul(ps[:], cn["onesc_b"][:], r[:, s], start=st, stop=sp))
            q.wait("tensor", "ac", q.cnt["ac"])
            q.op("tensor", lambda e, r=sq, st=(t == 0), sp=(t == NDT - 1), ps=cn["ps_st"][1]:
                 e.matmul(ps[:], cn["onesc_b"][:], r[:], start=st, stop=sp), inc=("mm", 1))
        q.barrier(engs_from=("tensor",), engs_to=("vector",))
        q.op("vector", lambda e: e.tensor_scalar(rowA[:], cn["ps_st"][0][:], -1.0 / D, None, op0=ALU.mult), inc=("ev", 1))
        q.op("vector", lambda e: e.tensor_scalar(rowB[:], cn["ps_st"][1][:], 1.0 / D, None, op0=ALU.mult), inc=("ev", 1))
        q.wait("vector", "ev", q.cnt["ev"])
        q.op("vector", lambda e: e.tensor_mul(rowC[:], rowA[:], rowA[:]), inc=("ev", 1))
        q.wait("vector", "ev", q.cnt["ev"])
        q.op("vector", lambda e: e.tensor_sub(rowB[:], rowB[:], rowC[:]), inc=("ev", 1))
        q.wait("vector", "ev", q.cnt["ev"])
        q.op("vector", lambda e: e.tensor_scalar(rowB[:], rowB[:], 1e-5, None, op0=ALU.add))
        q.barrier(engs_from=("vector",), engs_to=("scalar",))
        q.op("scalar", lambda e: e.activation(out=rowB[:], in_=rowB[:], func=AF.Sqrt))
        q.barrier(engs_from=("scalar",), engs_to=("vector",))
        q.op("vector", lambda e: e.reciprocal(out=rowB[:], in_=rowB[:]), inc=("ev", 1))
        q.wait("vector", "ev", q.cnt["ev"])
        q.op("vector", lambda e: e.tensor_mul(rowC[:], rowA[:], rowB[:]), inc=("ev", 1))
        q.barrier(engs_from=("vector",), engs_to=("tensor",))
        q.op("tensor", lambda e: e.matmul(cn["ps_bc"][0][:], cn["onesr_f"][:], rowB[:], start=True, stop=True))
        q.op("tensor", lambda e: e.matmul(cn["ps_bc"][1][:], cn["onesr_f"][:], rowC[:], start=True, stop=True), inc=("mm", 1))
        q.wait("vector", "mm", q.cnt["mm"])
        for t in range(NDT):
            q.op("vector", lambda e, a=srcs[t], s=sl: e.tensor_mul(tmpf[:], a[:, s], cn["ps_bc"][0][:]))
            q.op("vector", lambda e: e.tensor_add(tmpf[:], tmpf[:], cn["ps_bc"][1][:]))
            q.op("vector", lambda e, t=t, s=sl: e.tensor_scalar(
                xnb[t][:, s], tmpf[:], gamс[:, t:t + 1], betc[:, t:t + 1], op0=ALU.mult, op1=ALU.add), inc=("ev", 1))
        q.barrier(engs_from=("vector",), engs_to=("tensor",))
    q.barrier()

    for ch in range(4):
        sl = slice(ch * 512, (ch + 1) * 512)
        ps = pp[ch % 2]
        if ch >= 2:
            q.wait("tensor", "ev", q.cnt["ev"] - 1)
        for kin in range(NDT):
            inc = ("mm", 1) if kin == NDT - 1 else None
            q.op("tensor", lambda e, kin=kin, s=sl, ps=ps, st=(kin == 0), sp=(kin == NDT - 1):
                 e.matmul(ps[0:8, :], rwt[:, kin * 8:(kin + 1) * 8], xnb[kin][:, s], start=st, stop=sp), inc=inc)
        q.wait("vector", "mm", q.cnt["mm"])
        q.op("vector", lambda e, s=sl, ps=ps: e.tensor_copy(out=logr[:, s], in_=ps[0:8, :]), inc=("ev", 1))
    q.barrier()
    if dbg == 2:
        q.dma("gpsimd", ys_o[8:16, :], logr[:, :], store=True)
        q.op("vector", lambda e: e.memset(stg[0][:], 0.0), inc=("ev", 1))
        q.wait("gpsimd", "ev", q.cnt["ev"])
        for ot in range(NDT):
            for ch in range(4):
                q.dma("gpsimd", ys_o[ot * 128:(ot + 1) * 128, ch * 512:(ch + 1) * 512], stg[0][:], store=True)
        q.wait("gpsimd", "stg", q.cnt["stg"])
        q._flush_waits("gpsimd")
        return q.finish()
    for i in range(16):
        q.op("tensor", lambda e, i=i: e.transpose(ps_tr[:, 0:8], logr[:, i * 128:(i + 1) * 128], idb[0:8, 0:8]), inc=("mm", 1))
        q.wait("vector", "mm", q.cnt["mm"])
        q.op("vector", lambda e, i=i: e.tensor_copy(out=ltok[i][:], in_=ps_tr[0:128, 0:EXP]), inc=("ev", 1))
        q.wait("tensor", "ev", q.cnt["ev"])
    q.barrier()
    if dbg == 3:
        q.op("vector", lambda e: e.memset(dbgt[:], 0.0), inc=("ev", 1))
    for i in range(16):
        q.op("vector", lambda e, i=i: e.reduce_max(mcol[:], ltok[i][:], axis=AX.X), inc=("ev", 1))
        q.wait("vector", "ev", q.cnt["ev"])
        q.op("vector", lambda e: e.tensor_scalar(mcol[:], mcol[:], -1.0 / 1.5, None, op0=ALU.mult), inc=("ev", 1))
        q.wait("scalar", "ev", q.cnt["ev"])
        q.op("scalar", lambda e, i=i: e.activation(out=prb[:], in_=ltok[i][:], func=AF.Exp,
                                                   scale=1.0 / 1.5, bias=mcol[:], accum_out=zcol[:]), inc=("ac", 1))
        q.wait("vector", "ac", q.cnt["ac"])
        q.op("vector", lambda e: e.reciprocal(out=zcol[:], in_=zcol[:]), inc=("ev", 1))
        q.wait("vector", "ev", q.cnt["ev"])
        q.op("vector", lambda e: e.tensor_scalar(prb[:], prb[:], zcol[:], None, op0=ALU.mult), inc=("ev", 1))
        q.wait("vector", "ev", q.cnt["ev"])
        q.op("vector", lambda e: e.max(out=srt[:], in_=prb[:]), inc=("ev", 1))
        q.wait("vector", "ev", q.cnt["ev"])
        q.op("vector", lambda e: e.tensor_scalar(msk[:], prb[:], srt[:, 1:2], None, op0=ALU.is_ge), inc=("ev", 1))
        q.wait("vector", "ev", q.cnt["ev"])
        q.op("vector", lambda e: e.tensor_mul(wdt[:], prb[:], msk[:]), inc=("ev", 1))
        if dbg == 3:
            q.op("vector", lambda e, i=i: e.tensor_copy(out=dbgt[:, i * 64:i * 64 + 8], in_=prb[:]))
            q.op("vector", lambda e, i=i: e.tensor_copy(out=dbgt[:, i * 64 + 8:i * 64 + 16], in_=srt[:]))
            q.op("vector", lambda e, i=i: e.tensor_copy(out=dbgt[:, i * 64 + 16:i * 64 + 24], in_=msk[:]))
            q.op("vector", lambda e, i=i: e.tensor_copy(out=dbgt[:, i * 64 + 24:i * 64 + 32], in_=ltok[i][:]))
            q.op("vector", lambda e, i=i: e.tensor_copy(out=dbgt[:, i * 64 + 32:i * 64 + 33], in_=mcol[:]))
            q.op("vector", lambda e, i=i: e.tensor_copy(out=dbgt[:, i * 64 + 33:i * 64 + 34], in_=zcol[:]), inc=("ev", 1))
        q.barrier(engs_from=("vector",), engs_to=("tensor",))
        q.op("tensor", lambda e, i=i: e.transpose(ps_tr[0:8, :], wdt[:], idb[:]), inc=("mm", 1))
        q.wait("vector", "mm", q.cnt["mm"])
        q.op("vector", lambda e, i=i: e.tensor_copy(out=wdr[:, i * 128:(i + 1) * 128], in_=ps_tr[0:8, 0:128]), inc=("ev", 1))
        q.wait("tensor", "ev", q.cnt["ev"])
    q.barrier()
    q.dma("gpsimd", wd_d[:, :], wdr[:, :], store=True)
    q.wait("sync", "stg", q.cnt["stg"])
    if dbg == 3:
        q.barrier()
        q.dma("gpsimd", dbg_o[0:128, 0:1024], dbgt[:], store=True)
        q.dma("gpsimd", ys_o[0:8, :], wdr[:, :], store=True)
        q.wait("gpsimd", "stg", q.cnt["stg"])
        q._flush_waits("gpsimd")
        return q.finish()
    if dbg == 1:
        for t in range(NDT):
            q.dma("gpsimd", dbg_o[t * 128:(t + 1) * 128, :], xnb[t][:], store=True)
        q.dma("gpsimd", ys_o[0:8, :], wdr[:, :], store=True)
        q.dma("gpsimd", ys_o[8:16, :], logr[:, :], store=True)
        q.wait("gpsimd", "stg", q.cnt["stg"])
        q._flush_waits("gpsimd")
        return q.finish()

    for blk in range(NBLK):
        bsl = slice(blk * BLK, (blk + 1) * BLK)
        for e_i in range(EXP):
            for c2 in range(NC2):
                ps = pp[c2 % 2]
                wr = wrow[c2 % 2]
                q.dma("sync", wr[:], wd_d[e_i:e_i + 1, blk * BLK + c2 * 512: blk * BLK + (c2 + 1) * 512])
                q.wait("tensor", "lds", q.cnt["lds"])
                q.op("tensor", lambda e, ps=ps, wr=wr: e.matmul(
                    ps[:], onesr_b[:], wr[:], start=True, stop=True), inc=("mm", 1))
                q.wait("vector", "mm", q.cnt["mm"])
                q.op("vector", lambda e, c2=c2, ps=ps: e.tensor_copy(out=wdBs[c2][:], in_=ps[:]), inc=("ev", 1))
            q.barrier(engs_from=("vector",), engs_to=("tensor", "sync"))
            if dbg == 6 and blk == 0:
                q.op("vector", lambda e: e.tensor_copy(out=stg[0][:], in_=pp[0][:]), inc=("ev", 1))
                q.wait("gpsimd", "ev", q.cnt["ev"])
                for c2x in range(NC2):
                    q.dma("gpsimd", dbg_o[e_i * 128:(e_i + 1) * 128, c2x * 512:(c2x + 1) * 512], wdBs[c2x][:], store=True)
                q.dma("gpsimd", dbg_o[e_i * 128:e_i * 128 + 128, 1024:1536], stg[0][:], store=True)
                q.wait("tensor", "stg", q.cnt["stg"])
                q.wait("vector", "stg", q.cnt["stg"])
            gh = 0
            for jp in range(NJ_E // 2):
                wb = w1t[jp % 2]
                if jp >= 2:
                    q.wait("gpsimd", "mm", q.cnt["mm"] - 2 * NC2)
                q.dma("gpsimd", wb[:], w1r[(e_i * (NJ_E // 2) + jp) * 128:(e_i * (NJ_E // 2) + jp + 1) * 128, :])
                ldg_j = q.cnt["ldg"]
                for jhalf in range(2):
                    j = 2 * jp + jhalf
                    for c2 in range(NC2):
                        ps = pp[gh % 2]
                        q.wait("tensor", "ldg", ldg_j)
                        if gh >= 2:
                            q.wait("tensor", "ac", q.cnt["ac"] - 1)
                        for kin in range(NDT):
                            inc = ("mm", 1) if kin == NDT - 1 else None
                            q.op("tensor", lambda e, kin=kin, blk=blk, c2=c2, ps=ps, wb=wb, jh=jhalf,
                                 st=(kin == 0), sp=(kin == NDT - 1):
                                 e.matmul(ps[:], wb[:, jh * D + kin * 128:jh * D + (kin + 1) * 128],
                                          xnb[kin][:, blk * BLK + c2 * 512: blk * BLK + (c2 + 1) * 512],
                                          start=st, stop=sp), inc=inc)
                        q.wait("scalar", "mm", q.cnt["mm"])
                        q.op("scalar", lambda e, j=j, c2=c2, ps=ps: e.activation(
                            out=hb[j][:, c2 * 512:(c2 + 1) * 512], in_=ps[:], func=AF.Silu), inc=("ac", 1))
                        gh += 1
            q.barrier(engs_from=("scalar",), engs_to=("tensor",))
            if dbg == 4 and blk == 0 and e_i == 0:
                q.barrier()
                for j in range(8):
                    q.dma("gpsimd", dbg_o[j * 128:(j + 1) * 128, 0:BLK], hb[j][:], store=True)
                q.wait("gpsimd", "stg", q.cnt["stg"])
                q._flush_waits("gpsimd")
                return q.finish()
            banks = [pp[0], pp[1]] + ps_y
            for oth in range(2):
                for jp in range(NJ_E // 2):
                    wb = w2t[jp % 2]
                    if jp >= 2 or oth == 1 or e_i > 0 or blk > 0:
                        q.wait("gpsimd", "mm", q.cnt["mm"] - 1)
                    q.dma("gpsimd", wb[:], w2r[((e_i * 2 + oth) * (NJ_E // 2) + jp) * 128:
                                               ((e_i * 2 + oth) * (NJ_E // 2) + jp + 1) * 128, :])
                    ldg_j = q.cnt["ldg"]
                    q.wait("tensor", "ldg", ldg_j)
                    for jhalf in range(2):
                        jj = 2 * jp + jhalf
                        for c2 in range(NC2):
                            for ot4 in range(4):
                                inc = ("mm", 1) if (c2 == NC2 - 1 and ot4 == 3 and jhalf == 1) else None
                                q.op("tensor", lambda e, jj=jj, ot4=ot4, c2=c2, wb=wb, jh=jhalf,
                                     st=(jj == 0), sp=(jj == NJ_E - 1):
                                     e.matmul(banks[ot4 * 2 + c2][:],
                                              wb[:, jh * 512 + ot4 * 128:jh * 512 + (ot4 + 1) * 128],
                                              hb[jj][:, c2 * 512:(c2 + 1) * 512],
                                              start=st, stop=sp), inc=inc)
                q.wait("vector", "mm", q.cnt["mm"])
                for ot4 in range(4):
                    for c2 in range(NC2):
                        ot = oth * 4 + ot4
                        psx = banks[ot4 * 2 + c2]
                        if e_i == 0:
                            q.op("vector", lambda e, ot=ot, c2=c2, psx=psx: e.tensor_mul(
                                acc[ot][:, c2 * 512:(c2 + 1) * 512], psx[:], wdBs[c2][:]), inc=("ev", 1))
                        else:
                            q.op("vector", lambda e, ot=ot, c2=c2, psx=psx: e.tensor_mul(tmpf[:], psx[:], wdBs[c2][:]))
                            q.op("vector", lambda e, ot=ot, c2=c2: e.tensor_add(
                                acc[ot][:, c2 * 512:(c2 + 1) * 512], acc[ot][:, c2 * 512:(c2 + 1) * 512], tmpf[:]), inc=("ev", 1))
                q.barrier(engs_from=("vector",), engs_to=("tensor", "gpsimd"))
        if dbg in (5, 6) and blk == 0:
            q.barrier()
            for t in range(NDT):
                q.dma("gpsimd", dbg_o[t * 128:(t + 1) * 128, 0:BLK], acc[t][:], store=True)
            q.wait("gpsimd", "stg", q.cnt["stg"])
            q._flush_waits("gpsimd")
            return q.finish()
        stg0 = q.cnt["stg"]
        for ot in range(NDT):
            for c2 in range(NC2):
                gi = ot * NC2 + c2
                ii = gi % 2
                if gi >= 2:
                    q.wait("sync", "ev", q.cnt["ev"] - 1)
                q.dma("sync", l3s[ii][:], ln3f[ot * 128:(ot + 1) * 128, blk * BLK + c2 * 512: blk * BLK + (c2 + 1) * 512])
                q.wait("vector", "lds", q.cnt["lds"])
                if gi >= 2:
                    q.wait("vector", "stg", stg0 + 16 * (gi - 1))
                q.op("vector", lambda e, ot=ot, c2=c2, ii=ii, blk=blk: e.tensor_add(
                    stg[ii][:], acc[ot][:, c2 * 512:(c2 + 1) * 512], l3s[ii][:]), inc=("ev", 1))
                q.wait("gpsimd", "ev", q.cnt["ev"])
                q.dma("gpsimd", ys_o[ot * 128:(ot + 1) * 128, blk * BLK + c2 * 512: blk * BLK + (c2 + 1) * 512],
                      stg[ii][:], store=True)
        q.barrier()
    q.wait("gpsimd", "stg", q.cnt["stg"])
    q._flush_waits("gpsimd")
    return q.finish()


def stage_l1ffn(ln3b_cm, ln3f_cm, x3_cm, moe):
    W1 = np.asarray(moe["W1"], np.float32)
    w1r = np.empty((EXP * (NJ_E // 2) * 128, 2 * D), np.float32)
    for e_i in range(EXP):
        for jp in range(NJ_E // 2):
            r0 = (e_i * (NJ_E // 2) + jp) * 128
            for jh in range(2):
                j = 2 * jp + jh
                w1r[r0:r0 + 128, jh * D:(jh + 1) * D] = (
                    W1[e_i][:, j * 128:(j + 1) * 128].reshape(NDT, 128, 128).transpose(1, 0, 2).reshape(128, D))
    W2 = np.asarray(moe["W2"], np.float32)
    w2r = np.empty((EXP * HME, 2 * 512), np.float32)
    for e_i in range(EXP):
        for oth in range(2):
            for jp in range(NJ_E // 2):
                r0 = ((e_i * 2 + oth) * (NJ_E // 2) + jp) * 128
                w2r[r0:r0 + 128, 0:512] = W2[e_i][2 * jp * 128:(2 * jp + 1) * 128, oth * 512:(oth + 1) * 512]
                w2r[r0:r0 + 128, 512:1024] = W2[e_i][(2 * jp + 1) * 128:(2 * jp + 2) * 128, oth * 512:(oth + 1) * 512]
    gam = np.asarray(moe["gamma"], np.float32).reshape(NDT, 128).T.copy()
    bet = np.asarray(moe["beta"], np.float32).reshape(NDT, 128).T.copy()
    rwT = np.ascontiguousarray(np.asarray(moe["router_w"], np.float32).T)
    rwc = rwT.reshape(NDT, 128, EXP).transpose(1, 0, 2).reshape(128, NDT * EXP)
    nc = _prog("E", build_E)
    maps = []
    for c in range(NCORE):
        maps.append({
            "ln3": ln3b_cm[c], "ln3f": ln3f_cm[c], "gam": gam, "bet": bet,
            "rw": rwc, "w1r": w1r, "w2r": w2r, "idf": np.eye(128, dtype=np.float32),
            "onesc": _ONESC, "onesr": _ONESR,
        })
    r = _run(nc, maps).results
    ys = [r[c]["ys"] for c in range(NCORE)]
    return stage_sandwich(ys, x3_cm, mode="x")


# version 09 (final)
# speedup vs baseline: 1.3375x; 1.1704x over previous
import numpy as np
import ml_dtypes
import concourse.bass as bass
import concourse.mybir as mybir
from concourse.bass_utils import run_bass_kernel_spmd

F32 = mybir.dt.float32
BF16 = mybir.dt.bfloat16
AF = mybir.ActivationFunctionType
ALU = mybir.AluOpType
AX = mybir.AxisListType
BF = ml_dtypes.bfloat16

NCORE = 8
B = 4
S = 4096
D = 1024
SL = S // NCORE
TL = B * SL
NDT = D // 128

ENGS = ("sync", "gpsimd", "vector", "scalar", "tensor")


def _nc():
    return bass.Bass("TRN2", target_bir_lowering=False, debug=False)


class Q:

    def __init__(self):
        self.nc = _nc()
        self.cells = {e: [] for e in ENGS}
        self.sems = {}
        self.cnt = {}
        self.pending = {e: {} for e in ENGS}
        self.covered = {e: None for e in ENGS}
        for name in ("lds", "ldg", "sts", "stg", "mm", "ev", "ac",
                     "Dsync", "Dgpsimd", "Dvector", "Dscalar", "Dtensor"):
            self.sems[name] = self.nc.alloc_semaphore(name)
            self.cnt[name] = 0

    def sbuf(self, name, shape, dtype):
        return self.nc.alloc_sbuf_tensor(name, shape, dtype)

    def psum(self, name, shape, dtype=F32):
        return self.nc.alloc_psum_tensor(name, shape, dtype)

    def _flush_waits(self, eng):
        pend = self.pending[eng]
        if pend:
            items = [(self.sems[s], v) for s, v in pend.items()]
            self.cells[eng].append([lambda e, it=tuple(items): [e.wait_ge(h, v) for h, v in it], None, None, "wait"])
            pend.clear()

    def op(self, eng, fn, inc=None):
        self._flush_waits(eng)
        cell = [fn, None, None, "op"]
        if inc is not None:
            sem, k = inc
            self.cnt[sem] += k
            cell[1] = (sem, k)
            cell[2] = self.cnt[sem]
            self.covered[eng] = (sem, self.cnt[sem])
        else:
            self.covered[eng] = None
        self.cells[eng].append(cell)
        return cell[2]

    def wait(self, eng, sem, val):
        if val > 0:
            self.pending[eng][sem] = max(self.pending[eng].get(sem, 0), val)

    def dma(self, eng, out, in_, store=False):
        sem = ("st" if store else "ld") + ("g" if eng == "gpsimd" else "s")
        return self.op(eng, lambda e, o=out, i=in_: e.dma_start(out=o, in_=i), inc=(sem, 16))

    def wait_loads(self, *engs):
        for e in engs:
            self.wait(e, "lds", self.cnt["lds"])
            self.wait(e, "ldg", self.cnt["ldg"])

    def _cover(self, eng):
        cells = self.cells[eng]
        last = None
        for c in reversed(cells):
            if c[3] == "op":
                last = c
                break
        if last is None:
            return None
        if last[1] is None:
            ds = "D" + eng
            self.cnt[ds] += 1
            last[1] = (ds, 1)
            last[2] = self.cnt[ds]
            self.covered[eng] = (ds, self.cnt[ds])
        if self.covered[eng] is None:
            self.covered[eng] = (last[1][0], last[2])
        return self.covered[eng]

    def barrier(self, engs_from=None, engs_to=None):
        engs_from = engs_from or ENGS
        engs_to = engs_to or ENGS
        covers = []
        for e in engs_from:
            c = self._cover(e)
            if c is not None:
                covers.append((e, c))
        for et in engs_to:
            for e, (sem, val) in covers:
                if e == et:
                    continue
                self.wait(et, sem, val)

    def finish(self):
        for e in ENGS:
            self._flush_waits(e)
        nc = self.nc
        with nc.Block() as block:
            for e in ENGS:
                cells = self.cells[e]
                if not cells:
                    continue

                def body(eng, cells=cells):
                    for fn, inc, _, _k in cells:
                        ins = fn(eng)
                        if inc is not None:
                            ins.then_inc(self.sems[inc[0]], inc[1])

                getattr(block, e)(body)
        return nc



def emit_ln_finalize(q, cn):
    q.op("vector", lambda e: e.tensor_scalar(cn["rA"][:], cn["rA"][:], -1.0 / D, None, op0=ALU.mult))
    q.op("vector", lambda e: e.tensor_scalar(cn["rB"][:], cn["rB"][:], 1.0 / D, None, op0=ALU.mult), inc=("ev", 1))
    q.wait("vector", "ev", q.cnt["ev"])
    q.op("vector", lambda e: e.tensor_mul(cn["rC"][:], cn["rA"][:], cn["rA"][:]), inc=("ev", 1))
    q.wait("vector", "ev", q.cnt["ev"])
    q.op("vector", lambda e: e.tensor_sub(cn["rB"][:], cn["rB"][:], cn["rC"][:]), inc=("ev", 1))
    q.wait("vector", "ev", q.cnt["ev"])
    q.op("vector", lambda e: e.tensor_scalar(cn["rB"][:], cn["rB"][:], 1e-5, None, op0=ALU.add))
    q.barrier(engs_from=("vector",), engs_to=("scalar",))
    q.op("scalar", lambda e: e.activation(out=cn["rB"][:], in_=cn["rB"][:], func=AF.Sqrt))
    q.barrier(engs_from=("scalar",), engs_to=("vector",))
    q.op("vector", lambda e: e.reciprocal(out=cn["rB"][:], in_=cn["rB"][:]), inc=("ev", 1))
    q.wait("vector", "ev", q.cnt["ev"])
    q.op("vector", lambda e: e.tensor_mul(cn["rC"][:], cn["rA"][:], cn["rB"][:]))
    q.barrier(engs_from=("vector",), engs_to=("tensor",))


def emit_ln_cm(q, xt, xb, cn, T, mutate_ok=True):
    nch = T // 512
    for t in range(NDT):
        q.op("vector", lambda e, a=xb[t], b=xt[t]: e.tensor_copy(out=a[:], in_=b[:]))
    q.barrier(engs_from=("vector",), engs_to=("scalar", "tensor"))
    for ch in range(nch):
        sl = slice(ch * 512, (ch + 1) * 512)
        for t in range(NDT):
            sq = cn["sqs"][t % 2]
            if ch * NDT + t >= 2:
                q.wait("scalar", "mm", q.cnt["mm"] - 1)
            q.op("scalar", lambda e, a=sq, b=xb[t], s=sl: e.activation(out=a[:], in_=b[:, s], func=AF.Square), inc=("ac", 1))
            q.op("tensor", lambda e, r=xb[t], s=sl, st=(t == 0), sp=(t == NDT - 1), ps=cn["ps_st"][0]:
                 e.matmul(ps[:], cn["onesc_b"][:], r[:, s], start=st, stop=sp))
            q.wait("tensor", "ac", q.cnt["ac"])
            q.op("tensor", lambda e, r=sq, st=(t == 0), sp=(t == NDT - 1), ps=cn["ps_st"][1]:
                 e.matmul(ps[:], cn["onesc_b"][:], r[:], start=st, stop=sp), inc=("mm", 1))
        q.barrier(engs_from=("tensor",), engs_to=("vector",))
        q.op("vector", lambda e, s=sl: e.tensor_copy(out=cn["ssum"][:, s], in_=cn["ps_st"][0][:]))
        q.op("vector", lambda e, s=sl: e.tensor_copy(out=cn["sssq"][:, s], in_=cn["ps_st"][1][:]), inc=("ev", 1))
        q.barrier(engs_from=("vector",), engs_to=("tensor",))
    emit_ln_finalize(q, cn)
    for ch in range(nch):
        sl = slice(ch * 512, (ch + 1) * 512)
        q.op("tensor", lambda e, s=sl: e.matmul(cn["ps_bc"][0][:], cn["onesr_f"][:], cn["rstd"][:, s], start=True, stop=True))
        q.op("tensor", lambda e, s=sl: e.matmul(cn["ps_bc"][1][:], cn["onesr_f"][:], cn["s2"][:, s], start=True, stop=True), inc=("mm", 1))
        q.barrier(engs_from=("tensor",), engs_to=("vector",))
        q.op("vector", lambda e, s=sl: e.tensor_copy(out=cn["bcs"][0][:, s], in_=cn["ps_bc"][0][:]))
        q.op("vector", lambda e, s=sl: e.tensor_copy(out=cn["bcs"][1][:, s], in_=cn["ps_bc"][1][:]), inc=("ev", 1))
        q.barrier(engs_from=("vector",), engs_to=("tensor",))
    for t in range(NDT):
        q.op("vector", lambda e, a=xt[t]: e.tensor_mul(a[:], a[:], cn["bcs"][0][:, 0:T]))
        q.op("vector", lambda e, a=xt[t]: e.tensor_add(a[:], a[:], cn["bcs"][1][:, 0:T]))
        q.op("vector", lambda e, a=xt[t], o=xb[t]: e.tensor_copy(out=o[:], in_=a[:]))


def alloc_ln_common(q, T, ps_pool=None):
    cn = {}
    cn["sqs"] = [q.sbuf(f"lnsq{i}", [128, 512], BF16) for i in range(2)]
    for n in ("rA", "rB", "rC"):
        cn[n] = q.sbuf("ln_" + n, [1, T], F32)
    cn["ssum"] = cn["mun"] = cn["rA"]
    cn["sssq"] = cn["m2"] = cn["var"] = cn["rstd"] = cn["rB"]
    cn["s2"] = cn["rC"]
    cn["bcs"] = (q.sbuf("ln_bc0", [128, T], F32), q.sbuf("ln_bc1", [128, T], F32))
    cn["onesc_b"] = q.sbuf("onescb", [128, 1], BF16)
    cn["onesr_f"] = q.sbuf("onesrf", [1, 128], F32)
    if ps_pool is None:
        cn["ps_st"] = [q.psum(f"ps_st{i}", [1, 512]) for i in range(2)]
        cn["ps_bc"] = [q.psum(f"ps_bc{i}", [128, 512]) for i in range(2)]
    else:
        cn["ps_st"] = [ps_pool[0][0:1, :], ps_pool[1][0:1, :]]
        cn["ps_bc"] = [ps_pool[2][:], ps_pool[3][:]]
    return cn


def load_ln_consts(q, cn, onesc_ap, onesr_ap):
    q.dma("gpsimd", cn["onesc_b"][:], onesc_ap[:])
    q.dma("gpsimd", cn["onesr_f"][:], onesr_ap[:])



def emit_proj(q, wtiles, rhs_tiles, pp, evac, n_out_tiles=NDT, T=TL, kin_tiles=None):
    kt = kin_tiles if kin_tiles is not None else len(rhs_tiles)
    nch = T // 512
    hist = []
    q.barrier(engs_from=("vector", "scalar"), engs_to=("tensor",))
    for ot in range(n_out_tiles):
        for ch in range(nch):
            gi = ot * nch + ch
            sl = slice(ch * 512, (ch + 1) * 512)
            ps = pp[gi % 2]
            if gi >= 2:
                for sem, val in hist[gi - 2]:
                    q.wait("tensor", sem, val)
            for kin in range(kt):
                inc = ("mm", 1) if kin == kt - 1 else None
                q.op("tensor", lambda e, w=wtiles[kin], o=ot, s=sl, r=rhs_tiles[kin], ps=ps,
                     st=(kin == 0), sp=(kin == kt - 1):
                     e.matmul(ps[:], w[:, o * 128:(o + 1) * 128], r[:, s], start=st, stop=sp), inc=inc)
            covers = evac(ot, sl, ps, q.cnt["mm"])
            hist.append(covers)



def build_A1():
    q = Q()
    nc = q.nc
    x = nc.dram_tensor("x", [D, TL], F32, kind="ExternalInput").ap()
    wk = nc.dram_tensor("wk", [D, D], F32, kind="ExternalInput").ap()
    wv = nc.dram_tensor("wv", [D, D], F32, kind="ExternalInput").ap()
    wr = nc.dram_tensor("wr", [D, D], F32, kind="ExternalInput").ap()
    onesc = nc.dram_tensor("onesc", [128, 1], F32, kind="ExternalInput").ap()
    onesr = nc.dram_tensor("onesr", [1, 128], F32, kind="ExternalInput").ap()
    kv_o = nc.dram_tensor("kv", [D, TL], F32, kind="ExternalOutput").ap()
    r_o = nc.dram_tensor("r", [D, TL], BF16, kind="ExternalOutput").ap()

    xt = [q.sbuf(f"xt{t}", [128, TL], F32) for t in range(NDT)]
    xb = [q.sbuf(f"xb{t}", [128, TL], BF16) for t in range(NDT)]
    rt = [q.sbuf(f"rt{t}", [128, TL], BF16) for t in range(NDT)]
    wA = [q.sbuf(f"wA{t}", [128, D], BF16) for t in range(NDT)]
    wB = [q.sbuf(f"wB{t}", [128, D], BF16) for t in range(NDT)]
    kvt = xt
    cn = alloc_ln_common(q, TL)
    pp = (q.psum("ps_p0", [128, 512]), q.psum("ps_p1", [128, 512]))

    for t in range(NDT):
        q.dma("sync", xt[t][:], x[t * 128:(t + 1) * 128, :])
    for t in range(NDT):
        q.dma("gpsimd", wA[t][:], wk[t * 128:(t + 1) * 128, :])
    for t in range(NDT):
        q.dma("gpsimd", wB[t][:], wv[t * 128:(t + 1) * 128, :])
    load_ln_consts(q, cn, onesc, onesr)
    q.wait_loads("vector", "scalar", "tensor")

    emit_ln_cm(q, xt, xb, cn, TL)
    q.barrier(engs_from=("vector",), engs_to=("tensor", "scalar"))

    def evac_k(ot, sl, ps, mmv):
        q.wait("vector", "mm", mmv)
        q.op("vector", lambda e, o=ot, s=sl, ps=ps: e.tensor_copy(out=kvt[o][:, s], in_=ps[:]), inc=("ev", 1))
        return [("ev", q.cnt["ev"])]

    emit_proj(q, wA, xb, pp, evac_k)
    mm_after_k = q.cnt["mm"]
    q.wait("gpsimd", "mm", mm_after_k)
    for t in range(NDT):
        q.dma("gpsimd", wA[t][:], wr[t * 128:(t + 1) * 128, :])
    ld_r = q.cnt["ldg"]

    def evac_v(ot, sl, ps, mmv):
        q.wait("vector", "mm", mmv)
        q.op("vector", lambda e, o=ot, s=sl, ps=ps: e.tensor_mul(kvt[o][:, s], kvt[o][:, s], ps[:]), inc=("ev", 1))
        return [("ev", q.cnt["ev"])]

    emit_proj(q, wB, xb, pp, evac_v)
    q.wait("tensor", "ldg", ld_r)

    def evac_r(ot, sl, ps, mmv):
        q.wait("scalar", "mm", mmv)
        q.op("scalar", lambda e, o=ot, s=sl, ps=ps: e.activation(out=rt[o][:, s], in_=ps[:], func=AF.Sigmoid), inc=("ac", 1))
        return [("ac", q.cnt["ac"])]

    emit_proj(q, wA, xb, pp, evac_r)
    q.barrier(engs_from=("vector", "scalar"), engs_to=("gpsimd", "sync"))
    for t in range(NDT):
        q.dma("gpsimd", kv_o[t * 128:(t + 1) * 128, :], kvt[t][:], store=True)
        q.dma("sync", r_o[t * 128:(t + 1) * 128, :], rt[t][:], store=True)
    q.wait("gpsimd", "stg", q.cnt["stg"])
    q.wait("gpsimd", "sts", q.cnt["sts"])
    q._flush_waits("gpsimd")
    return q.finish()



def build_A2():
    q = Q()
    nc = q.nc
    kv = nc.dram_tensor("kv", [128, B * S], F32, kind="ExternalInput").ap()
    r = nc.dram_tensor("r", [128, B * S], BF16, kind="ExternalInput").ap()
    td = nc.dram_tensor("td", [128, 1], F32, kind="ExternalInput").ap()
    tf = nc.dram_tensor("tf", [128, 1], F32, kind="ExternalInput").ap()
    out = nc.dram_tensor("out", [128, B * S], BF16, kind="ExternalOutput").ap()

    kvb = [q.sbuf(f"kv{i}", [128, S], F32) for i in range(2)]
    rb = [q.sbuf(f"r{i}", [128, S], BF16) for i in range(2)]
    sb = q.sbuf("s", [128, S], F32)
    eb = q.sbuf("e", [128, S], F32)
    ob = [q.sbuf(f"o{i}", [128, S], BF16) for i in range(2)]
    wcol = q.sbuf("wcol", [128, 1], F32)
    dcol = q.sbuf("dcol", [128, 1], F32)
    ucol = q.sbuf("ucol", [128, 1], F32)
    tdc = q.sbuf("tdc", [128, 1], F32)
    tfc = q.sbuf("tfc", [128, 1], F32)

    q.dma("sync", tdc[:], td[:])
    q.dma("sync", tfc[:], tf[:])
    q.wait("scalar", "lds", 32)
    q.op("scalar", lambda e: e.activation(out=wcol[:], in_=tdc[:], func=AF.Exp))
    q.barrier(engs_from=("scalar",), engs_to=("vector",))
    q.op("vector", lambda e: e.tensor_scalar(wcol[:], wcol[:], -1.0, None, op0=ALU.mult))
    q.barrier(engs_from=("vector",), engs_to=("scalar",))
    q.op("scalar", lambda e: e.activation(out=dcol[:], in_=wcol[:], func=AF.Exp))
    q.op("scalar", lambda e: e.activation(out=ucol[:], in_=tfc[:], func=AF.Exp))
    q.barrier(engs_from=("scalar",), engs_to=("vector",))

    evdone = []
    for b in range(B):
        i = b % 2
        if b >= 2:
            q.wait("sync", "ev", evdone[b - 2])
        q.dma("sync", kvb[i][:], kv[:, b * S:(b + 1) * S])
        q.dma("sync", rb[i][:], r[:, b * S:(b + 1) * S])
        q.wait("vector", "lds", 32 + 32 * (b + 1))
        if b >= 2:
            q.wait("vector", "stg", 16 * (b - 1))
        q.op("vector", lambda e, k=kvb[i]: e.tensor_tensor_scan(
            out=sb[:], data0=dcol[:].to_broadcast([128, S]), data1=k[:],
            initial=0.0, op0=ALU.mult, op1=ALU.add))
        q.op("vector", lambda e, k=kvb[i]: e.tensor_scalar(k[:], k[:], ucol[:], None, op0=ALU.mult), inc=("ev", 1))
        q.wait("vector", "ev", q.cnt["ev"])
        q.op("vector", lambda e, k=kvb[i]: e.tensor_copy(out=eb[:, 0:1], in_=k[:, 0:1]))
        q.op("vector", lambda e, k=kvb[i]: e.tensor_add(eb[:, 1:S], sb[:, 0:S - 1], k[:, 1:S]))
        q.op("vector", lambda e, rr=rb[i], o=ob[i]: e.tensor_mul(o[:], rr[:], eb[:]), inc=("ev", 1))
        evdone.append(q.cnt["ev"])
        q.wait("gpsimd", "ev", q.cnt["ev"])
        q.dma("gpsimd", out[:, b * S:(b + 1) * S], ob[i][:], store=True)
    q.wait("gpsimd", "stg", q.cnt["stg"])
    q._flush_waits("gpsimd")
    return q.finish()



def build_A3(dbg=0):
    q = Q()
    nc = q.nc
    rw = nc.dram_tensor("rw", [D, TL], BF16, kind="ExternalInput").ap()
    xr = nc.dram_tensor("xr", [D, TL], F32, kind="ExternalInput").ap()
    wo = nc.dram_tensor("wo", [D, D], F32, kind="ExternalInput").ap()
    onesc = nc.dram_tensor("onesc", [128, 1], F32, kind="ExternalInput").ap()
    onesr = nc.dram_tensor("onesr", [1, 128], F32, kind="ExternalInput").ap()
    x1_o = nc.dram_tensor("x1", [D, TL], F32, kind="ExternalOutput").ap()
    ln1_o = nc.dram_tensor("ln1", [D, TL], BF16, kind="ExternalOutput").ap()

    rwt = [q.sbuf(f"rw{t}", [128, TL], BF16) for t in range(NDT)]
    xt = [q.sbuf(f"xs{t}", [128, TL], F32) for t in range(NDT)]
    ab = [q.sbuf(f"ab{t}", [128, TL], BF16) for t in range(NDT)]
    wsb = [q.sbuf(f"wo{t}", [128, D], BF16) for t in range(NDT)]
    tmp = q.sbuf("tmp", [128, TL], F32)
    cn = alloc_ln_common(q, TL)
    pp = (q.psum("ps_p0", [128, 512]), q.psum("ps_p1", [128, 512]))

    for t in range(NDT):
        q.dma("sync", rwt[t][:], rw[t * 128:(t + 1) * 128, :])
        q.dma("sync", xt[t][:], xr[t * 128:(t + 1) * 128, :])
    for t in range(NDT):
        q.dma("gpsimd", wsb[t][:], wo[t * 128:(t + 1) * 128, :])
    load_ln_consts(q, cn, onesc, onesr)
    q.wait_loads("vector", "scalar", "tensor")

    if dbg == 4:
        q.wait_loads("gpsimd")
        q.barrier()
        for t in range(NDT):
            q.dma("gpsimd", ln1_o[t * 128:(t + 1) * 128, :], rwt[t][:], store=True)
            q.dma("sync", x1_o[t * 128:(t + 1) * 128, :], xt[t][:], store=True)
        q.wait("gpsimd", "stg", q.cnt["stg"])
        q.wait("gpsimd", "sts", q.cnt["sts"])
        q._flush_waits("gpsimd")
        return q.finish()
    if dbg == 5:
        def evac5(ot, sl, ps, mmv):
            q.wait("vector", "mm", mmv)
            q.op("vector", lambda e, o=ot, s=sl, ps=ps: e.tensor_copy(out=ab[o][:, s], in_=ps[:]), inc=("ev", 1))
            return [("ev", q.cnt["ev"])]
        emit_proj(q, wsb, rwt, pp, evac5)
        q.barrier()
        for t in range(NDT):
            q.dma("gpsimd", ln1_o[t * 128:(t + 1) * 128, :], ab[t][:], store=True)
            q.dma("sync", x1_o[t * 128:(t + 1) * 128, :], xt[t][:], store=True)
        q.wait("gpsimd", "stg", q.cnt["stg"])
        q.wait("gpsimd", "sts", q.cnt["sts"])
        q._flush_waits("gpsimd")
        return q.finish()
    evdone = []
    mmstat = []
    for ch in range(4):
        sl = slice(ch * 512, (ch + 1) * 512)
        for ot in range(NDT):
            gi = ch * NDT + ot
            ps = pp[gi % 2]
            if gi >= 2:
                for sem, val in evdone[gi - 2]:
                    q.wait("tensor", sem, val)
            for kin in range(NDT):
                inc = ("mm", 1) if kin == NDT - 1 else None
                q.op("tensor", lambda e, w=wsb[kin], o=ot, s=sl, rr=rwt[kin], ps=ps,
                     st=(kin == 0), sp=(kin == NDT - 1):
                     e.matmul(ps[:], w[:, o * 128:(o + 1) * 128], rr[:, s], start=st, stop=sp), inc=inc)
            mmv = q.cnt["mm"]
            q.wait("vector", "mm", mmv)
            q.wait("scalar", "mm", mmv)
            q.op("vector", lambda e, o=ot, s=sl, ps=ps: e.tensor_copy(out=ab[o][:, s], in_=ps[:]), inc=("ev", 1))
            sq = cn["sqs"][gi % 2]
            if gi >= 2:
                q.wait("scalar", "mm", mmstat[gi - 2])
            q.wait("scalar", "ev", q.cnt["ev"])
            q.op("scalar", lambda e, o=ot, s=sl, a=sq: e.activation(out=a[:], in_=ab[o][:, s], func=AF.Square), inc=("ac", 1))
            evdone.append([("ev", q.cnt["ev"]), ("ac", q.cnt["ac"])])
            if dbg != 3:
                q.wait("tensor", "ev", q.cnt["ev"])
                q.wait("tensor", "ac", q.cnt["ac"])
                q.op("tensor", lambda e, o=ot, s=sl, st=(ot == 0), sp=(ot == NDT - 1), ps=cn["ps_st"][0], a=ab[ot]:
                     e.matmul(ps[:], cn["onesc_b"][:], a[:, s], start=st, stop=sp))
                q.op("tensor", lambda e, st=(ot == 0), sp=(ot == NDT - 1), ps=cn["ps_st"][1], a=sq:
                     e.matmul(ps[:], cn["onesc_b"][:], a[:], start=st, stop=sp), inc=("mm", 1))
            mmstat.append(q.cnt["mm"])
        q.barrier(engs_from=("tensor",), engs_to=("vector",))
        q.op("vector", lambda e, s=sl: e.tensor_copy(out=cn["ssum"][:, s], in_=cn["ps_st"][0][:]))
        q.op("vector", lambda e, s=sl: e.tensor_copy(out=cn["sssq"][:, s], in_=cn["ps_st"][1][:]), inc=("ev", 1))
        q.barrier(engs_from=("vector",), engs_to=("tensor",))
    if dbg in (1, 3):
        q.barrier()
        for t in range(NDT):
            q.dma("gpsimd", ln1_o[t * 128:(t + 1) * 128, :], ab[t][:], store=True)
            q.dma("sync", x1_o[t * 128:(t + 1) * 128, :], xt[t][:], store=True)
        q.wait("gpsimd", "stg", q.cnt["stg"])
        q.wait("gpsimd", "sts", q.cnt["sts"])
        q._flush_waits("gpsimd")
        return q.finish()
    emit_ln_finalize(q, cn)
    for ch in range(4):
        sl = slice(ch * 512, (ch + 1) * 512)
        q.op("tensor", lambda e, s=sl: e.matmul(cn["ps_bc"][0][:], cn["onesr_f"][:], cn["rstd"][:, s], start=True, stop=True))
        q.op("tensor", lambda e, s=sl: e.matmul(cn["ps_bc"][1][:], cn["onesr_f"][:], cn["s2"][:, s], start=True, stop=True), inc=("mm", 1))
        q.barrier(engs_from=("tensor",), engs_to=("vector",))
        q.op("vector", lambda e, s=sl: e.tensor_copy(out=cn["bcs"][0][:, s], in_=cn["ps_bc"][0][:]))
        q.op("vector", lambda e, s=sl: e.tensor_copy(out=cn["bcs"][1][:, s], in_=cn["ps_bc"][1][:]), inc=("ev", 1))
        q.barrier(engs_from=("vector",), engs_to=("tensor",))
    for t in range(NDT):
        q.op("vector", lambda e, x=xt[t]: e.tensor_add(x[:], x[:], cn["bcs"][1][:]))
        q.op("vector", lambda e, a=ab[t]: e.tensor_mul(tmp[:], a[:], cn["bcs"][0][:]))
        q.op("vector", lambda e, x=xt[t]: e.tensor_add(x[:], x[:], tmp[:]), inc=("ev", 1))
        q.wait("sync", "ev", q.cnt["ev"])
        q.dma("sync", x1_o[t * 128:(t + 1) * 128, :], xt[t][:], store=True)
    if dbg == 2:
        q.barrier()
        for t in range(NDT):
            q.dma("gpsimd", ln1_o[t * 128:(t + 1) * 128, :], ab[t][:], store=True)
        q.wait("gpsimd", "stg", q.cnt["stg"])
        q.wait("gpsimd", "sts", q.cnt["sts"])
        q._flush_waits("gpsimd")
        return q.finish()
    q.wait("vector", "sts", q.cnt["sts"])
    emit_ln_cm(q, xt, ab, cn, TL)
    q.barrier(engs_from=("vector",), engs_to=("gpsimd",))
    for t in range(NDT):
        q.dma("gpsimd", ln1_o[t * 128:(t + 1) * 128, :], ab[t][:], store=True)
    q.wait("gpsimd", "stg", q.cnt["stg"])
    q.wait("gpsimd", "sts", q.cnt["sts"])
    q._flush_waits("gpsimd")
    return q.finish()



def _cm(x_bsd, c):
    sl = x_bsd[:, c * SL:(c + 1) * SL, :]
    return np.ascontiguousarray(sl.transpose(2, 0, 1).reshape(D, TL))


_ONESC = np.ones((128, 1), np.float32)
_ONESR = np.ones((1, 128), np.float32)


import os
_TRACE = bool(os.environ.get("KTRACE"))
_TIMES = []


def _run(nc, in_maps):
    r = run_bass_kernel_spmd(nc, in_maps, core_ids=list(range(NCORE)), trace=_TRACE)
    if _TRACE:
        _TIMES.append(r.exec_time_ns)
    return r


def stage_l0attn(x_cm, Wk, Wv, Wr, Wo, td, tf):
    wkT = np.ascontiguousarray(Wk.T)
    wvT = np.ascontiguousarray(Wv.T)
    wrT = np.ascontiguousarray(Wr.T)
    woT = np.ascontiguousarray(Wo.T)

    nc1 = _prog("A1", build_A1)
    maps = [{"x": x_cm[c], "wk": wkT, "wv": wvT, "wr": wrT, "onesc": _ONESC, "onesr": _ONESR}
            for c in range(NCORE)]
    r1 = _run(nc1, maps).results

    kv_full = np.empty((D, B, S), np.float32)
    r_full = np.empty((D, B, S), BF)
    for c in range(NCORE):
        kv_full[:, :, c * SL:(c + 1) * SL] = r1[c]["kv"].reshape(D, B, SL)
        r_full[:, :, c * SL:(c + 1) * SL] = r1[c]["r"].reshape(D, B, SL)
    kv_full = kv_full.reshape(D, B * S)
    r_full = r_full.reshape(D, B * S)
    td8 = td.reshape(NCORE, 128, 1).astype(np.float32)
    tf8 = tf.reshape(NCORE, 128, 1).astype(np.float32)

    nc2 = _prog("A2", build_A2)
    maps = [{"kv": np.ascontiguousarray(kv_full[c * 128:(c + 1) * 128]),
             "r": np.ascontiguousarray(r_full[c * 128:(c + 1) * 128]),
             "td": td8[c], "tf": tf8[c]} for c in range(NCORE)]
    r2 = _run(nc2, maps).results

    rw_full = np.concatenate([r2[c]["out"].reshape(128, B, S) for c in range(NCORE)], axis=0)
    nc3 = _prog("A3", build_A3)
    maps = []
    for c in range(NCORE):
        rw_c = np.ascontiguousarray(rw_full[:, :, c * SL:(c + 1) * SL].reshape(D, TL))
        maps.append({"rw": rw_c, "xr": x_cm[c], "wo": woT, "onesc": _ONESC, "onesr": _ONESR})
    r3 = _run(nc3, maps).results
    x1 = [r3[c]["x1"] for c in range(NCORE)]
    ln1 = [r3[c]["ln1"] for c in range(NCORE)]
    return x1, ln1


def kernel(**inputs):
    inp = {k: np.asarray(v) for k, v in inputs.items()}
    x = inp["L0_attn_x"].astype(np.float32, copy=False)
    x_cm = [_cm(x, c) for c in range(NCORE)]
    x1, ln1 = stage_l0attn(x_cm, inp["L0_attn_Wk"], inp["L0_attn_Wv"], inp["L0_attn_Wr"],
                           inp["L0_attn_Wo"], inp["L0_attn_time_decay"], inp["L0_attn_time_first"])
    ffn = {k[len("L0_ffn_"):]: v for k, v in inp.items() if k.startswith("L0_ffn_")}
    x2, ln2b, ln2f = stage_l0ffn(ln1, x1, ffn)
    att = {k[len("L1_attn_"):]: v for k, v in inp.items() if k.startswith("L1_attn_")}
    l1o = stage_l1attn(ln2b, att)
    x3, ln3b, ln3f = stage_sandwich(l1o, x2)
    moe = {k[len("L1_ffn_"):]: v for k, v in inp.items() if k.startswith("L1_ffn_")}
    x4, _, _ = stage_l1ffn(ln3b, ln3f, x3, moe)
    out = np.empty((B, S, D), np.float32)
    for c in range(NCORE):
        out[:, c * SL:(c + 1) * SL, :] = np.asarray(x4[c]).reshape(D, B, SL).transpose(1, 2, 0)
    return out


INNER = 512
HEADS = 8
DH = 64
HID = 2730
HIDP = 2816
NJ = 44
NHT = HIDP // 128


def _shift_tiles(q, xf, xprev, hs):
    for i in range(4):
        t = 4 + i
        for b in range(B):
            q.op("vector", lambda e, i=i, t=t, b=b: e.tensor_copy(
                out=hs[i][:, b * SL + 1:(b + 1) * SL], in_=xf[t][:, b * SL:(b + 1) * SL - 1]))
            q.op("vector", lambda e, i=i, t=t, b=b: e.tensor_copy(
                out=hs[i][:, b * SL:b * SL + 1], in_=xprev[t][:, b:b + 1]))


def build_B0():
    q = Q()
    nc = q.nc
    xf = nc.dram_tensor("xf", [D, TL], BF16, kind="ExternalInput").ap()
    xprev = nc.dram_tensor("xprev", [D, B], BF16, kind="ExternalInput").ap()
    wqkv = nc.dram_tensor("wqkv", [D, 3 * INNER], F32, kind="ExternalInput").ap()
    csq = nc.dram_tensor("csq", [128, TL], BF16, kind="ExternalInput").ap()
    ssq = nc.dram_tensor("ssq", [128, TL], BF16, kind="ExternalInput").ap()
    csk = nc.dram_tensor("csk", [128, TL], BF16, kind="ExternalInput").ap()
    ssk = nc.dram_tensor("ssk", [128, TL], BF16, kind="ExternalInput").ap()
    qscol = nc.dram_tensor("qscol", [128, 1], F32, kind="ExternalInput").ap()
    kscol = nc.dram_tensor("kscol", [128, 1], F32, kind="ExternalInput").ap()
    selqk = nc.dram_tensor("selqk", [128, 8 * 16], F32, kind="ExternalInput").ap()
    expd = nc.dram_tensor("expd", [16, 8 * 128], F32, kind="ExternalInput").ap()
    p32 = nc.dram_tensor("p32", [128, 128], F32, kind="ExternalInput").ap()
    idf = nc.dram_tensor("idf", [128, 128], F32, kind="ExternalInput").ap()
    qf_o = nc.dram_tensor("qf", [INNER, TL], BF16, kind="ExternalOutput").ap()
    kf_o = nc.dram_tensor("kf", [INNER, TL], BF16, kind="ExternalOutput").ap()
    vt_o = nc.dram_tensor("vt", [TL, INNER], BF16, kind="ExternalOutput").ap()

    xft = [q.sbuf(f"xf{t}", [128, TL], BF16) for t in range(NDT)]
    xpv = [q.sbuf(f"xpv{t}", [128, B], BF16) for t in range(NDT)]
    hs = [q.sbuf(f"hs{i}", [128, TL], BF16) for i in range(4)]
    wt = [q.sbuf(f"w{t}", [128, 3 * INNER], BF16) for t in range(NDT)]
    qr = [q.sbuf(f"qr{t}", [128, TL], BF16) for t in range(4)]
    kr = [q.sbuf(f"kr{t}", [128, TL], BF16) for t in range(4)]
    vr = [q.sbuf(f"vr{t}", [128, TL], BF16) for t in range(4)]
    qf = [q.sbuf(f"qfo{t}", [128, TL], BF16) for t in range(4)]
    kf = [q.sbuf(f"kfo{t}", [128, TL], BF16) for t in range(4)]
    vtok = [q.sbuf(f"vt{t}", [128, INNER], BF16) for t in range(16)]
    tq1 = q.sbuf("tq1", [128, TL], BF16)
    tq2 = q.sbuf("tq2", [128, TL], BF16)
    tk1 = q.sbuf("tk1", [128, TL], BF16)
    tk2 = q.sbuf("tk2", [128, TL], BF16)
    qs8 = q.sbuf("qs8", [128, 1], F32)
    qs8w = q.sbuf("qs8w", [128, 1], F32)
    ksc = q.sbuf("ksc", [128, 1], F32)
    kscw = q.sbuf("kscw", [128, 1], F32)
    selb = q.sbuf("selb", [128, 8 * 16], BF16)
    expb = q.sbuf("expb", [16, 8 * 128], F32)
    p32b = q.sbuf("p32b", [128, 128], BF16)
    idb = q.sbuf("idb", [128, 128], BF16)
    sqs = [q.sbuf(f"sq{i}", [128, 512], BF16) for i in range(2)]
    nrm = q.sbuf("nrm", [16, TL], F32)
    tA = q.sbuf("tA", [128, 512], F32)
    tB = q.sbuf("tB", [128, 512], F32)
    tC = q.sbuf("tC", [128, 512], F32)
    ps_p = [q.psum(f"psp{i}", [128, 512]) for i in range(2)]
    ps_n = q.psum("psn", [16, 512])
    ps_sw = [q.psum(f"pssw{i}", [128, 512]) for i in range(2)]
    ps_bc = [q.psum(f"psbc{i}", [128, 512]) for i in range(2)]
    ps_tr = q.psum("pstr", [128, 128], BF16)

    for t in range(NDT):
        q.dma("sync", xft[t][:], xf[t * 128:(t + 1) * 128, :])
        q.dma("sync", xpv[t][:], xprev[t * 128:(t + 1) * 128, :])
        q.dma("gpsimd", wt[t][:], wqkv[t * 128:(t + 1) * 128, :])
    for dst, src in ((tq1, csq), (tq2, ssq), (tk1, csk), (tk2, ssk)):
        q.dma("sync", dst[:], src[:])
    for dst, src in ((qs8, qscol), (ksc, kscol)):
        q.dma("sync", dst[:], src[:])
    q.dma("gpsimd", selb[:], selqk[:])
    q.dma("sync", expb[:], expd[:])
    q.dma("gpsimd", p32b[:], p32[:])
    q.dma("gpsimd", idb[:], idf[:])
    q.wait_loads("vector", "scalar", "tensor")

    _shift_tiles(q, xft, xpv, hs)
    q.op("vector", lambda e: e.tensor_scalar(qs8[:], qs8[:], 8.0, None, op0=ALU.mult), inc=("ev", 1))
    q.wait("vector", "ev", q.cnt["ev"])
    for base in (0, 64):
        q.op("vector", lambda e, b=base: e.tensor_copy(out=qs8w[b:b + 32, :], in_=qs8[b + 32:b + 64, :]))
        q.op("vector", lambda e, b=base: e.tensor_copy(out=qs8w[b + 32:b + 64, :], in_=qs8[b:b + 32, :]))
        q.op("vector", lambda e, b=base: e.tensor_copy(out=kscw[b:b + 32, :], in_=ksc[b + 32:b + 64, :]))
        inc2 = ("ev", 1) if base == 64 else None
        q.op("vector", lambda e, b=base: e.tensor_copy(out=kscw[b + 32:b + 64, :], in_=ksc[b:b + 32, :]), inc=inc2)
    q.wait("vector", "ev", q.cnt["ev"])
    q.op("vector", lambda e: e.tensor_scalar(tq1[:], tq1[:], qs8[:], None, op0=ALU.mult))
    q.op("vector", lambda e: e.tensor_scalar(tq2[:], tq2[:], qs8w[:], None, op0=ALU.mult))
    q.op("vector", lambda e: e.tensor_scalar(tk1[:], tk1[:], ksc[:], None, op0=ALU.mult))
    q.op("vector", lambda e: e.tensor_scalar(tk2[:], tk2[:], kscw[:], None, op0=ALU.mult))
    q.barrier(engs_from=("vector",), engs_to=("tensor", "scalar"))

    rhs = xft[:4] + hs

    outs = qr + kr + vr

    def evac_qkv(ot, sl, ps, mmv):
        q.wait("vector", "mm", mmv)
        q.op("vector", lambda e, o=outs[ot], s=sl, ps=ps: e.tensor_copy(out=o[:, s], in_=ps[:]), inc=("ev", 1))
        return [("ev", q.cnt["ev"])]

    emit_proj(q, wt, rhs, ps_p, evac_qkv, n_out_tiles=12)
    q.barrier()

    tr = 0
    for vti in range(16):
        for cj in range(4):
            q.op("tensor", lambda e, v=vr[cj], c=vti: e.transpose(ps_tr[:], v[:, c * 128:(c + 1) * 128], idb[:]), inc=("mm", 1))
            mmv = q.cnt["mm"]
            q.wait("vector", "mm", mmv)
            q.op("vector", lambda e, t=vti, c=cj: e.tensor_copy(out=vtok[t][:, c * 128:(c + 1) * 128], in_=ps_tr[:]), inc=("ev", 1))
            q.wait("tensor", "ev", q.cnt["ev"])
            tr += 1
    q.barrier()

    srcs = qr + kr
    for ch in range(4):
        sl = slice(ch * 512, (ch + 1) * 512)
        for t in range(8):
            sq = sqs[t % 2]
            gi = ch * 8 + t
            if gi >= 2:
                q.wait("scalar", "mm", q.cnt["mm"] - 1)
            q.op("scalar", lambda e, a=sq, b=srcs[t], s=sl: e.activation(out=a[:], in_=b[:, s], func=AF.Square), inc=("ac", 1))
            q.wait("tensor", "ac", q.cnt["ac"])
            q.op("tensor", lambda e, a=sq, t=t, st=(t == 0), sp=(t == 7):
                 e.matmul(ps_n[:], selb[:, t * 16:(t + 1) * 16], a[:], start=st, stop=sp), inc=("mm", 1))
        q.barrier(engs_from=("tensor",), engs_to=("vector",))
        q.op("vector", lambda e, s=sl: e.tensor_copy(out=nrm[:, s], in_=ps_n[:]), inc=("ev", 1))
        q.barrier(engs_from=("vector",), engs_to=("tensor",))
    q.barrier(engs_from=("vector",), engs_to=("scalar",))
    q.op("scalar", lambda e: e.activation(out=nrm[:], in_=nrm[:], func=AF.Sqrt))
    q.barrier(engs_from=("scalar",), engs_to=("vector",))
    q.op("vector", lambda e: e.tensor_scalar(nrm[:], nrm[:], 1e-12, None, op0=ALU.max))
    q.op("vector", lambda e: e.reciprocal(out=nrm[:], in_=nrm[:]))
    q.barrier(engs_from=("vector",), engs_to=("tensor",))

    tabs1 = [tq1] * 4 + [tk1] * 4
    tabs2 = [tq2] * 4 + [tk2] * 4
    fins = qf + kf
    rothist = []
    for t in range(8):
        for ch in range(4):
            sl = slice(ch * 512, (ch + 1) * 512)
            gi = t * 4 + ch
            pb = ps_bc[gi % 2]
            pw = ps_sw[gi % 2]
            if gi >= 2:
                q.wait("tensor", "ev", rothist[gi - 2])
            q.op("tensor", lambda e, t=t, s=sl, pb=pb: e.matmul(pb[:], expb[:, t * 128:(t + 1) * 128], nrm[:, s], start=True, stop=True))
            q.op("tensor", lambda e, t=t, s=sl, pw=pw: e.matmul(pw[:], p32b[:], srcs[t][:, s], start=True, stop=True), inc=("mm", 1))
            mmv = q.cnt["mm"]
            q.wait("vector", "mm", mmv)
            q.op("vector", lambda e, pb=pb: e.tensor_copy(out=tC[:], in_=pb[:]))
            q.op("vector", lambda e, pw=pw: e.tensor_copy(out=tB[:], in_=pw[:]))
            q.op("vector", lambda e, t=t, s=sl: e.tensor_mul(tA[:], srcs[t][:, s], tC[:]))
            q.op("vector", lambda e: e.tensor_mul(tB[:], tB[:], tC[:]))
            q.op("vector", lambda e, t=t, s=sl: e.tensor_mul(tA[:], tA[:], tabs1[t][:, s]))
            q.op("vector", lambda e, t=t, s=sl: e.tensor_mul(tB[:], tB[:], tabs2[t][:, s]))
            q.op("vector", lambda e, t=t, s=sl: e.tensor_add(fins[t][:, s], tA[:], tB[:]), inc=("ev", 1))
            rothist.append(q.cnt["ev"])
    q.barrier()
    for t in range(4):
        q.dma("gpsimd", qf_o[t * 128:(t + 1) * 128, :], qf[t][:], store=True)
        q.dma("gpsimd", kf_o[t * 128:(t + 1) * 128, :], kf[t][:], store=True)
    for t in range(16):
        q.dma("sync", vt_o[t * 128:(t + 1) * 128, :], vtok[t][:], store=True)
    q.wait("gpsimd", "stg", q.cnt["stg"])
    q.wait("gpsimd", "sts", q.cnt["sts"])
    q._flush_waits("gpsimd")
    return q.finish()


def build_B1():
    q = Q()
    nc = q.nc
    qf = nc.dram_tensor("qf", [INNER, TL], BF16, kind="ExternalInput").ap()
    kfx = nc.dram_tensor("kfx", [INNER, 2 * TL], BF16, kind="ExternalInput").ap()
    vtx = nc.dram_tensor("vtx", [2 * TL, INNER], BF16, kind="ExternalInput").ap()
    xfb = nc.dram_tensor("xfb", [D, TL], BF16, kind="ExternalInput").ap()
    wo = nc.dram_tensor("wo", [INNER, D], F32, kind="ExternalInput").ap()
    mkm = nc.dram_tensor("mkm", [128, 8 * 512], BF16, kind="ExternalInput").ap()
    idf = nc.dram_tensor("idf", [128, 128], F32, kind="ExternalInput").ap()
    onesc = nc.dram_tensor("onesc", [128, 1], F32, kind="ExternalInput").ap()
    onesr = nc.dram_tensor("onesr", [1, 128], F32, kind="ExternalInput").ap()
    xm_o = nc.dram_tensor("xm", [D, TL], F32, kind="ExternalOutput").ap()
    xmb_o = nc.dram_tensor("xmb", [D, TL], BF16, kind="ExternalOutput").ap()

    qft = [q.sbuf(f"qf{t}", [128, TL], BF16) for t in range(4)]
    kft = [q.sbuf(f"kf{t}", [128, 2 * TL], BF16) for t in range(4)]
    vtt = [q.sbuf(f"vt{t}", [128, INNER], BF16) for t in range(32)]
    xft = [q.sbuf(f"xf{t}", [128, TL], BF16) for t in range(NDT)]
    wt = [q.sbuf(f"wo{t}", [128, D], BF16) for t in range(4)]
    mkmb = q.sbuf("mkmb", [128, 8 * 512], BF16)
    idb = q.sbuf("idb", [128, 128], BF16)
    onescb = q.sbuf("onescb", [128, 1], BF16)
    onesrf = q.sbuf("onesrf", [1, 128], F32)
    obuf = [q.sbuf(f"ob{t}", [128, TL], BF16) for t in range(4)]
    pTp = [q.sbuf(f"pT{i}", [128, 8 * 512], BF16) for i in range(2)]
    zrowp = [q.sbuf(f"zrow{i}", [1, 512], F32) for i in range(2)]
    zb = q.sbuf("zb", [64, 512], F32)
    xstg = [q.sbuf(f"xstg{i}", [128, 512], F32) for i in range(2)]
    xstgb = [q.sbuf(f"xstgb{i}", [128, 512], BF16) for i in range(2)]
    ps_T = [q.psum(f"ps_T{i}", [128, 512]) for i in range(2)]
    ps_zp = [q.psum(f"ps_z{i}", [1, 512]) for i in range(2)]
    ps_op = [q.psum(f"ps_o{i}", [64, 512]) for i in range(2)]
    ps_zb = q.psum("ps_zb", [64, 512])

    for t in range(4):
        q.dma("sync", qft[t][:], qf[t * 128:(t + 1) * 128, :])
        q.dma("sync", kft[t][:], kfx[t * 128:(t + 1) * 128, :])
    for t in range(32):
        q.dma("sync", vtt[t][:], vtx[t * 128:(t + 1) * 128, :])
    for t in range(NDT):
        q.dma("sync", xft[t][:], xfb[t * 128:(t + 1) * 128, :])
    for t in range(4):
        q.dma("gpsimd", wt[t][:], wo[t * 128:(t + 1) * 128, :])
    q.dma("sync", mkmb[:], mkm[:])
    q.dma("gpsimd", idb[:], idf[:])
    q.dma("gpsimd", onescb[:], onesc[:])
    q.dma("gpsimd", onesrf[:], onesr[:])
    q.wait_loads("vector", "scalar", "tensor")

    blocks = [(b, h, h // 2, (h % 2) * 64) for b in range(B) for h in range(HEADS)]
    NBL = len(blocks)
    ac_mark = {}
    ev_tail = 0
    for i in range(NBL + 1):
        cur = blocks[i] if i < NBL else None
        pv = blocks[i - 1] if i >= 1 else None
        pv2 = blocks[i - 2] if i >= 2 else None
        par = i % 2
        for kt in range(8):
            if cur is not None:
                b, h, ht, hr = cur
                ps = ps_T[kt % 2]
                if i * 8 + kt >= 2:
                    q.wait("tensor", "ac", q.cnt["ac"] - 1)
                q.op("tensor", lambda e, kt=kt, ps=ps: e.matmul(
                    ps[:], idb[:], mkmb[:, kt * 512:(kt + 1) * 512], start=True, stop=False))
                q.op("tensor", lambda e, b=b, kt=kt, ps=ps, ht=ht, hr=hr: e.matmul(
                    ps[:], kft[ht][hr:hr + 64, b * 1024 + kt * 128: b * 1024 + (kt + 1) * 128],
                    qft[ht][hr:hr + 64, b * 512:(b + 1) * 512],
                    start=False, stop=True), inc=("mm", 1))
                q.wait("scalar", "mm", q.cnt["mm"])
                q.op("scalar", lambda e, kt=kt, ps=ps, par=par: e.activation(
                    out=pTp[par][:, kt * 512:(kt + 1) * 512], in_=ps[:], func=AF.Exp), inc=("ac", 1))
                ac_mark[(i, kt)] = q.cnt["ac"]
            if pv is not None:
                pb, ph, pht, phr = pv
                if kt == 0:
                    q.wait("tensor", "ev", ev_tail)
                q.wait("tensor", "ac", ac_mark[(i - 1, kt)])
                q.op("tensor", lambda e, kt=kt, par=par, st=(kt == 0), sp=(kt == 7): e.matmul(
                    ps_zp[1 - par][:], onescb[:], pTp[1 - par][:, kt * 512:(kt + 1) * 512],
                    start=st, stop=sp))
                q.op("tensor", lambda e, pb=pb, kt=kt, ph=ph, par=par, st=(kt == 0), sp=(kt == 7): e.matmul(
                    ps_op[1 - par][:], vtt[pb * 8 + kt][:, ph * 64:(ph + 1) * 64],
                    pTp[1 - par][:, kt * 512:(kt + 1) * 512],
                    start=st, stop=sp), inc=(("mm", 1) if kt == 7 else None))
        if pv2 is not None:
            q.op("tensor", lambda e, par=par: e.matmul(
                ps_zb[:], onesrf[:, 0:64], zrowp[par][:], start=True, stop=True), inc=("mm", 1))
        if pv is not None:
            q.wait("vector", "mm", q.cnt["mm"])
        if pv2 is not None:
            p2b, p2h, p2ht, p2hr = pv2
            q.op("vector", lambda e: e.tensor_copy(out=zb[:], in_=ps_zb[:]))
            q.op("vector", lambda e, p2b=p2b, p2ht=p2ht, p2hr=p2hr, par=par: e.tensor_mul(
                obuf[p2ht][p2hr:p2hr + 64, p2b * 512:(p2b + 1) * 512], ps_op[par][:], zb[:]), inc=("ev", 1))
        if pv is not None:
            q.op("vector", lambda e, par=par: e.tensor_copy(out=zrowp[1 - par][:], in_=ps_zp[1 - par][:]), inc=("ev", 1))
            q.wait("vector", "ev", q.cnt["ev"])
            q.op("vector", lambda e, par=par: e.reciprocal(out=zrowp[1 - par][:], in_=zrowp[1 - par][:]), inc=("ev", 1))
        ev_tail = q.cnt["ev"]
    q.barrier(engs_from=("vector",), engs_to=("tensor",))
    q.op("tensor", lambda e: e.matmul(
        ps_zb[:], onesrf[:, 0:64], zrowp[1][:], start=True, stop=True), inc=("mm", 1))
    q.wait("vector", "mm", q.cnt["mm"])
    pb, ph, pht, phr = blocks[NBL - 1]
    q.op("vector", lambda e: e.tensor_copy(out=zb[:], in_=ps_zb[:]))
    q.op("vector", lambda e, pb=pb, pht=pht, phr=phr: e.tensor_mul(
        obuf[pht][phr:phr + 64, pb * 512:(pb + 1) * 512], ps_op[1][:], zb[:]), inc=("ev", 1))
    q.barrier()

    for ot in range(NDT):
        for ch in range(4):
            sl = slice(ch * 512, (ch + 1) * 512)
            gi = ot * 4 + ch
            ps = ps_T[gi % 2]
            ii = gi % 2
            if gi >= 2:
                q.wait("tensor", "stg", 16 * (2 * (gi - 1)))
            for kin in range(4):
                inc = ("mm", 1) if kin == 3 else None
                q.op("tensor", lambda e, kin=kin, ot=ot, s=sl, ps=ps, st=(kin == 0), sp=(kin == 3): e.matmul(
                    ps[:], wt[kin][:, ot * 128:(ot + 1) * 128], obuf[kin][:, s], start=st, stop=sp), inc=inc)
            q.wait("vector", "mm", q.cnt["mm"])
            q.op("vector", lambda e, ot=ot, s=sl, ps=ps, ii=ii: e.tensor_add(xstg[ii][:], xft[ot][:, s], ps[:]))
            q.op("vector", lambda e, ii=ii: e.tensor_copy(out=xstgb[ii][:], in_=xstg[ii][:]), inc=("ev", 1))
            q.wait("gpsimd", "ev", q.cnt["ev"])
            q.dma("gpsimd", xm_o[ot * 128:(ot + 1) * 128, sl], xstg[ii][:], store=True)
            q.dma("gpsimd", xmb_o[ot * 128:(ot + 1) * 128, sl], xstgb[ii][:], store=True)
    q.wait("gpsimd", "stg", q.cnt["stg"])
    q._flush_waits("gpsimd")
    return q.finish()



def _rot_tables(c, for_q):
    i = np.arange(32)
    freq = 10000.0 ** (-(2 * i) / 64.0)
    base = (2 * i + 0.4 * 64) / (1.4 * 64)
    t_glob = (512 * c + np.arange(SL)).astype(np.float64)
    theta = t_glob[None, :] * freq[:, None]
    power = (t_glob - 2048.0) / 256.0
    scale = base[:, None] ** power[None, :]
    if not for_q:
        scale = 1.0 / scale
    cos = np.cos(theta) * scale
    sin = np.sin(theta) * scale
    cs64 = np.concatenate([cos, cos], 0)
    ss64 = np.concatenate([-sin, sin], 0)
    CS = np.tile(np.tile(cs64, (2, 1)), (1, B)).astype(BF)
    SS = np.tile(np.tile(ss64, (2, 1)), (1, B)).astype(BF)
    return CS, SS


def _sel_exp_consts():
    selqk = np.zeros((128, 8 * 16), np.float32)
    expd = np.zeros((16, 8 * 128), np.float32)
    for t in range(8):
        for half in range(2):
            row = (2 * t + half) if t < 4 else (8 + 2 * (t - 4) + half)
            selqk[half * 64:(half + 1) * 64, t * 16 + row] = 1.0
            expd[row, t * 128 + half * 64: t * 128 + (half + 1) * 64] = 1.0
    p32 = np.zeros((128, 128), np.float32)
    for r in range(128):
        k = r + 32 if (r % 64) < 32 else r - 32
        p32[k, r] = 1.0
    idf = np.eye(128, dtype=np.float32)
    return selqk, expd, p32, idf


def _ffn_masks(c):
    mkm = np.zeros((128, 8 * 512), np.float32)
    qpos = 512 * c + np.arange(512)
    kpos = 512 * (c - 1) + np.arange(1024)
    kvalid = (kpos >= 0)
    allowed = (kpos[None, :] <= qpos[:, None]) & (kpos[None, :] >= qpos[:, None] - 512) & kvalid[None, :]
    neg = -1e30
    for kt in range(8):
        blk = np.where(allowed[:, kt * 128:(kt + 1) * 128], -20.0, neg).T
        mkm[:, kt * 512:(kt + 1) * 512] = blk
    return mkm.astype(BF)


def _qs_col(qs):
    return np.tile(np.asarray(qs, np.float32), 2).reshape(128, 1)


def stage_l0ffn_B0(xf_cm, wqkv_l, q_scale_l, k_scale_l):
    selqk, expd, p32, idf = _sel_exp_consts()
    nc = _prog("B0", build_B0)
    maps = []
    for c in range(NCORE):
        xprev = np.zeros((D, B), BF) if c == 0 else np.ascontiguousarray(
            xf_cm[c - 1][:, SL - 1::SL])
        csq, ssq = _rot_tables(c, True)
        csk, ssk = _rot_tables(c, False)
        maps.append({
            "xf": xf_cm[c], "xprev": xprev, "wqkv": np.ascontiguousarray(wqkv_l, np.float32),
            "csq": csq, "ssq": ssq, "csk": csk, "ssk": ssk,
            "qscol": _qs_col(q_scale_l), "kscol": _qs_col(k_scale_l),
            "selqk": selqk, "expd": expd, "p32": p32, "idf": idf,
        })
    r = _run(nc, maps).results
    return ([r[c]["qf"] for c in range(NCORE)], [r[c]["kf"] for c in range(NCORE)],
            [r[c]["vt"] for c in range(NCORE)])


def stage_l0ffn_B1(qf, kf, vt, xf_cm, wo_l):
    selqk, expd, p32, idf = _sel_exp_consts()
    nc = _prog("B1", build_B1)
    maps = []
    for c in range(NCORE):
        kfx = np.zeros((INNER, 2 * TL), BF)
        vtx = np.zeros((2 * TL, INNER), BF)
        for b in range(B):
            if c > 0:
                kfx[:, b * 1024: b * 1024 + 512] = kf[c - 1][:, b * 512:(b + 1) * 512]
                vtx[b * 1024: b * 1024 + 512, :] = vt[c - 1][b * 512:(b + 1) * 512, :]
            kfx[:, b * 1024 + 512:(b + 1) * 1024] = kf[c][:, b * 512:(b + 1) * 512]
            vtx[b * 1024 + 512:(b + 1) * 1024, :] = vt[c][b * 512:(b + 1) * 512, :]
        mkm = _ffn_masks(c)
        maps.append({
            "qf": qf[c], "kfx": kfx, "vtx": vtx, "xfb": xf_cm[c],
            "wo": np.ascontiguousarray(wo_l, np.float32),
            "mkm": mkm, "idf": idf, "onesc": _ONESC, "onesr": _ONESR,
        })
    r = _run(nc, maps).results
    return [r[c]["xm"] for c in range(NCORE)], [r[c]["xmb"] for c in range(NCORE)]


def build_B2(dbg=0):
    q = Q()
    nc = q.nc
    xmb = nc.dram_tensor("xmb", [D, TL], BF16, kind="ExternalInput").ap()
    xmprev = nc.dram_tensor("xmprev", [D, B], BF16, kind="ExternalInput").ap()
    xm = nc.dram_tensor("xm", [D, TL], F32, kind="ExternalInput").ap()
    lngb = nc.dram_tensor("lngb", [128, 16], F32, kind="ExternalInput").ap()
    w1r = nc.dram_tensor("w1r", [NJ * 128, D], F32, kind="ExternalInput").ap()
    b1r = nc.dram_tensor("b1r", [128, NJ], F32, kind="ExternalInput").ap()
    w2r = nc.dram_tensor("w2r", [HIDP, D], F32, kind="ExternalInput").ap()
    b2c = nc.dram_tensor("b2c", [128, NDT], F32, kind="ExternalInput").ap()
    onesc = nc.dram_tensor("onesc", [128, 1], F32, kind="ExternalInput").ap()
    onesr = nc.dram_tensor("onesr", [1, 128], F32, kind="ExternalInput").ap()
    xf2_o = nc.dram_tensor("xf2", [D, TL], F32, kind="ExternalOutput").ap()
    xf2b_o = nc.dram_tensor("xf2b", [D, TL], BF16, kind="ExternalOutput").ap()
    hdbg_o = nc.dram_tensor("hdbg", [HIDP, TL], BF16, kind="ExternalOutput").ap() if dbg == 2 else None

    xmt = [q.sbuf(f"xmb{t}", [128, TL], BF16) for t in range(NDT)]
    xpv = [q.sbuf(f"xpv{t}", [128, B], BF16) for t in range(NDT)]
    sh = [q.sbuf(f"sh{i}", [128, TL], BF16) for i in range(4)]
    hbuf = [q.sbuf(f"hb{j}", [128, TL], BF16) for j in range(NHT)]
    w1t = [q.sbuf(f"w1t{i}", [128, D], BF16) for i in range(2)]
    lngbs = q.sbuf("lngbs", [128, 16], F32)
    b1s = q.sbuf("b1s", [128, NJ], F32)
    b2s = q.sbuf("b2s", [128, NDT], F32)
    gl = [q.sbuf(f"gl{i}", [64, 512], F32) for i in range(2)]
    asb = q.sbuf("asb", [64, 512], F32)
    tmpf = q.sbuf("tmpf", [128, 512], F32)
    xmf = [q.sbuf(f"xmf{i}", [128, 512], F32) for i in range(2)]
    stg = [q.sbuf(f"stg{i}", [128, 512], F32) for i in range(2)]
    stgb = [q.sbuf(f"stgb{i}", [128, 512], BF16) for i in range(2)]
    cn = alloc_ln_common(q, TL)
    pp = (q.psum("ps_p0", [128, 512]), q.psum("ps_p1", [128, 512]))

    for t in range(NDT):
        q.dma("sync", xmt[t][:], xmb[t * 128:(t + 1) * 128, :])
        q.dma("sync", xpv[t][:], xmprev[t * 128:(t + 1) * 128, :])
    q.dma("sync", lngbs[:], lngb[:])
    q.dma("sync", b1s[:], b1r[:])
    q.dma("sync", b2s[:], b2c[:])
    load_ln_consts(q, cn, onesc, onesr)
    q.wait_loads("vector", "scalar", "tensor")

    _shift_tiles(q, xmt, xpv, sh)
    q.barrier(engs_from=("vector",), engs_to=("scalar", "tensor", "vector"))

    srcs = xmt[:4] + sh
    lnx = srcs
    nch = 4
    for ch in range(nch):
        sl = slice(ch * 512, (ch + 1) * 512)
        for t in range(NDT):
            sq = cn["sqs"][t % 2]
            if ch * NDT + t >= 2:
                q.wait("scalar", "mm", q.cnt["mm"] - 1)
            q.op("scalar", lambda e, a=sq, b=srcs[t], s=sl: e.activation(out=a[:], in_=b[:, s], func=AF.Square), inc=("ac", 1))
            q.op("tensor", lambda e, r=srcs[t], s=sl, st=(t == 0), sp=(t == NDT - 1), ps=cn["ps_st"][0]:
                 e.matmul(ps[:], cn["onesc_b"][:], r[:, s], start=st, stop=sp))
            q.wait("tensor", "ac", q.cnt["ac"])
            q.op("tensor", lambda e, r=sq, st=(t == 0), sp=(t == NDT - 1), ps=cn["ps_st"][1]:
                 e.matmul(ps[:], cn["onesc_b"][:], r[:], start=st, stop=sp), inc=("mm", 1))
        q.barrier(engs_from=("tensor",), engs_to=("vector",))
        q.op("vector", lambda e, s=sl: e.tensor_copy(out=cn["ssum"][:, s], in_=cn["ps_st"][0][:]))
        q.op("vector", lambda e, s=sl: e.tensor_copy(out=cn["sssq"][:, s], in_=cn["ps_st"][1][:]), inc=("ev", 1))
        q.barrier(engs_from=("vector",), engs_to=("tensor",))
    emit_ln_finalize(q, cn)
    for ch in range(nch):
        sl = slice(ch * 512, (ch + 1) * 512)
        q.op("tensor", lambda e, s=sl: e.matmul(cn["ps_bc"][0][:], cn["onesr_f"][:], cn["rB"][:, s], start=True, stop=True))
        q.op("tensor", lambda e, s=sl: e.matmul(cn["ps_bc"][1][:], cn["onesr_f"][:], cn["rC"][:, s], start=True, stop=True), inc=("mm", 1))
        q.barrier(engs_from=("tensor",), engs_to=("vector",))
        q.op("vector", lambda e, s=sl: e.tensor_copy(out=cn["bcs"][0][:, s], in_=cn["ps_bc"][0][:]))
        q.op("vector", lambda e, s=sl: e.tensor_copy(out=cn["bcs"][1][:, s], in_=cn["ps_bc"][1][:]), inc=("ev", 1))
        q.barrier(engs_from=("vector",), engs_to=("tensor",))
    for t in range(NDT):
        for ch in range(4):
            sl = slice(ch * 512, (ch + 1) * 512)
            q.op("vector", lambda e, a=srcs[t], s=sl: e.tensor_mul(tmpf[:], a[:, s], cn["bcs"][0][:, s]))
            q.op("vector", lambda e, s=sl: e.tensor_add(tmpf[:], tmpf[:], cn["bcs"][1][:, s]))
            q.op("vector", lambda e, t=t, o=lnx[t], s=sl: e.tensor_scalar(
                o[:, s], tmpf[:], lngbs[:, t:t + 1], lngbs[:, NDT + t:NDT + t + 1], op0=ALU.mult, op1=ALU.add), inc=("ev", 1))
    q.barrier()
    if dbg == 1:
        for t in range(NDT):
            q.dma("gpsimd", xf2b_o[t * 128:(t + 1) * 128, :], lnx[t][:], store=True)
        q.dma("gpsimd", xf2_o[:, :], xm[:, :], store=True)
        q.wait("gpsimd", "stg", q.cnt["stg"])
        q._flush_waits("gpsimd")
        return q.finish()

    for j in range(NJ):
        wb = w1t[j % 2]
        if j >= 2:
            q.wait("gpsimd", "mm", q.cnt["mm"] - 4)
        q.dma("gpsimd", wb[:], w1r[j * 128:(j + 1) * 128, :])
        ldg_j = q.cnt["ldg"]
        jj, half = j // 2, (j % 2) * 64
        for ch in range(4):
            sl = slice(ch * 512, (ch + 1) * 512)
            gi = j * 4 + ch
            ps = pp[gi % 2]
            if gi >= 2:
                q.wait("tensor", "ev", q.cnt["ev"] - 1)
            q.wait("tensor", "ldg", ldg_j)
            for kin in range(NDT):
                inc = ("mm", 1) if kin == NDT - 1 else None
                q.op("tensor", lambda e, kin=kin, s=sl, ps=ps, wb=wb, st=(kin == 0), sp=(kin == NDT - 1):
                     e.matmul(ps[:], wb[:, kin * 128:(kin + 1) * 128], lnx[kin][:, s], start=st, stop=sp), inc=inc)
            mmv = q.cnt["mm"]
            q.wait("scalar", "mm", mmv)
            if gi >= 2:
                q.wait("scalar", "ev", q.cnt["ev"] - 1)
            gli = gl[gi % 2]
            q.op("scalar", lambda e, ps=ps, j=j, g=gli: e.activation(
                out=g[:], in_=ps[64:128, :], func=AF.Gelu, bias=b1s[64:128, j:j + 1]), inc=("ac", 1))
            q.wait("vector", "ac", q.cnt["ac"])
            q.op("vector", lambda e, ps=ps, j=j: e.tensor_scalar(asb[:], ps[0:64, :], b1s[0:64, j:j + 1], None, op0=ALU.add))
            q.op("vector", lambda e, jj=jj, half=half, s=sl, g=gli: e.tensor_mul(hbuf[jj][half:half + 64, s], asb[:], g[:]), inc=("ev", 1))
    q.barrier()

    if dbg == 2:
        q.barrier()
        for jj in range(NHT):
            q.dma("gpsimd", hdbg_o[jj * 128:(jj + 1) * 128, :], hbuf[jj][:], store=True)
        q.dma("gpsimd", xf2_o[:, :], xm[:, :], store=True)
        for t in range(NDT):
            q.dma("sync", xf2b_o[t * 128:(t + 1) * 128, :], lnx[t][:], store=True)
        q.wait("gpsimd", "stg", q.cnt["stg"])
        q.wait("gpsimd", "sts", q.cnt["sts"])
        q._flush_waits("gpsimd")
        return q.finish()

    w2t = []
    for jj in range(NHT):
        if jj < 16:
            view = xmt[jj // 2][:, (jj % 2) * D:(jj % 2) * D + D]
        else:
            k = jj - 16
            view = sh[k // 2][:, (k % 2) * D:(k % 2) * D + D]
        w2t.append(view)
        q.dma("gpsimd", view, w2r[jj * 128:(jj + 1) * 128, :])
    q.wait_loads("tensor", "vector")
    hist3 = []
    lds0 = q.cnt["lds"]
    for ot in range(NDT):
        for ch in range(4):
            sl = slice(ch * 512, (ch + 1) * 512)
            gi = ot * 4 + ch
            ps = pp[gi % 2]
            if gi >= 2:
                q.wait("sync", "ev", hist3[gi - 2])
            q.dma("sync", xmf[gi % 2][:], xm[ot * 128:(ot + 1) * 128, sl])
            if gi >= 2:
                q.wait("tensor", "ev", hist3[gi - 2])
            for jj in range(NHT):
                inc = ("mm", 1) if jj == NHT - 1 else None
                q.op("tensor", lambda e, jj=jj, ot=ot, s=sl, ps=ps, st=(jj == 0), sp=(jj == NHT - 1):
                     e.matmul(ps[:], w2t[jj][:, ot * 128:(ot + 1) * 128], hbuf[jj][:, s], start=st, stop=sp), inc=inc)
            q.wait("vector", "mm", q.cnt["mm"])
            q.wait("vector", "lds", q.cnt["lds"])
            if gi >= 2:
                q.wait("vector", "stg", 16 * (2 * (gi - 1)))
            ii = gi % 2
            q.op("vector", lambda e, ps=ps, ot=ot, ii=ii: e.scalar_tensor_tensor(
                out=stg[ii][:], in0=ps[:], scalar=b2s[:, ot:ot + 1], in1=xmf[ii][:],
                op0=ALU.add, op1=ALU.add))
            q.op("vector", lambda e, ii=ii: e.tensor_copy(out=stgb[ii][:], in_=stg[ii][:]), inc=("ev", 1))
            hist3.append(q.cnt["ev"])
            q.wait("gpsimd", "ev", q.cnt["ev"])
            q.dma("gpsimd", xf2_o[ot * 128:(ot + 1) * 128, sl], stg[ii][:], store=True)
            q.dma("gpsimd", xf2b_o[ot * 128:(ot + 1) * 128, sl], stgb[ii][:], store=True)
    q.wait("gpsimd", "stg", q.cnt["stg"])
    q._flush_waits("gpsimd")
    return q.finish()


def _w1_prep(w1_l, b1_l):
    w1 = np.asarray(w1_l, np.float32)
    b1 = np.asarray(b1_l, np.float32)
    a, g = w1[:, :HID], w1[:, HID:]
    ba, bg = b1[:HID], b1[HID:]
    ap = np.zeros((D, HIDP), np.float32); ap[:, :HID] = a
    gp = np.zeros((D, HIDP), np.float32); gp[:, :HID] = g
    bap = np.zeros(HIDP, np.float32); bap[:HID] = ba
    bgp = np.zeros(HIDP, np.float32); bgp[:HID] = bg
    w1r = np.zeros((NJ * 128, D), np.float32)
    b1r = np.zeros((128, NJ), np.float32)
    for j in range(NJ):
        cols_a = ap[:, j * 64:(j + 1) * 64]
        cols_g = gp[:, j * 64:(j + 1) * 64]
        w128 = np.concatenate([cols_a, cols_g], axis=1)
        w1r[j * 128:(j + 1) * 128] = w128.reshape(NDT, 128, 128).transpose(1, 0, 2).reshape(128, D)
        b1r[0:64, j] = bap[j * 64:(j + 1) * 64]
        b1r[64:128, j] = bgp[j * 64:(j + 1) * 64]
    return w1r, b1r


def _w2_prep(w2_l, b2_l):
    w2 = np.zeros((HIDP, D), np.float32)
    w2[:HID] = np.asarray(w2_l, np.float32)
    b2c = np.asarray(b2_l, np.float32).reshape(NDT, 128).T.copy()
    return w2, b2c


def stage_l0ffn_B2(xm, xmb, w1_l, b1_l, w2_l, b2_l, lng_l, lnb_l):
    w1r, b1r = _w1_prep(w1_l, b1_l)
    w2r, b2c = _w2_prep(w2_l, b2_l)
    lngb = np.stack([np.asarray(lng_l, np.float32).reshape(NDT, 128).T,
                     np.asarray(lnb_l, np.float32).reshape(NDT, 128).T], axis=1).reshape(128, 16)
    lngb = np.concatenate([np.asarray(lng_l, np.float32).reshape(NDT, 128).T,
                           np.asarray(lnb_l, np.float32).reshape(NDT, 128).T], axis=1)
    nc = _prog("B2", build_B2)
    maps = []
    for c in range(NCORE):
        xmprev = np.zeros((D, B), BF) if c == 0 else np.ascontiguousarray(xmb[c - 1][:, SL - 1::SL])
        maps.append({
            "xmb": xmb[c], "xmprev": xmprev, "xm": xm[c], "lngb": lngb,
            "w1r": w1r, "b1r": b1r, "w2r": w2r, "b2c": b2c,
            "onesc": _ONESC, "onesr": _ONESR,
        })
    r = _run(nc, maps).results
    return [r[c]["xf2"] for c in range(NCORE)], [r[c]["xf2b"] for c in range(NCORE)]


def build_SW(mode="full"):
    q = Q()
    nc = q.nc
    y = nc.dram_tensor("y", [D, TL], F32, kind="ExternalInput").ap()
    res = nc.dram_tensor("res", [D, TL], F32, kind="ExternalInput").ap()
    onesc = nc.dram_tensor("onesc", [128, 1], F32, kind="ExternalInput").ap()
    onesr = nc.dram_tensor("onesr", [1, 128], F32, kind="ExternalInput").ap()
    x_o = nc.dram_tensor("x", [D, TL], F32, kind="ExternalOutput").ap()
    ln_o = nc.dram_tensor("ln", [D, TL], BF16, kind="ExternalOutput").ap() if mode != "x" else None
    lnf_o = nc.dram_tensor("lnf", [D, TL], F32, kind="ExternalOutput").ap() if mode == "full" else None

    yt = [q.sbuf(f"y{t}", [128, TL], F32) for t in range(NDT)]
    rt = [q.sbuf(f"r{t}", [128, TL], F32) for t in range(NDT)]
    xb = [q.sbuf(f"b{t}", [128, TL], BF16) for t in range(NDT)]
    cn = alloc_ln_common(q, TL)
    for t in range(NDT):
        q.dma("sync", yt[t][:], y[t * 128:(t + 1) * 128, :])
        q.dma("sync", rt[t][:], res[t * 128:(t + 1) * 128, :])
    load_ln_consts(q, cn, onesc, onesr)
    q.wait_loads("vector", "scalar", "tensor")
    emit_ln_cm(q, yt, xb, cn, TL)
    for t in range(NDT):
        q.op("vector", lambda e, t=t: e.tensor_add(rt[t][:], rt[t][:], yt[t][:]), inc=("ev", 1))
        q.wait("sync", "ev", q.cnt["ev"])
        q.dma("sync", x_o[t * 128:(t + 1) * 128, :], rt[t][:], store=True)
    if mode == "x":
        q.wait("sync", "sts", q.cnt["sts"])
        q._flush_waits("sync")
        return q.finish()
    q.wait("vector", "sts", q.cnt["sts"])
    emit_ln_cm(q, rt, xb, cn, TL)
    q.barrier(engs_from=("vector",), engs_to=("gpsimd", "sync"))
    for t in range(NDT):
        q.dma("gpsimd", ln_o[t * 128:(t + 1) * 128, :], xb[t][:], store=True)
        if mode == "full":
            q.dma("sync", lnf_o[t * 128:(t + 1) * 128, :], rt[t][:], store=True)
    q.wait("gpsimd", "stg", q.cnt["stg"])
    if mode == "full":
        q.wait("gpsimd", "sts", q.cnt["sts"])
    q._flush_waits("gpsimd")
    return q.finish()


_PROG = {}


def _prog(name, builder):
    if name not in _PROG:
        _PROG[name] = builder()
    return _PROG[name]


def stage_sandwich(y_cm, res_cm, mode="full"):
    nc = _prog("SW_" + mode, lambda: build_SW(mode))
    maps = [{"y": y_cm[c], "res": res_cm[c], "onesc": _ONESC, "onesr": _ONESR}
            for c in range(NCORE)]
    r = _run(nc, maps).results
    return ([r[c]["x"] for c in range(NCORE)],
            [r[c]["ln"] for c in range(NCORE)] if mode != "x" else None,
            [r[c]["lnf"] for c in range(NCORE)] if mode == "full" else None)


def stage_l0ffn(ln1_cm, x1_cm, ffn):
    xf = ln1_cm
    xf_f32 = None
    for l in range(2):
        qf, kf, vt = stage_l0ffn_B0(xf, ffn["wqkv"][l], ffn["q_scale"][l], ffn["k_scale"][l])
        xm, xmb = stage_l0ffn_B1(qf, kf, vt, xf, ffn["wo"][l])
        xf2, xf2b = stage_l0ffn_B2(xm, xmb, ffn["w1"][l], ffn["b1"][l], ffn["w2"][l], ffn["b2"][l],
                                   ffn["ln_g"][l], ffn["ln_b"][l])
        xf = xf2b
        xf_f32 = xf2
    return stage_sandwich(xf_f32, x1_cm, mode="xln")


H16 = 16
DK = 32
DV = 64
PAIRS = [(t, s) for t in range(B) for s in range(t + 1)]


def build_D(dbg=0):
    q = Q()
    nc = q.nc
    xn_i = nc.dram_tensor("xn", [D, TL], BF16, kind="ExternalInput").ap()
    convw = nc.dram_tensor("convw", [128, 32], F32, kind="ExternalInput").ap()
    wq = nc.dram_tensor("wq", [D, 512], F32, kind="ExternalInput").ap()
    wkg = nc.dram_tensor("wkg", [D, 512], F32, kind="ExternalInput").ap()
    wv = nc.dram_tensor("wv", [D, D], F32, kind="ExternalInput").ap()
    wg = nc.dram_tensor("wg", [D, D], F32, kind="ExternalInput").ap()
    wout = nc.dram_tensor("wout", [D, D], F32, kind="ExternalInput").ap()
    bgc = nc.dram_tensor("bgc", [128, NDT], F32, kind="ExternalInput").ap()
    augc = nc.dram_tensor("augc", [128, 4], F32, kind="ExternalInput").ap()
    selH = nc.dram_tensor("selH", [128, 64], F32, kind="ExternalInput").ap()
    selV = nc.dram_tensor("selV", [128, 128], F32, kind="ExternalInput").ap()
    expH = nc.dram_tensor("expH", [16, 8 * 128], F32, kind="ExternalInput").ap()
    l1o_o = nc.dram_tensor("l1o", [D, TL], F32, kind="ExternalOutput").ap()
    gsd = nc.dram_tensor("gsd", [D, TL], BF16, kind="Internal").ap()
    dbg_o = nc.dram_tensor("dbg", [D, TL], BF16, kind="ExternalOutput").ap() if dbg else None

    xnb = [q.sbuf(f"xn{t}", [128, TL], BF16) for t in range(NDT)]
    wts = {}
    for nm, wd in (("wq", 4), ("wkg", 4), ("wv", 8), ("wg", 8)):
        wts[nm] = [q.sbuf(f"{nm}{t}", [128, wd * 128], BF16) for t in range(NDT)]
    qb = [q.sbuf(f"qb{t}", [128, TL], BF16) for t in range(4)]
    gkf = [q.sbuf(f"gk{t}", [128, TL], F32) for t in range(4)]
    k1b = [q.sbuf(f"k1{t}", [128, TL], BF16) for t in range(4)]
    vb = [q.sbuf(f"vb{t}", [128, TL], BF16) for t in range(NDT)]
    gss = [q.sbuf(f"gss{i}", [128, 512], BF16) for i in range(2)]
    gsl = [q.sbuf(f"gsl{i}", [128, 512], BF16) for i in range(2)]
    wob = [qb[t][:, 0:D] for t in range(4)] + [k1b[t][:, 0:D] for t in range(4)]
    ob = [gkf[dt // 4][:, (dt % 4) * 512:(dt % 4 + 1) * 512] for dt in range(NDT)]
    obb = [gkf[2][:].bitcast(BF16)[:, dt * 512:(dt + 1) * 512] for dt in range(NDT)]
    gob = [gkf[3][:].bitcast(BF16)[:, dt * 512:(dt + 1) * 512] for dt in range(NDT)]
    abuf = q.sbuf("abuf", [16, 10 * 512], BF16)
    augw = q.sbuf("augw", [16, 4 * 512], BF16)
    cw = q.sbuf("cw", [128, 32], F32)
    bgs = q.sbuf("bgs", [128, NDT], F32)
    augs = q.sbuf("augs", [128, 4], F32)
    selHb = q.sbuf("selHb", [128, 64], BF16)
    selAb = q.sbuf("selAb", [128, 64], BF16)
    selVb = q.sbuf("selVb", [128, 128], BF16)
    selVf = q.sbuf("selVf", [128, 128], F32)
    expHf = q.sbuf("expHf", [16, 8 * 128], F32)
    expHb = q.sbuf("expHb", [16, 8 * 128], BF16)
    scr = [q.sbuf(f"scr{i}", [128, 512], F32) for i in range(3)]
    scrb = [gss[0], gss[1], gsl[0], gsl[1]]
    escr = scr[2]
    nrA = wts["wg"][2][0:16, :].bitcast(F32)
    nrB = wts["wg"][3][0:16, :].bitcast(F32)
    nrC = wts["wg"][4][0:16, :].bitcast(F32)
    stg = [wts["wg"][0][:].bitcast(F32), wts["wg"][1][:].bitcast(F32)]
    ps_p = [q.psum(f"psp{i}", [128, 512]) for i in range(2)]
    ps_A = [q.psum(f"psA{i}", [16, 512]) for i in range(2)]
    ps_B = [q.psum(f"psB{i}", [128, 512]) for i in range(2)]
    ps_s = [q.psum(f"pss{i}", [16, 512]) for i in range(2)]

    for t in range(NDT):
        q.dma("sync", xnb[t][:], xn_i[t * 128:(t + 1) * 128, :])
        q.dma("gpsimd", wts["wq"][t][:], wq[t * 128:(t + 1) * 128, :])
        q.dma("gpsimd", wts["wkg"][t][:], wkg[t * 128:(t + 1) * 128, :])
        q.dma("gpsimd", wts["wv"][t][:], wv[t * 128:(t + 1) * 128, :])
        q.dma("gpsimd", wts["wg"][t][:], wg[t * 128:(t + 1) * 128, :])
    q.dma("sync", cw[:], convw[:])
    q.dma("sync", bgs[:], bgc[:])
    q.dma("sync", augs[:], augc[:])
    q.dma("gpsimd", selHb[:], selH[:])
    q.dma("gpsimd", selVb[:], selV[:])
    q.dma("sync", selVf[:], selV[:])
    q.dma("sync", expHf[:], expH[:])
    q.dma("gpsimd", expHb[:], expH[:])
    q.wait_loads("vector", "scalar", "tensor")

    for dt in range(NDT):
        for b in reversed(range(B)):
            terms = [(bp, 3 - (b - bp)) for bp in range(max(0, b - 3), b + 1)]
            first_bp, first_k = terms[0]
            inc0 = ("ev", 1) if len(terms) == 1 else None
            q.op("vector", lambda e, dt=dt, bp=first_bp, k=first_k: e.tensor_scalar(
                scr[0][:], xnb[dt][:, bp * SL:(bp + 1) * SL], cw[:, dt * 4 + k:dt * 4 + k + 1], None, op0=ALU.mult), inc=inc0)
            si = 0
            for ti, (bp, k) in enumerate(terms[1:]):
                inc1 = ("ev", 1) if ti == len(terms) - 2 else None
                q.op("vector", lambda e, dt=dt, bp=bp, k=k, si=si: e.scalar_tensor_tensor(
                    out=scr[1 - si][:], in0=xnb[dt][:, bp * SL:(bp + 1) * SL],
                    scalar=cw[:, dt * 4 + k:dt * 4 + k + 1], in1=scr[si][:], op0=ALU.mult, op1=ALU.add), inc=inc1)
                si = 1 - si
            q.wait("scalar", "ev", q.cnt["ev"])
            q.op("scalar", lambda e, dt=dt, b=b, si=si: e.activation(
                out=xnb[dt][:, b * SL:(b + 1) * SL], in_=scr[si][:], func=AF.Silu), inc=("ac", 1))
            q.wait("vector", "ac", q.cnt["ac"])
    q.barrier()
    if dbg == 1:
        for t in range(NDT):
            q.dma("gpsimd", dbg_o[t * 128:(t + 1) * 128, :], xnb[t][:], store=True)
        q.op("vector", lambda e: e.memset(stg[0], 0.0), inc=("ev", 1))
        q.wait("gpsimd", "ev", q.cnt["ev"])
        for t in range(NDT):
            for ch in range(4):
                q.dma("gpsimd", l1o_o[t * 128:(t + 1) * 128, ch * 512:(ch + 1) * 512], stg[0], store=True)
        q.wait("gpsimd", "stg", q.cnt["stg"])
        q._flush_waits("gpsimd")
        return q.finish()

    def evac_copy(dst):
        def f(ot, sl, ps, mmv):
            q.wait("vector", "mm", mmv)
            q.op("vector", lambda e, o=dst[ot], s=sl, ps=ps: e.tensor_copy(out=o[:, s], in_=ps[:]), inc=("ev", 1))
            return [("ev", q.cnt["ev"])]
        return f

    emit_proj(q, wts["wq"], xnb, ps_p, evac_copy(qb), n_out_tiles=4)

    def evac_kg(ot, sl, ps, mmv):
        q.wait("scalar", "mm", mmv)
        q.wait("scalar", "ev", q.cnt["ev"])
        q.op("scalar", lambda e, ps=ps: e.activation(out=scr[0][:], in_=ps[:], func=AF.Abs), inc=("ac", 1))
        q.wait("vector", "ac", q.cnt["ac"])
        q.op("vector", lambda e, ps=ps: e.tensor_scalar_min(scr[1][:], ps[:], 0.0), inc=("ev", 1))
        q.op("scalar", lambda e: e.activation(out=scr[2][:], in_=scr[0][:], func=AF.Exp, scale=-1.0))
        q.op("scalar", lambda e: e.activation(out=scr[2][:], in_=scr[2][:], func=AF.Ln, bias=1.0), inc=("ac", 1))
        q.wait("vector", "ac", q.cnt["ac"])
        q.op("vector", lambda e: e.tensor_sub(scr[1][:], scr[1][:], scr[2][:]))
        q.op("vector", lambda e, o=gkf[ot], s=sl: e.tensor_scalar(o[:, s], scr[1][:], 1.0 / 16.0, None, op0=ALU.mult), inc=("ev", 1))
        return [("ev", q.cnt["ev"]), ("ac", q.cnt["ac"])]

    emit_proj(q, wts["wkg"], xnb, ps_p, evac_kg, n_out_tiles=4)
    emit_proj(q, wts["wv"], xnb, ps_p, evac_copy(vb), n_out_tiles=8)

    g_ctr = [0]

    def evac_g(ot, sl, ps, mmv):
        gi = g_ctr[0]; g_ctr[0] += 1
        ii = gi % 2
        q.wait("scalar", "mm", mmv)
        if gi >= 2:
            q.wait("scalar", "stg", 16 * (gi - 1))
        q.op("scalar", lambda e, s=sl, ps=ps, ot=ot, ii=ii: e.activation(
            out=gss[ii][:], in_=ps[:], func=AF.Silu, bias=bgs[:, ot:ot + 1]), inc=("ac", 1))
        q.wait("gpsimd", "ac", q.cnt["ac"])
        q.dma("gpsimd", gsd[ot * 128:(ot + 1) * 128, sl], gss[ii][:], store=True)
        return [("ac", q.cnt["ac"])]

    emit_proj(q, wts["wg"], xnb, ps_p, evac_g, n_out_tiles=8)
    q.barrier()
    if dbg == 2:
        for t in range(4):
            q.dma("gpsimd", dbg_o[t * 128:(t + 1) * 128, :], qb[t][:], store=True)
        for t in range(4):
            q.dma("sync", l1o_o[(4 + t) * 128:(5 + t) * 128, :], gkf[t][:], store=True)
        q.op("vector", lambda e: e.memset(stg[0], 0.0), inc=("ev", 1))
        q.wait("gpsimd", "ev", q.cnt["ev"])
        for t in range(4):
            for ch in range(4):
                q.dma("gpsimd", l1o_o[t * 128:(t + 1) * 128, ch * 512:(ch + 1) * 512], stg[0], store=True)
        q.wait("gpsimd", "stg", q.cnt["stg"])
        q.wait("gpsimd", "sts", q.cnt["sts"])
        q._flush_waits("gpsimd")
        return q.finish()

    for dt in range(4):
        for b in range(B):
            bs = slice(b * SL, (b + 1) * SL)
            q.op("scalar", lambda e, dt=dt, bs=bs: e.activation(out=escr[:], in_=gkf[dt][:, bs], func=AF.Exp), inc=("ac", 1))
            q.wait("vector", "ac", q.cnt["ac"])
            q.op("vector", lambda e, dt=dt, bs=bs: e.tensor_scalar(k1b[dt][:, bs], escr[:], -1.0, 1.0, op0=ALU.mult, op1=ALU.add), inc=("ev", 1))
            q.wait("scalar", "ev", q.cnt["ev"])
    for dt in range(4):
        q.op("vector", lambda e, dt=dt: e.tensor_scalar(
            selAb[:, dt * 16:(dt + 1) * 16], selHb[:, dt * 16:(dt + 1) * 16],
            augs[:, dt:dt + 1], None, op0=ALU.mult), inc=("ev", 1))
    q.barrier(engs_from=("vector",), engs_to=("tensor",))
    for t in range(B):
        pa = ps_A[t % 2]
        if t >= 2:
            q.wait("tensor", "ev", q.cnt["ev"] - 1)
        for dt in range(4):
            q.op("vector", lambda e, dt=dt, t=t: e.tensor_mul(
                scrb[dt][:], qb[dt][:, t * SL:(t + 1) * SL], k1b[dt][:, t * SL:(t + 1) * SL]))
        q.barrier(engs_from=("vector",), engs_to=("tensor",))
        for dt in range(4):
            q.op("tensor", lambda e, dt=dt, pa=pa, st=(dt == 0), sp=(dt == 3): e.matmul(
                pa[:], selAb[:, dt * 16:(dt + 1) * 16], scrb[dt][:], start=st, stop=sp),
                inc=("mm", 1) if dt == 3 else None)
        q.wait("vector", "mm", q.cnt["mm"])
        q.op("vector", lambda e, t=t, pa=pa: e.tensor_copy(out=augw[:, t * 512:(t + 1) * 512], in_=pa[:]), inc=("ev", 1))
    q.barrier()
    for dt in range(4):
        for b in range(1, B):
            q.op("vector", lambda e, dt=dt, b=b: e.tensor_add(
                gkf[dt][:, b * SL:(b + 1) * SL], gkf[dt][:, b * SL:(b + 1) * SL],
                gkf[dt][:, (b - 1) * SL:b * SL]))
    q.barrier(engs_from=("vector",), engs_to=("scalar",))
    for dt in range(4):
        for b in range(B):
            bs = slice(b * SL, (b + 1) * SL)
            q.op("scalar", lambda e, dt=dt, bs=bs: e.activation(out=escr[:], in_=gkf[dt][:, bs], func=AF.Exp), inc=("ac", 1))
            q.wait("vector", "ac", q.cnt["ac"])
            q.op("vector", lambda e, dt=dt, bs=bs: e.tensor_mul(qb[dt][:, bs], qb[dt][:, bs], escr[:]), inc=("ev", 1))
            q.wait("scalar", "ev", q.cnt["ev"])
            q.op("scalar", lambda e, dt=dt, bs=bs: e.activation(out=escr[:], in_=gkf[dt][:, bs], func=AF.Exp, scale=-1.0), inc=("ac", 1))
            q.wait("vector", "ac", q.cnt["ac"])
            q.op("vector", lambda e, dt=dt, bs=bs: e.tensor_mul(k1b[dt][:, bs], k1b[dt][:, bs], escr[:]), inc=("ev", 1))
            q.wait("scalar", "ev", q.cnt["ev"])
    q.barrier()
    for pi, (t, s) in enumerate(PAIRS):
        pa = ps_A[pi % 2]
        if pi >= 2:
            q.wait("tensor", "ev", q.cnt["ev"] - 1)
        for dt in range(4):
            q.op("vector", lambda e, dt=dt, t=t, s=s: e.tensor_mul(
                scrb[dt][:], qb[dt][:, t * SL:(t + 1) * SL], k1b[dt][:, s * SL:(s + 1) * SL]))
        q.barrier(engs_from=("vector",), engs_to=("tensor",))
        for dt in range(4):
            q.op("tensor", lambda e, dt=dt, pa=pa, st=(dt == 0), sp=(dt == 3): e.matmul(
                pa[:], selHb[:, dt * 16:(dt + 1) * 16], scrb[dt][:], start=st, stop=sp),
                inc=("mm", 1) if dt == 3 else None)
        q.wait("vector", "mm", q.cnt["mm"])
        q.op("vector", lambda e, pi=pi, pa=pa: e.tensor_scalar(
            abuf[:, pi * 512:(pi + 1) * 512], pa[:], 0.17677669529663687, None, op0=ALU.mult), inc=("ev", 1))
    q.barrier()
    if dbg == 4:
        q.dma("gpsimd", dbg_o[0:16, 0:2048], abuf[:, 0:2048], store=True)
        q.dma("gpsimd", dbg_o[16:32, 0:2048], abuf[:, 2048:4096], store=True)
        q.dma("gpsimd", dbg_o[32:48, 0:1024], abuf[:, 4096:5120], store=True)
        q.dma("gpsimd", dbg_o[48:64, 0:2048], augw[:], store=True)
        q.op("vector", lambda e: e.memset(stg[0], 0.0), inc=("ev", 1))
        q.wait("gpsimd", "ev", q.cnt["ev"])
        for t in range(NDT):
            for ch in range(4):
                q.dma("gpsimd", l1o_o[t * 128:(t + 1) * 128, ch * 512:(ch + 1) * 512], stg[0], store=True)
        q.wait("gpsimd", "stg", q.cnt["stg"])
        q._flush_waits("gpsimd")
        return q.finish()
    if dbg == 3:
        for t in range(4):
            q.dma("gpsimd", dbg_o[t * 128:(t + 1) * 128, :], qb[t][:], store=True)
            q.dma("gpsimd", dbg_o[(4 + t) * 128:(5 + t) * 128, :], k1b[t][:], store=True)
        q.op("vector", lambda e: e.memset(stg[0], 0.0), inc=("ev", 1))
        q.wait("gpsimd", "ev", q.cnt["ev"])
        for t in range(NDT):
            for ch in range(4):
                q.dma("gpsimd", l1o_o[t * 128:(t + 1) * 128, ch * 512:(ch + 1) * 512], stg[0], store=True)
        q.wait("gpsimd", "stg", q.cnt["stg"])
        q._flush_waits("gpsimd")
        return q.finish()
    for t in range(NDT):
        q.dma("gpsimd", wob[t], wout[t * 128:(t + 1) * 128, :])
    q.wait("tensor", "ldg", q.cnt["ldg"])

    pair_idx = {(t, s): i for i, (t, s) in enumerate(PAIRS)}
    for t in range(B):
        sl = slice(t * SL, (t + 1) * SL)
        for dt in range(NDT):
            for s in range(t + 1):
                pi = pair_idx[(t, s)]
                pb = ps_B[0]
                q.op("tensor", lambda e, dt=dt, pi=pi, pb=pb: e.matmul(
                    pb[:], expHb[:, dt * 128:(dt + 1) * 128], abuf[:, pi * 512:(pi + 1) * 512],
                    start=True, stop=True), inc=("mm", 1))
                q.wait("vector", "mm", q.cnt["mm"])
                if s == 0:
                    q.op("vector", lambda e, dt=dt, s=s, pb=pb: e.tensor_mul(
                        ob[dt][:], vb[dt][:, s * SL:(s + 1) * SL], pb[:]), inc=("ev", 1))
                else:
                    q.op("vector", lambda e, dt=dt, s=s, pb=pb: e.tensor_mul(scr[0][:], vb[dt][:, s * SL:(s + 1) * SL], pb[:]))
                    q.op("vector", lambda e, dt=dt: e.tensor_add(ob[dt][:], ob[dt][:], scr[0][:]), inc=("ev", 1))
                q.wait("tensor", "ev", q.cnt["ev"])
            pb = ps_B[1]
            q.op("tensor", lambda e, dt=dt, t=t, pb=pb: e.matmul(
                pb[:], expHb[:, dt * 128:(dt + 1) * 128], augw[:, t * 512:(t + 1) * 512],
                start=True, stop=True), inc=("mm", 1))
            q.wait("vector", "mm", q.cnt["mm"])
            q.op("vector", lambda e, dt=dt, s=sl, pb=pb: e.tensor_mul(scr[1][:], vb[dt][:, s], pb[:]), inc=("ev", 1))
            q.wait("scalar", "ev", q.cnt["ev"])
            q.op("scalar", lambda e: e.activation(out=scr[2][:], in_=scr[1][:], func=AF.Sigmoid), inc=("ac", 1))
            q.wait("vector", "ac", q.cnt["ac"])
            q.op("vector", lambda e, dt=dt: e.tensor_add(ob[dt][:], ob[dt][:], scr[2][:]))
            q.op("vector", lambda e, dt=dt: e.tensor_copy(out=obb[dt][:], in_=ob[dt][:]), inc=("ev", 1))
            q.wait("tensor", "ev", q.cnt["ev"])
        q.barrier(engs_from=("vector",), engs_to=("tensor", "scalar"))
        if dbg == 5 and t == 1:
            q.barrier()
            for dt in range(NDT):
                q.dma("gpsimd", l1o_o[dt * 128:(dt + 1) * 128, 0:512], ob[dt], store=True)
            q.wait("gpsimd", "stg", q.cnt["stg"])
            q._flush_waits("gpsimd")
            return q.finish()
        for dt in range(NDT):
            q.op("tensor", lambda e, dt=dt, st=(dt == 0), sp=(dt == NDT - 1): e.matmul(
                ps_s[0][:], selVf[:, dt * 16:(dt + 1) * 16], ob[dt], start=st, stop=sp), inc=("mm", 1))
        q.barrier(engs_from=("tensor",), engs_to=("vector",))
        q.op("vector", lambda e: e.tensor_scalar(nrA, ps_s[0][:], -1.0 / DV, None, op0=ALU.mult), inc=("ev", 1))
        q.wait("vector", "ev", q.cnt["ev"])
        q.barrier(engs_from=("vector",), engs_to=("tensor",))
        for dt in range(NDT):
            pb = ps_B[dt % 2]
            if dt >= 2:
                q.wait("tensor", "ev", q.cnt["ev"] - 1)
                q.wait("tensor", "ac", q.cnt["ac"] - 1)
            q.op("tensor", lambda e, dt=dt, pb=pb: e.matmul(
                pb[:], expHf[:, dt * 128:(dt + 1) * 128], nrA, start=True, stop=True), inc=("mm", 1))
            q.wait("vector", "mm", q.cnt["mm"])
            q.op("vector", lambda e, dt=dt, pb=pb: e.tensor_add(ob[dt], ob[dt], pb[:]))
            q.op("vector", lambda e, dt=dt: e.tensor_copy(out=obb[dt], in_=ob[dt]), inc=("ev", 1))
            sq = scrb[dt % 2]
            q.wait("scalar", "ev", q.cnt["ev"])
            if dt >= 2:
                q.wait("scalar", "mm", q.cnt["mm"])
            q.op("scalar", lambda e, dt=dt, sq=sq: e.activation(out=sq[:], in_=obb[dt], func=AF.Square), inc=("ac", 1))
            q.wait("tensor", "ac", q.cnt["ac"])
            q.op("tensor", lambda e, sq=sq, st=(dt == 0), sp=(dt == NDT - 1), dt=dt: e.matmul(
                ps_s[1][:], selVb[:, dt * 16:(dt + 1) * 16], sq[:], start=st, stop=sp), inc=("mm", 1))
        q.barrier(engs_from=("tensor",), engs_to=("vector",))
        q.op("vector", lambda e: e.tensor_scalar(nrB, ps_s[1][:], 1.0 / DV, None, op0=ALU.mult), inc=("ev", 1))
        q.wait("vector", "ev", q.cnt["ev"])
        q.op("vector", lambda e: e.tensor_scalar(nrB, nrB, 1e-5, None, op0=ALU.add), inc=("ev", 1))
        q.barrier(engs_from=("vector",), engs_to=("scalar", "tensor"))
        q.op("scalar", lambda e: e.activation(out=nrB, in_=nrB, func=AF.Sqrt), inc=("ac", 1))
        q.wait("vector", "ac", q.cnt["ac"])
        q.op("vector", lambda e: e.reciprocal(out=nrB, in_=nrB), inc=("ev", 1))
        q.wait("vector", "ev", q.cnt["ev"])
        q.barrier(engs_from=("vector",), engs_to=("tensor",))
        if dbg == 7 and t == 1:
            q.barrier()
            q.dma("gpsimd", l1o_o[0:16, 0:512], nrA, store=True)
            q.dma("gpsimd", l1o_o[16:32, 0:512], nrB, store=True)
            q.wait("gpsimd", "stg", q.cnt["stg"])
            q._flush_waits("gpsimd")
            return q.finish()
        for dt in range(NDT):
            pb = ps_B[dt % 2]
            pa = ps_p[dt % 2]
            if dt >= 2:
                q.wait("tensor", "ev", q.cnt["ev"] - 1)
                q.wait("sync", "ev", q.cnt["ev"] - 1)
            q.dma("sync", gsl[dt % 2][:], gsd[dt * 128:(dt + 1) * 128, sl])
            q.op("tensor", lambda e, dt=dt, pb=pb: e.matmul(
                pb[:], expHf[:, dt * 128:(dt + 1) * 128], nrB, start=True, stop=True), inc=("mm", 1))
            q.wait("vector", "mm", q.cnt["mm"])
            q.wait("vector", "lds", q.cnt["lds"])
            q.op("vector", lambda e, dt=dt, pb=pb: e.tensor_mul(scr[0][:], ob[dt], pb[:]))
            q.op("vector", lambda e, dt=dt: e.tensor_mul(gob[dt], gsl[dt % 2][:], scr[0][:]), inc=("ev", 1))
        q.barrier()
        if dbg == 6 and t == 1:
            for dt in range(NDT):
                q.dma("gpsimd", dbg_o[dt * 128:(dt + 1) * 128, 0:512], gob[dt], store=True)
            q.op("vector", lambda e: e.memset(stg[0], 0.0), inc=("ev", 1))
            q.wait("gpsimd", "ev", q.cnt["ev"])
            for tt in range(NDT):
                for ch in range(4):
                    q.dma("gpsimd", l1o_o[tt * 128:(tt + 1) * 128, ch * 512:(ch + 1) * 512], stg[0], store=True)
            q.wait("gpsimd", "stg", q.cnt["stg"])
            q._flush_waits("gpsimd")
            return q.finish()
        for ot in range(NDT):
            psw = ps_p[ot % 2]
            if ot >= 2:
                q.wait("tensor", "stg", q.cnt["stg"] - 16)
            for kin in range(NDT):
                inc = ("mm", 1) if kin == NDT - 1 else None
                q.op("tensor", lambda e, kin=kin, ot=ot, psw=psw, st=(kin == 0), sp=(kin == NDT - 1):
                     e.matmul(psw[:], wob[kin][:, ot * 128:(ot + 1) * 128], gob[kin], start=st, stop=sp), inc=inc)
            q.wait("vector", "mm", q.cnt["mm"])
            q.op("vector", lambda e, psw=psw, ii=ot % 2: e.tensor_copy(out=stg[ii], in_=psw[:]), inc=("ev", 1))
            q.wait("gpsimd", "ev", q.cnt["ev"])
            q.dma("gpsimd", l1o_o[ot * 128:(ot + 1) * 128, sl], stg[ot % 2], store=True)
        q.barrier()

    q.wait("gpsimd", "stg", q.cnt["stg"])
    q._flush_waits("gpsimd")
    return q.finish()


def _d_consts():
    selH = np.zeros((128, 64), np.float32)
    selV = np.zeros((128, 128), np.float32)
    expH = np.zeros((16, 8 * 128), np.float32)
    for dt in range(4):
        for r4 in range(4):
            h = 4 * dt + r4
            selH[r4 * 32:(r4 + 1) * 32, dt * 16 + h] = 1.0
    for dt in range(8):
        for r2 in range(2):
            h = 2 * dt + r2
            selV[r2 * 64:(r2 + 1) * 64, dt * 16 + h] = 1.0
            expH[h, dt * 128 + r2 * 64: dt * 128 + (r2 + 1) * 64] = 1.0
    return selH, selV, expH


def stage_l1attn(ln2_cm, att):
    selH, selV, expH = _d_consts()
    nc = _prog("D", build_D)
    bgc = np.asarray(att["bg"], np.float32).reshape(NDT, 128).T.copy()
    augc = np.asarray(att["aug_balance"], np.float32).reshape(4, 128).T.copy()
    cwc = np.asarray(att["conv_w"], np.float32).reshape(NDT, 128, 4).transpose(1, 0, 2).reshape(128, 32)
    maps = []
    for c in range(NCORE):
        maps.append({
            "xn": ln2_cm[c], "convw": cwc,
            "wq": np.asarray(att["Wq"], np.float32), "wkg": np.asarray(att["Wkg"], np.float32),
            "wv": np.asarray(att["Wv"], np.float32), "wg": np.asarray(att["Wg"], np.float32),
            "wout": np.asarray(att["Wout"], np.float32),
            "bgc": bgc, "augc": augc, "selH": selH, "selV": selV, "expH": expH,
        })
    r = _run(nc, maps).results
    return [r[c]["l1o"] for c in range(NCORE)]


EXP = 8
HME = 4096
NJ_E = HME // 128
BLK = 1024
NBLK = TL // BLK
NC2 = BLK // 512


def build_E(dbg=0):
    q = Q()
    nc = q.nc
    ln3 = nc.dram_tensor("ln3", [D, TL], BF16, kind="ExternalInput").ap()
    ln3f = nc.dram_tensor("ln3f", [D, TL], F32, kind="ExternalInput").ap()
    gam = nc.dram_tensor("gam", [128, NDT], F32, kind="ExternalInput").ap()
    bet = nc.dram_tensor("bet", [128, NDT], F32, kind="ExternalInput").ap()
    rw = nc.dram_tensor("rw", [128, NDT * EXP], F32, kind="ExternalInput").ap()
    w1r = nc.dram_tensor("w1r", [EXP * (NJ_E // 2) * 128, 2 * D], F32, kind="ExternalInput").ap()
    w2r = nc.dram_tensor("w2r", [EXP * HME, 2 * 512], F32, kind="ExternalInput").ap()
    idf = nc.dram_tensor("idf", [128, 128], F32, kind="ExternalInput").ap()
    onesc = nc.dram_tensor("onesc", [128, 1], F32, kind="ExternalInput").ap()
    onesr = nc.dram_tensor("onesr", [1, 128], F32, kind="ExternalInput").ap()
    ys_o = nc.dram_tensor("ys", [D, TL], F32, kind="ExternalOutput").ap()
    wd_d = nc.dram_tensor("wd_d", [EXP, TL], BF16, kind="Internal").ap()
    dbg_o = nc.dram_tensor("dbgE", [D, TL], BF16, kind="ExternalOutput").ap() if dbg else None

    l3b = [q.sbuf(f"l3{t}", [128, TL], BF16) for t in range(NDT)]
    xnb = [q.sbuf(f"xnb{t}", [128, TL], BF16) for t in range(NDT)]
    hb = [q.sbuf(f"hb{j}", [128, BLK], BF16) for j in range(NJ_E)]
    acc = [q.sbuf(f"acc{t}", [128, BLK], F32) for t in range(NDT)]
    w1t = [q.sbuf(f"w1t{i}", [128, 2 * D], BF16) for i in range(2)]
    w2t = [q.sbuf(f"w2t{i}", [128, 2 * 512], BF16) for i in range(2)]
    rwt = q.sbuf("rwt", [128, EXP * NDT], BF16)
    gamс = q.sbuf("gamc", [128, NDT], F32)
    betc = q.sbuf("betc", [128, NDT], F32)
    idb = q.sbuf("idb", [128, 128], BF16)
    logr = q.sbuf("logr", [8, TL], BF16)
    ltok = [q.sbuf(f"lt{i}", [128, EXP], BF16) for i in range(16)]
    wdr = q.sbuf("wdr", [8, TL], BF16)
    wdBs = [q.sbuf(f"wdB{i}", [128, 512], F32) for i in range(2)]
    wrow = [q.sbuf(f"wrow{i}", [1, 512], BF16) for i in range(2)]
    onesr_b = q.sbuf("onesrb", [1, 128], BF16)
    mcol = q.sbuf("mcol", [128, 1], F32)
    zcol = q.sbuf("zcol", [128, 1], F32)
    srt = q.sbuf("srt", [128, 8], F32)
    prb = q.sbuf("prb", [128, 8], F32)
    msk = q.sbuf("msk", [128, 8], F32)
    wdt = q.sbuf("wdt", [128, 8], BF16)
    tmpf = q.sbuf("tmpE", [128, 512], F32)
    l3s = [q.sbuf(f"l3s{i}", [128, 512], F32) for i in range(2)]
    stg = [q.sbuf(f"stgE{i}", [128, 512], F32) for i in range(2)]
    pp = (q.psum("pse0", [128, 512]), q.psum("pse1", [128, 512]))
    ps_y = [q.psum(f"psy{i}", [128, 512]) for i in range(6)]
    cn = {}
    cn["sqs"] = [q.sbuf(f"lnsq{i}", [128, 512], BF16) for i in range(2)]
    cn["onesc_b"] = q.sbuf("onescb", [128, 1], BF16)
    cn["onesr_f"] = q.sbuf("onesrf", [1, 128], F32)
    cn["ps_st"] = [ps_y[0][0:1, :], ps_y[1][0:1, :]]
    cn["ps_bc"] = [ps_y[2][:], ps_y[3][:]]
    rowA = q.sbuf("rowA", [1, 512], F32)
    rowB = q.sbuf("rowB", [1, 512], F32)
    rowC = q.sbuf("rowC", [1, 512], F32)
    ps_tr = ps_y[4][:].bitcast(BF16)[:, 0:128]
    dbgt = hb[0] if dbg == 3 else None

    for t in range(NDT):
        q.dma("sync", l3b[t][:], ln3[t * 128:(t + 1) * 128, :])
    q.dma("sync", gamс[:], gam[:])
    q.dma("sync", betc[:], bet[:])
    q.dma("gpsimd", rwt[:], rw[:])
    q.dma("gpsimd", idb[:], idf[:])
    q.dma("gpsimd", onesr_b[:], onesr[:])
    load_ln_consts(q, cn, onesc, onesr)
    q.wait_loads("vector", "scalar", "tensor")

    srcs = l3b
    for ch in range(4):
        sl = slice(ch * 512, (ch + 1) * 512)
        for t in range(NDT):
            sq = cn["sqs"][t % 2]
            if t >= 2:
                q.wait("scalar", "mm", q.cnt["mm"] - 1)
            q.op("scalar", lambda e, a=sq, b=srcs[t], s=sl: e.activation(out=a[:], in_=b[:, s], func=AF.Square), inc=("ac", 1))
            q.op("tensor", lambda e, r=srcs[t], s=sl, st=(t == 0), sp=(t == NDT - 1), ps=cn["ps_st"][0]:
                 e.matmul(ps[:], cn["onesc_b"][:], r[:, s], start=st, stop=sp))
            q.wait("tensor", "ac", q.cnt["ac"])
            q.op("tensor", lambda e, r=sq, st=(t == 0), sp=(t == NDT - 1), ps=cn["ps_st"][1]:
                 e.matmul(ps[:], cn["onesc_b"][:], r[:], start=st, stop=sp), inc=("mm", 1))
        q.barrier(engs_from=("tensor",), engs_to=("vector",))
        q.op("vector", lambda e: e.tensor_scalar(rowA[:], cn["ps_st"][0][:], -1.0 / D, None, op0=ALU.mult), inc=("ev", 1))
        q.op("vector", lambda e: e.tensor_scalar(rowB[:], cn["ps_st"][1][:], 1.0 / D, None, op0=ALU.mult), inc=("ev", 1))
        q.wait("vector", "ev", q.cnt["ev"])
        q.op("vector", lambda e: e.tensor_mul(rowC[:], rowA[:], rowA[:]), inc=("ev", 1))
        q.wait("vector", "ev", q.cnt["ev"])
        q.op("vector", lambda e: e.tensor_sub(rowB[:], rowB[:], rowC[:]), inc=("ev", 1))
        q.wait("vector", "ev", q.cnt["ev"])
        q.op("vector", lambda e: e.tensor_scalar(rowB[:], rowB[:], 1e-5, None, op0=ALU.add))
        q.barrier(engs_from=("vector",), engs_to=("scalar",))
        q.op("scalar", lambda e: e.activation(out=rowB[:], in_=rowB[:], func=AF.Sqrt))
        q.barrier(engs_from=("scalar",), engs_to=("vector",))
        q.op("vector", lambda e: e.reciprocal(out=rowB[:], in_=rowB[:]), inc=("ev", 1))
        q.wait("vector", "ev", q.cnt["ev"])
        q.op("vector", lambda e: e.tensor_mul(rowC[:], rowA[:], rowB[:]), inc=("ev", 1))
        q.barrier(engs_from=("vector",), engs_to=("tensor",))
        q.op("tensor", lambda e: e.matmul(cn["ps_bc"][0][:], cn["onesr_f"][:], rowB[:], start=True, stop=True))
        q.op("tensor", lambda e: e.matmul(cn["ps_bc"][1][:], cn["onesr_f"][:], rowC[:], start=True, stop=True), inc=("mm", 1))
        q.wait("vector", "mm", q.cnt["mm"])
        for t in range(NDT):
            q.op("vector", lambda e, a=srcs[t], s=sl: e.tensor_mul(tmpf[:], a[:, s], cn["ps_bc"][0][:]))
            q.op("vector", lambda e: e.tensor_add(tmpf[:], tmpf[:], cn["ps_bc"][1][:]))
            q.op("vector", lambda e, t=t, s=sl: e.tensor_scalar(
                xnb[t][:, s], tmpf[:], gamс[:, t:t + 1], betc[:, t:t + 1], op0=ALU.mult, op1=ALU.add), inc=("ev", 1))
        q.barrier(engs_from=("vector",), engs_to=("tensor",))
    q.barrier()

    for ch in range(4):
        sl = slice(ch * 512, (ch + 1) * 512)
        ps = pp[ch % 2]
        if ch >= 2:
            q.wait("tensor", "ev", q.cnt["ev"] - 1)
        for kin in range(NDT):
            inc = ("mm", 1) if kin == NDT - 1 else None
            q.op("tensor", lambda e, kin=kin, s=sl, ps=ps, st=(kin == 0), sp=(kin == NDT - 1):
                 e.matmul(ps[0:8, :], rwt[:, kin * 8:(kin + 1) * 8], xnb[kin][:, s], start=st, stop=sp), inc=inc)
        q.wait("vector", "mm", q.cnt["mm"])
        q.op("vector", lambda e, s=sl, ps=ps: e.tensor_copy(out=logr[:, s], in_=ps[0:8, :]), inc=("ev", 1))
    q.barrier()
    if dbg == 2:
        q.dma("gpsimd", ys_o[8:16, :], logr[:, :], store=True)
        q.op("vector", lambda e: e.memset(stg[0][:], 0.0), inc=("ev", 1))
        q.wait("gpsimd", "ev", q.cnt["ev"])
        for ot in range(NDT):
            for ch in range(4):
                q.dma("gpsimd", ys_o[ot * 128:(ot + 1) * 128, ch * 512:(ch + 1) * 512], stg[0][:], store=True)
        q.wait("gpsimd", "stg", q.cnt["stg"])
        q._flush_waits("gpsimd")
        return q.finish()
    for i in range(16):
        q.op("tensor", lambda e, i=i: e.transpose(ps_tr[:, 0:8], logr[:, i * 128:(i + 1) * 128], idb[0:8, 0:8]), inc=("mm", 1))
        q.wait("vector", "mm", q.cnt["mm"])
        q.op("vector", lambda e, i=i: e.tensor_copy(out=ltok[i][:], in_=ps_tr[0:128, 0:EXP]), inc=("ev", 1))
        q.wait("tensor", "ev", q.cnt["ev"])
    q.barrier()
    if dbg == 3:
        q.op("vector", lambda e: e.memset(dbgt[:], 0.0), inc=("ev", 1))
    for i in range(16):
        q.op("vector", lambda e, i=i: e.reduce_max(mcol[:], ltok[i][:], axis=AX.X), inc=("ev", 1))
        q.wait("vector", "ev", q.cnt["ev"])
        q.op("vector", lambda e: e.tensor_scalar(mcol[:], mcol[:], -1.0 / 1.5, None, op0=ALU.mult), inc=("ev", 1))
        q.wait("scalar", "ev", q.cnt["ev"])
        q.op("scalar", lambda e, i=i: e.activation(out=prb[:], in_=ltok[i][:], func=AF.Exp,
                                                   scale=1.0 / 1.5, bias=mcol[:], accum_out=zcol[:]), inc=("ac", 1))
        q.wait("vector", "ac", q.cnt["ac"])
        q.op("vector", lambda e: e.reciprocal(out=zcol[:], in_=zcol[:]), inc=("ev", 1))
        q.wait("vector", "ev", q.cnt["ev"])
        q.op("vector", lambda e: e.tensor_scalar(prb[:], prb[:], zcol[:], None, op0=ALU.mult), inc=("ev", 1))
        q.wait("vector", "ev", q.cnt["ev"])
        q.op("vector", lambda e: e.max(out=srt[:], in_=prb[:]), inc=("ev", 1))
        q.wait("vector", "ev", q.cnt["ev"])
        q.op("vector", lambda e: e.tensor_scalar(msk[:], prb[:], srt[:, 1:2], None, op0=ALU.is_ge), inc=("ev", 1))
        q.wait("vector", "ev", q.cnt["ev"])
        q.op("vector", lambda e: e.tensor_mul(wdt[:], prb[:], msk[:]), inc=("ev", 1))
        if dbg == 3:
            q.op("vector", lambda e, i=i: e.tensor_copy(out=dbgt[:, i * 64:i * 64 + 8], in_=prb[:]))
            q.op("vector", lambda e, i=i: e.tensor_copy(out=dbgt[:, i * 64 + 8:i * 64 + 16], in_=srt[:]))
            q.op("vector", lambda e, i=i: e.tensor_copy(out=dbgt[:, i * 64 + 16:i * 64 + 24], in_=msk[:]))
            q.op("vector", lambda e, i=i: e.tensor_copy(out=dbgt[:, i * 64 + 24:i * 64 + 32], in_=ltok[i][:]))
            q.op("vector", lambda e, i=i: e.tensor_copy(out=dbgt[:, i * 64 + 32:i * 64 + 33], in_=mcol[:]))
            q.op("vector", lambda e, i=i: e.tensor_copy(out=dbgt[:, i * 64 + 33:i * 64 + 34], in_=zcol[:]), inc=("ev", 1))
        q.barrier(engs_from=("vector",), engs_to=("tensor",))
        q.op("tensor", lambda e, i=i: e.transpose(ps_tr[0:8, :], wdt[:], idb[:]), inc=("mm", 1))
        q.wait("vector", "mm", q.cnt["mm"])
        q.op("vector", lambda e, i=i: e.tensor_copy(out=wdr[:, i * 128:(i + 1) * 128], in_=ps_tr[0:8, 0:128]), inc=("ev", 1))
        q.wait("tensor", "ev", q.cnt["ev"])
    q.barrier()
    q.dma("gpsimd", wd_d[:, :], wdr[:, :], store=True)
    q.wait("sync", "stg", q.cnt["stg"])
    if dbg == 3:
        q.barrier()
        q.dma("gpsimd", dbg_o[0:128, 0:1024], dbgt[:], store=True)
        q.dma("gpsimd", ys_o[0:8, :], wdr[:, :], store=True)
        q.wait("gpsimd", "stg", q.cnt["stg"])
        q._flush_waits("gpsimd")
        return q.finish()
    if dbg == 1:
        for t in range(NDT):
            q.dma("gpsimd", dbg_o[t * 128:(t + 1) * 128, :], xnb[t][:], store=True)
        q.dma("gpsimd", ys_o[0:8, :], wdr[:, :], store=True)
        q.dma("gpsimd", ys_o[8:16, :], logr[:, :], store=True)
        q.wait("gpsimd", "stg", q.cnt["stg"])
        q._flush_waits("gpsimd")
        return q.finish()

    for blk in range(NBLK):
        bsl = slice(blk * BLK, (blk + 1) * BLK)
        for e_i in range(EXP):
            for c2 in range(NC2):
                ps = pp[c2 % 2]
                wr = wrow[c2 % 2]
                q.dma("sync", wr[:], wd_d[e_i:e_i + 1, blk * BLK + c2 * 512: blk * BLK + (c2 + 1) * 512])
                q.wait("tensor", "lds", q.cnt["lds"])
                q.op("tensor", lambda e, ps=ps, wr=wr: e.matmul(
                    ps[:], onesr_b[:], wr[:], start=True, stop=True), inc=("mm", 1))
                q.wait("vector", "mm", q.cnt["mm"])
                q.op("vector", lambda e, c2=c2, ps=ps: e.tensor_copy(out=wdBs[c2][:], in_=ps[:]), inc=("ev", 1))
            q.barrier(engs_from=("vector",), engs_to=("tensor", "sync"))
            if dbg == 6 and blk == 0:
                q.op("vector", lambda e: e.tensor_copy(out=stg[0][:], in_=pp[0][:]), inc=("ev", 1))
                q.wait("gpsimd", "ev", q.cnt["ev"])
                for c2x in range(NC2):
                    q.dma("gpsimd", dbg_o[e_i * 128:(e_i + 1) * 128, c2x * 512:(c2x + 1) * 512], wdBs[c2x][:], store=True)
                q.dma("gpsimd", dbg_o[e_i * 128:e_i * 128 + 128, 1024:1536], stg[0][:], store=True)
                q.wait("tensor", "stg", q.cnt["stg"])
                q.wait("vector", "stg", q.cnt["stg"])
            gh = 0
            for jp in range(NJ_E // 2):
                wb = w1t[jp % 2]
                if jp >= 2:
                    q.wait("gpsimd", "mm", q.cnt["mm"] - 2 * NC2)
                q.dma("gpsimd", wb[:], w1r[(e_i * (NJ_E // 2) + jp) * 128:(e_i * (NJ_E // 2) + jp + 1) * 128, :])
                ldg_j = q.cnt["ldg"]
                for jhalf in range(2):
                    j = 2 * jp + jhalf
                    for c2 in range(NC2):
                        ps = pp[gh % 2]
                        q.wait("tensor", "ldg", ldg_j)
                        if gh >= 2:
                            q.wait("tensor", "ac", q.cnt["ac"] - 1)
                        for kin in range(NDT):
                            inc = ("mm", 1) if kin == NDT - 1 else None
                            q.op("tensor", lambda e, kin=kin, blk=blk, c2=c2, ps=ps, wb=wb, jh=jhalf,
                                 st=(kin == 0), sp=(kin == NDT - 1):
                                 e.matmul(ps[:], wb[:, jh * D + kin * 128:jh * D + (kin + 1) * 128],
                                          xnb[kin][:, blk * BLK + c2 * 512: blk * BLK + (c2 + 1) * 512],
                                          start=st, stop=sp), inc=inc)
                        q.wait("scalar", "mm", q.cnt["mm"])
                        q.op("scalar", lambda e, j=j, c2=c2, ps=ps: e.activation(
                            out=hb[j][:, c2 * 512:(c2 + 1) * 512], in_=ps[:], func=AF.Silu), inc=("ac", 1))
                        gh += 1
            q.barrier(engs_from=("scalar",), engs_to=("tensor",))
            if dbg == 4 and blk == 0 and e_i == 0:
                q.barrier()
                for j in range(8):
                    q.dma("gpsimd", dbg_o[j * 128:(j + 1) * 128, 0:BLK], hb[j][:], store=True)
                q.wait("gpsimd", "stg", q.cnt["stg"])
                q._flush_waits("gpsimd")
                return q.finish()
            banks = [pp[0], pp[1]] + ps_y
            for oth in range(2):
                for jp in range(NJ_E // 2):
                    wb = w2t[jp % 2]
                    if jp >= 2 or oth == 1 or e_i > 0 or blk > 0:
                        q.wait("gpsimd", "mm", q.cnt["mm"] - 1)
                    q.dma("gpsimd", wb[:], w2r[((e_i * 2 + oth) * (NJ_E // 2) + jp) * 128:
                                               ((e_i * 2 + oth) * (NJ_E // 2) + jp + 1) * 128, :])
                    ldg_j = q.cnt["ldg"]
                    q.wait("tensor", "ldg", ldg_j)
                    for jhalf in range(2):
                        jj = 2 * jp + jhalf
                        for c2 in range(NC2):
                            for ot4 in range(4):
                                inc = ("mm", 1) if (c2 == NC2 - 1 and ot4 == 3 and jhalf == 1) else None
                                q.op("tensor", lambda e, jj=jj, ot4=ot4, c2=c2, wb=wb, jh=jhalf,
                                     st=(jj == 0), sp=(jj == NJ_E - 1):
                                     e.matmul(banks[ot4 * 2 + c2][:],
                                              wb[:, jh * 512 + ot4 * 128:jh * 512 + (ot4 + 1) * 128],
                                              hb[jj][:, c2 * 512:(c2 + 1) * 512],
                                              start=st, stop=sp), inc=inc)
                q.wait("vector", "mm", q.cnt["mm"])
                for ot4 in range(4):
                    for c2 in range(NC2):
                        ot = oth * 4 + ot4
                        psx = banks[ot4 * 2 + c2]
                        if e_i == 0:
                            q.op("vector", lambda e, ot=ot, c2=c2, psx=psx: e.tensor_mul(
                                acc[ot][:, c2 * 512:(c2 + 1) * 512], psx[:], wdBs[c2][:]), inc=("ev", 1))
                        else:
                            q.op("vector", lambda e, ot=ot, c2=c2, psx=psx: e.tensor_mul(tmpf[:], psx[:], wdBs[c2][:]))
                            q.op("vector", lambda e, ot=ot, c2=c2: e.tensor_add(
                                acc[ot][:, c2 * 512:(c2 + 1) * 512], acc[ot][:, c2 * 512:(c2 + 1) * 512], tmpf[:]), inc=("ev", 1))
                q.barrier(engs_from=("vector",), engs_to=("tensor", "gpsimd"))
        if dbg in (5, 6) and blk == 0:
            q.barrier()
            for t in range(NDT):
                q.dma("gpsimd", dbg_o[t * 128:(t + 1) * 128, 0:BLK], acc[t][:], store=True)
            q.wait("gpsimd", "stg", q.cnt["stg"])
            q._flush_waits("gpsimd")
            return q.finish()
        stg0 = q.cnt["stg"]
        for ot in range(NDT):
            for c2 in range(NC2):
                gi = ot * NC2 + c2
                ii = gi % 2
                if gi >= 2:
                    q.wait("sync", "ev", q.cnt["ev"] - 1)
                q.dma("sync", l3s[ii][:], ln3f[ot * 128:(ot + 1) * 128, blk * BLK + c2 * 512: blk * BLK + (c2 + 1) * 512])
                q.wait("vector", "lds", q.cnt["lds"])
                if gi >= 2:
                    q.wait("vector", "stg", stg0 + 16 * (gi - 1))
                q.op("vector", lambda e, ot=ot, c2=c2, ii=ii, blk=blk: e.tensor_add(
                    stg[ii][:], acc[ot][:, c2 * 512:(c2 + 1) * 512], l3s[ii][:]), inc=("ev", 1))
                q.wait("gpsimd", "ev", q.cnt["ev"])
                q.dma("gpsimd", ys_o[ot * 128:(ot + 1) * 128, blk * BLK + c2 * 512: blk * BLK + (c2 + 1) * 512],
                      stg[ii][:], store=True)
        q.barrier()
    q.wait("gpsimd", "stg", q.cnt["stg"])
    q._flush_waits("gpsimd")
    return q.finish()


def stage_l1ffn(ln3b_cm, ln3f_cm, x3_cm, moe):
    W1 = np.asarray(moe["W1"], np.float32)
    w1r = np.empty((EXP * (NJ_E // 2) * 128, 2 * D), np.float32)
    for e_i in range(EXP):
        for jp in range(NJ_E // 2):
            r0 = (e_i * (NJ_E // 2) + jp) * 128
            for jh in range(2):
                j = 2 * jp + jh
                w1r[r0:r0 + 128, jh * D:(jh + 1) * D] = (
                    W1[e_i][:, j * 128:(j + 1) * 128].reshape(NDT, 128, 128).transpose(1, 0, 2).reshape(128, D))
    W2 = np.asarray(moe["W2"], np.float32)
    w2r = np.empty((EXP * HME, 2 * 512), np.float32)
    for e_i in range(EXP):
        for oth in range(2):
            for jp in range(NJ_E // 2):
                r0 = ((e_i * 2 + oth) * (NJ_E // 2) + jp) * 128
                w2r[r0:r0 + 128, 0:512] = W2[e_i][2 * jp * 128:(2 * jp + 1) * 128, oth * 512:(oth + 1) * 512]
                w2r[r0:r0 + 128, 512:1024] = W2[e_i][(2 * jp + 1) * 128:(2 * jp + 2) * 128, oth * 512:(oth + 1) * 512]
    gam = np.asarray(moe["gamma"], np.float32).reshape(NDT, 128).T.copy()
    bet = np.asarray(moe["beta"], np.float32).reshape(NDT, 128).T.copy()
    rwT = np.ascontiguousarray(np.asarray(moe["router_w"], np.float32).T)
    rwc = rwT.reshape(NDT, 128, EXP).transpose(1, 0, 2).reshape(128, NDT * EXP)
    nc = _prog("E", build_E)
    maps = []
    for c in range(NCORE):
        maps.append({
            "ln3": ln3b_cm[c], "ln3f": ln3f_cm[c], "gam": gam, "bet": bet,
            "rw": rwc, "w1r": w1r, "w2r": w2r, "idf": np.eye(128, dtype=np.float32),
            "onesc": _ONESC, "onesr": _ONESR,
        })
    r = _run(nc, maps).results
    ys = [r[c]["ys"] for c in range(NCORE)]
    return stage_sandwich(ys, x3_cm, mode="x")
